# version 08
# speedup vs baseline: 1.3951x; 1.0025x over previous
import numpy as np
import ml_dtypes
import concourse.bass as bass
import concourse.mybir as mybir
from concourse.bass_utils import run_bass_kernel_spmd
from concourse.aot_env import allow_ici_all_to_all

F32 = mybir.dt.float32
BF16 = mybir.dt.bfloat16
AL = mybir.AluOpType
ACT = mybir.ActivationFunctionType
AXL = mybir.AxisListType
BF = ml_dtypes.bfloat16

R = 8
D = 1024
N16 = 16
TOK = 2048
T = R * TOK
DC = 8
TH = 1024
STAGE = 4

C_EPS, C_NEG1, C_NEGH, C_ONE = 0, 1, 2, 3
NCONST = 8
ENGS = ("sync", "vector", "scalar", "tensor", "gpsimd")


class Prog:

    def __init__(self, nc):
        self.nc = nc
        self.q = {e: [] for e in ENGS}
        self.sems = {}
        self.cnt = {}
        for s in ("vv", "ss", "tt", "gg", "cc", "hs_sync", "hs_gpsimd"):
            self.sems[s] = nc.alloc_semaphore(s)
            self.cnt[s] = 0
        self.pend = {e: [] for e in ENGS}
        self.psfree = {}
        self.slot_eng = {}

    def cur(self, sem):
        return (sem, self.cnt[sem]) if self.cnt[sem] > 0 else None

    def _emit(self, eng, fn, waits, inc):
        waits = [w for w in waits if w is not None] + self.pend[eng]
        self.pend[eng] = []
        for (s, v) in waits:
            assert v <= self.cnt[s], f"future token {s}={v} now {self.cnt[s]}"
        tok = None
        if inc is not None:
            s, amt = inc
            self.cnt[s] += amt
            tok = (s, self.cnt[s])
        self.q[eng].append((tuple(waits), fn, inc))
        return tok

    def v(self, fn, wait=()):
        return self._emit("vector", fn, wait, ("vv", 1))

    def s(self, fn, wait=()):
        return self._emit("scalar", fn, wait, ("ss", 1))

    def t(self, fn, wait=()):
        return self._emit("tensor", fn, wait, ("tt", 1))

    def g(self, fn, wait=()):
        return self._emit("gpsimd", fn, wait, ("gg", 1))

    def dma(self, eng, out, in_, wait=(), slot=None):
        assert slot is not None, "dma needs a slot"
        name = "D_" + slot
        if name not in self.sems:
            self.sems[name] = self.nc.alloc_semaphore(name)
            self.cnt[name] = 0
            self.slot_eng[name] = eng
        assert self.slot_eng[name] == eng, f"slot {slot} used from two engines"
        return self._emit(eng, lambda e, o=out, i=in_: e.dma_start(out=o, in_=i),
                          wait, (name, 16))

    def drain(self, eng):
        waits = [(n, self.cnt[n]) for n in self.cnt
                 if n.startswith("D_") and self.slot_eng.get(n) == eng
                 and self.cnt[n] > 0]
        hs = "hs_" + eng
        if hs not in self.sems:
            self.sems[hs] = self.nc.alloc_semaphore(hs)
            self.cnt[hs] = 0
        return self._emit(eng, lambda e: e.nop(), waits, (hs, 1))

    def barrier(self):
        toks = []
        for eng in ENGS:
            if any(n.startswith("D_") and self.slot_eng.get(n) == eng
                   and self.cnt[n] > 0 for n in self.cnt):
                toks.append(self.drain(eng))
        toks += [(s, self.cnt[s]) for s in ("vv", "ss", "tt", "gg", "cc")
                 if self.cnt[s] > 0]
        for e in ENGS:
            self.pend[e] = list(toks)

    def coll(self, kind, ins, outs, wait=()):
        def fn(g, kind=kind, ins=ins, outs=outs):
            with allow_ici_all_to_all():
                return g.collective_compute(
                    kind, AL.bypass, replica_groups=[list(range(R))],
                    ins=[a.opt() for a in ins], outs=[a.opt() for a in outs])
        return self._emit("gpsimd", fn, wait, ("cc", 1))

    def build(self, block):
        hw = {e: {} for e in ENGS}

        def runner(eng):
            def run(engobj):
                h = hw[eng]
                for waits, fn, inc in self.q[eng]:
                    for (s, v) in waits:
                        if h.get(s, 0) < v:
                            engobj.wait_ge(self.sems[s], v)
                            h[s] = v
                    inst = fn(engobj)
                    if inc is not None:
                        s, amt = inc
                        inst.then_inc(self.sems[s], amt)
            return run
        block.sync(runner("sync"))
        block.vector(runner("vector"))
        block.scalar(runner("scalar"))
        block.tensor(runner("tensor"))
        block.gpsimd(runner("gpsimd"))


class Arena:
    def __init__(self, nc, name, cols, dtype):
        self.t = nc.alloc_sbuf_tensor(name, [128, cols], dtype)
        self.cols, self.off, self.marks = cols, 0, []

    def take(self, cols):
        assert self.off + cols <= self.cols, \
            f"{self.t.name} overflow {self.off}+{cols}>{self.cols}"
        ap = self.t[:, self.off:self.off + cols]
        self.off += cols
        return ap

    def push(self):
        self.marks.append(self.off)

    def pop(self):
        self.off = self.marks.pop()


def build_program(stage=STAGE):
    nc = bass.Bass("TRN2", target_bir_lowering=False,
                   detect_race_conditions=False)

    def din(name, shape, dtype=F32):
        return nc.dram_tensor(name, list(shape), dtype, kind="ExternalInput")

    x_tok = din("x_tok", [D, TOK])
    x_dsh = din("x_dsh", [128, T])
    wstackT = din("wstackT", [D, 96])
    wdt2T = din("wdt2T", [64, 128])
    b_dt2c = din("b_dt2c", [128, 1])
    logAc = din("logAc", [128, N16])
    dskipc = din("dskipc", [128, 1])
    l0w_T = din("l0w_T", [D, D])
    dishaI = din("dishaI", [64, D])
    pool_pat = din("pool_pat", [128, 64], BF16)
    w_inT = din("w_inT", [D, 2 * D])
    wdT = din("wdT", [D, D])
    wBT = din("wBT", [D, D])
    wCT = din("wCT", [D, D])
    woT = din("woT", [D, D])
    b_in_ppI = din("b_in_ppI", [128, 16])
    b_d_ppI = din("b_d_ppI", [128, 8])
    b_B_ppI = din("b_B_ppI", [128, 8])
    b_C_ppI = din("b_C_ppI", [128, 8])
    b_o_ppI = din("b_o_ppI", [128, 8])
    b_cv_ppI = din("b_cv_ppI", [128, 8])
    wcv_ppI = din("wcv_ppI", [128, 32])
    g_ppI = din("g_ppI", [128, 8])
    be_ppI = din("be_ppI", [128, 8])
    w1T = din("w1T", [D, 4 * D])
    w2T = din("w2T", [4 * D, D])
    w1N = din("w1N", [4 * D, D])
    w2N = din("w2N", [D, 4 * D])
    b1_ppI = din("b1_ppI", [128, 32])
    b2_ppI = din("b2_ppI", [128, 8])
    lnlI = din("lnlI", [128, 4], BF16)
    ones_bfI = din("ones_bfI", [128, 1], BF16)
    id_bfI = din("id_bfI", [128, 128], BF16)
    constr = din("constr", [1, NCONST])
    halomskI = din("halomskI", [128, 8])
    halozI = din("halozI", [128, 1])
    hmqI = din("hmqI", [128, 8])
    hmpaI = din("hmpaI", [128, 8])

    out_c = nc.dram_tensor("out_c", [D, TOK], F32, kind="ExternalOutput")

    ag1_src = nc.dram_tensor("ag1_src", [98, TOK], F32)
    ag1_dst = nc.dram_tensor("ag1_dst", [R, 98, TOK], F32)
    ya_src = nc.dram_tensor("ya_src", [R, 128, TOK], F32)
    ya_dst = nc.dram_tensor("ya_dst", [R, 128, TOK], F32)
    x3t_src = nc.dram_tensor("x3t_src", [128, 24], F32)
    x3t_dst = nc.dram_tensor("x3t_dst", [R, 128, 24], F32)
    qp_src = nc.dram_tensor("qp_src", [128, 16], F32)
    qp_dst = nc.dram_tensor("qp_dst", [R, 128, 16], F32)
    browD = nc.dram_tensor("browD", [8, TOK], F32)
    sv_spill = nc.dram_tensor("sv_spill", [D, TOK], BF16)
    inp_spill = nc.dram_tensor("inp_spill", [D, TH], BF16)
    cm_spill = nc.dram_tensor("cm_spill", [D, TH], BF16)
    ec_spill = nc.dram_tensor("ec_spill", [D, TOK], BF16)
    bw1_dram = nc.dram_tensor("bw1_dram", [D, 4 * D], BF16)
    bw2_dram = nc.dram_tensor("bw2_dram", [4 * D, D], BF16)

    XS = [nc.alloc_sbuf_tensor(f"xs{k}", [128, TOK], F32) for k in range(DC)]
    AF = Arena(nc, "AF", 12800, F32)
    AB = Arena(nc, "AB", 34560, BF16)
    SM = Arena(nc, "SM", 2000, F32)
    ROWS = nc.alloc_sbuf_tensor("ROWS", [8, TOK], F32)
    id_sb = nc.alloc_sbuf_tensor("id_sb", [128, 128], BF16)
    lnl_sb = nc.alloc_sbuf_tensor("lnl_sb", [128, 4], BF16)
    ones_sb = nc.alloc_sbuf_tensor("ones_sb", [128, 1], BF16)

    ps_y = nc.alloc_psum_tensor("ps_y", [128, 2048], F32)
    psA = nc.alloc_psum_tensor("psA", [128, 1024], F32)
    psB = nc.alloc_psum_tensor("psB", [128, 1024], F32)
    ps0 = psA[:, 0:512]
    ps1 = psA[:, 512:1024]
    ps_s = psB[:, 0:512]
    ps_p = psB[:, 512:1024]

    p = Prog(nc)

    def psget(name):
        return p.psfree.get(name)

    def psset(name, tok):
        p.psfree[name] = tok

    cst = SM.take(NCONST)
    A_sb = SM.take(N16)
    iA_sb = SM.take(N16)
    dsk = SM.take(1)
    bdt2 = SM.take(1)
    carry0 = SM.take(N16)
    bin_pp = SM.take(16)
    binh_pp = SM.take(8)
    bd_pp = SM.take(8)
    nbd_pp = SM.take(8)
    bB_pp = SM.take(8)
    bC_pp = SM.take(8)
    bo_pp = SM.take(8)
    bcv_pp = SM.take(8)
    wcv = SM.take(32)
    g2_pp = SM.take(8)
    be2_pp = SM.take(8)
    b1p = SM.take(32)
    b2p = SM.take(8)
    a1_pp = SM.take(32)
    a2_pp = SM.take(8)
    ascr = SM.take(1)
    cL1 = SM.take(8)
    cE1 = SM.take(8)
    H_a = SM.take(8)
    H_b = SM.take(8)
    qpt = SM.take(16)
    s96n = SM.take(1)
    hmsk = SM.take(8)
    hz = SM.take(1)
    hmq = SM.take(8)
    hmpa = SM.take(8)

    p.dma("sync", cst[:, :], constr[0:1, :].partition_broadcast(128), slot="ini")
    p.dma("sync", A_sb[:, :], logAc[:, :], slot="ini")
    for (dst, src) in ((dsk, dskipc), (bdt2, b_dt2c), (bin_pp, b_in_ppI),
                       (bd_pp, b_d_ppI), (bB_pp, b_B_ppI), (bC_pp, b_C_ppI),
                       (bo_pp, b_o_ppI), (bcv_pp, b_cv_ppI), (wcv, wcv_ppI),
                       (g2_pp, g_ppI), (be2_pp, be_ppI), (b1p, b1_ppI),
                       (b2p, b2_ppI), (hmsk, halomskI), (hz, halozI),
                       (hmq, hmqI), (hmpa, hmpaI)):
        p.dma("sync", dst[:, :], src[:, :], slot="ini")
    p.dma("sync", id_sb[:, :], id_bfI[:, :], slot="ini")
    p.dma("sync", lnl_sb[:, :], lnlI[:, :], slot="ini")
    p.dma("sync", ones_sb[:, :], ones_bfI[:, :], slot="ini")
    tk_ini = p.drain("sync")
    tk_c = tk_la = tk_bp = tk_id = tk_lnl = tk_ones = tk_ini
    tk_A = p.s(lambda s: s.activation(A_sb[:, :], A_sb[:, :], ACT.Exp),
               wait=[tk_la, tk_c])
    tk_A = p.v(lambda v: v.tensor_scalar(A_sb[:, :], A_sb[:, :], -1.0, None,
                                         AL.mult), wait=[tk_A])
    tk_iA = p.v(lambda v: v.tensor_scalar(iA_sb[:, :], A_sb[:, :], 1e-8, None,
                                          AL.add), wait=[tk_A])
    tk_iA = p.v(lambda v: v.reciprocal(iA_sb[:, :], iA_sb[:, :]), wait=[tk_iA])
    tk_nbd = p.v(lambda v: v.tensor_scalar(nbd_pp[:, :], bd_pp[:, :], -1.0, None,
                                           AL.mult), wait=[tk_bp])
    tk_binh = p.v(lambda v: v.tensor_scalar(binh_pp[:, :], bin_pp[:, 0:8],
                                            hz[:, 0:1], None, AL.mult),
                  wait=[tk_bp])

    def mm_group(psname, psum, pairs, extra_free=None, start0=True,
                 stop_last=True):
        tok = None
        n = len(pairs)
        free = psget(psname)
        for i, (lh, rh, w) in enumerate(pairs):
            ww = list(w)
            if i == 0:
                ww += [free, extra_free]
            tok = p.t(lambda t, lh=lh, rh=rh, i=i, psum=psum, n=n: t.matmul(
                psum, lh, rh, start=(start0 and i == 0),
                stop=(stop_last and i == n - 1)), wait=ww)
        return tok

    def bcast(dst, dram_row_ap, slot, wait=()):
        return p.dma("gpsimd", dst, dram_row_ap.partition_broadcast(128),
                     wait=wait, slot=slot)

    def kload(eng, out2d, dram_ap, nchunk, slot, wait=()):
        in3 = dram_ap.rearrange("(c p) m -> p c m", p=128)
        out3 = out2d.rearrange("p (c m) -> p c m", c=nchunk)
        return p.dma(eng, out3, in3, wait=wait, slot=slot)

    def ln_stats(width=TOK):
        AB.push()
        xb = [AB.take(512) for _ in range(DC)]
        sq = [AB.take(512) for _ in range(DC)]
        evs = []
        xs_ready = p.cur("vv")
        prevg = {}
        for i in range(width // 512):
            tcp = []
            for k in range(DC):
                t1 = p.v(lambda v, k=k, i=i, xb=xb: v.tensor_copy(
                    xb[k][:, :], XS[k][:, i * 512:(i + 1) * 512]),
                    wait=[prevg.get(k)])
                t2 = p.s(lambda s, k=k, i=i, sq=sq: s.activation(
                    sq[k][:, :], XS[k][:, i * 512:(i + 1) * 512], ACT.Square),
                    wait=[xs_ready, prevg.get(k)])
                tcp.append((t1, t2))
            pairs = [(lnl_sb[:, 0:2], xb[k][:, :], [tcp[k][0], tk_lnl])
                     for k in range(DC)]
            pairs += [(lnl_sb[:, 2:4], sq[k][:, :], [tcp[k][1]])
                      for k in range(DC)]
            tok = mm_group("s", ps_s[0:2, :], pairs)
            ev = p.v(lambda v, i=i: v.tensor_copy(
                ROWS[0:2, i * 512:(i + 1) * 512], ps_s[0:2, :]), wait=[tok])
            psset("s", ev)
            evs.append(ev)
            for k in range(DC):
                prevg[k] = tok
        for e in ("vector", "scalar", "gpsimd", "sync"):
            p.pend[e] = p.pend[e] + [evs[-1]]
        AB.pop()
        return evs

    def finalize_reps(width, muR, m2R, scrF, rR, rcR=None):
        t1 = p.s(lambda s: s.activation(scrF, muR, ACT.Square))
        t2 = p.v(lambda v: v.tensor_tensor(m2R, m2R, scrF, AL.subtract),
                 wait=[t1])
        t3 = p.s(lambda s: s.activation(scrF, m2R, ACT.Ln,
                                        bias=cst[:, C_EPS:C_EPS + 1]), wait=[t2])
        t4 = p.s(lambda s: s.activation(rR, scrF, ACT.Exp,
                                        scale=cst[:, C_NEGH:C_NEGH + 1]),
                 wait=[t3])
        if rcR is None:
            return t4
        t5 = p.s(lambda s: s.activation(scrF, rR, ACT.Square), wait=[t4])
        t6 = p.v(lambda v: v.tensor_tensor(scrF, scrF, m2R, AL.mult), wait=[t5])
        t7 = p.s(lambda s: s.activation(scrF, scrF, ACT.Ln,
                                        bias=cst[:, C_EPS:C_EPS + 1]), wait=[t6])
        t8 = p.s(lambda s: s.activation(scrF, scrF, ACT.Exp,
                                        scale=cst[:, C_NEGH:C_NEGH + 1]),
                 wait=[t7])
        t9 = p.v(lambda v: v.tensor_tensor(rcR, rR, scrF, AL.mult), wait=[t8])
        return t9

    AB.push()
    wst = AB.take(8 * 96)
    tk_wst = kload("gpsimd", wst[:, :], wstackT.ap(), 8, "wst")
    for k in range(DC):
        p.dma("sync", XS[k][:, :], x_tok[k * 128:(k + 1) * 128, :],
              slot="xt")
    p.barrier()
    evs1 = ln_stats()
    tk_st1 = p.dma("sync", browD[0:2, :], ROWS[0:2, :], wait=evs1, slot="brw")
    s96pairs = [(wst[:, kc * 96:(kc + 1) * 96], ones_sb[:, :], [tk_wst, tk_ones])
                for kc in range(DC)]
    tk_s96 = mm_group("1", ps1[0:96, 0:1], s96pairs)
    tk_s96 = p.v(lambda v: v.tensor_scalar(s96n[0:96, :], ps1[0:96, 0:1], -1.0,
                                           None, AL.mult), wait=[tk_s96])
    psset("1", tk_s96)
    AF.push()
    mur96 = AF.take(TOK)
    rr96 = AF.take(TOK)
    scr96 = AF.take(TOK)
    tk_mu96 = bcast(mur96[:, :], browD[0:1, :], "mu", wait=[tk_st1])
    tk_m296 = bcast(rr96[:, :], browD[1:2, :], "rr", wait=[tk_st1])
    p.pend["vector"] += [tk_mu96, tk_m296]
    p.pend["scalar"] += [tk_mu96, tk_m296]
    tk_fin1 = finalize_reps(TOK, mur96[:, :], rr96[:, :], scr96[:, :],
                            rr96[:, :])
    tk_rr96 = tk_fin1
    xbz = [AB.take(TOK) for _ in range(DC)]
    tkxb = [p.v(lambda v, k=k: v.tensor_copy(xbz[k][:, :], XS[k][:, :]))
            for k in range(DC)]
    z96 = AF.take(TOK)
    tk_ag_in = []
    for i in range(TOK // 512):
        pairs = [(wst[:, kc * 96:(kc + 1) * 96],
                  xbz[kc][:, i * 512:(i + 1) * 512], [tkxb[kc], tk_wst])
                 for kc in range(DC)]
        tokm = mm_group("0", ps0[0:96, :], pairs)
        tf = p.v(lambda v, i=i: v.scalar_tensor_tensor(
            z96[0:96, i * 512:(i + 1) * 512], mur96[0:96, i * 512:(i + 1) * 512],
            s96n[0:96, 0:1], ps0[0:96, :], AL.mult, AL.add),
            wait=[tokm, tk_mu96, tk_fin1, tk_s96])
        psset("0", tf)
        evp = p.v(lambda v, i=i: v.tensor_tensor(
            z96[0:96, i * 512:(i + 1) * 512], z96[0:96, i * 512:(i + 1) * 512],
            rr96[0:96, i * 512:(i + 1) * 512], AL.mult), wait=[tf, tk_rr96])
        tk_ag_in.append(evp)
    p.dma("gpsimd", ag1_src[0:96, :], z96[0:96, :], wait=tk_ag_in, slot="agf")
    p.dma("gpsimd", ag1_src[96:97, :], mur96[0:1, :],
          wait=[tk_fin1] + tk_ag_in, slot="agf")
    p.dma("gpsimd", ag1_src[97:98, :], rr96[0:1, :],
          wait=[tk_fin1] + tk_ag_in, slot="agf")
    tk_ag1 = p.coll("AllGather", [ag1_src.ap()], [ag1_dst.ap()],
                    wait=[p.drain("gpsimd")])
    AF.pop()
    AB.pop()

    p.barrier()
    AF.push()
    AB.push()
    wdt2_sb = AB.take(128)
    tk_wdt2 = p.dma("gpsimd", wdt2_sb[0:64, :], wdt2T[:, :], slot="wd2")
    xd = AF.take(2048)
    dtt = AF.take(2048)
    xnf = AF.take(2048)
    aT = [AF.take(2048), AF.take(2048)]
    bscan = AF.take(2048)
    murB = AB.take(2048)
    rrB = AB.take(2048)
    zdtB = AB.take(2048)
    xnB = AB.take(2048)
    browB = [AB.take(2048), AB.take(2048)]
    crowB = [AB.take(2048), AB.take(2048)]
    brepB = [AB.take(2048), AB.take(2048)]
    crepB = [AB.take(2048), AB.take(2048)]
    onesr = AB.take(128)
    hB = AB.take(2048)
    pB = [AB.take(2048), AB.take(2048)]
    wtmpP = AB.take(1024)
    sgnP = AB.take(1024)
    tk_onesr = p.v(lambda v: v.memset(onesr[:, :], 1.0))

    P0B = []
    for kc in range(DC):
        for q in range(4):
            P0B.append(("s", w1T[kc * 128:(kc + 1) * 128,
                                 q * 1024:(q + 1) * 1024],
                        bw1_dram[kc * 128:(kc + 1) * 128,
                                 q * 1024:(q + 1) * 1024]))
            P0B.append(("a1", w1N[(kc * 4 + q) * 128:(kc * 4 + q + 1) * 128,
                                  :], kc * 4 + q))
    for kc2 in range(32):
        P0B.append(("s", w2T[kc2 * 128:(kc2 + 1) * 128, :],
                    bw2_dram[kc2 * 128:(kc2 + 1) * 128, :]))
        kco, q = kc2 // 4, kc2 % 4
        P0B.append(("a2", w2N[kco * 128:(kco + 1) * 128,
                              q * 1024:(q + 1) * 1024], kco))
    tk_az2 = p.v(lambda v: v.memset(a2_pp[:, :], 0.0))
    pz_cons = {}
    pz_load = {}

    def p0b_load(k):
        src = P0B[k][1]
        pz_load[k] = p.dma("gpsimd", wtmpP[:, :], src,
                           wait=[pz_cons.get(k - 1)], slot="wbl")

    def p0b_consume(k):
        kind = P0B[k][0]
        if kind == "s":
            dst = P0B[k][2]
            tsg = p.s(lambda s: s.activation(sgnP[:, :], wtmpP[:, :],
                                             ACT.Sign),
                      wait=[pz_load[k], pz_state["store"]])
            pz_state["store"] = p.dma("gpsimd", dst, sgnP[:, :],
                                      wait=[tsg, pz_state["store"]],
                                      slot="wst1")
            pz_cons[k] = tsg
        elif kind == "a1":
            col = P0B[k][2]
            trd = p.v(lambda v, col=col: v.tensor_reduce(
                a1_pp[:, col:col + 1], wtmpP[:, :], AXL.X, AL.add,
                apply_absolute_value=True), wait=[pz_load[k]])
            pz_cons[k] = trd
            pz_state["a1last"] = trd
        else:
            col = P0B[k][2]
            trd = p.v(lambda v: v.tensor_reduce(
                ascr[:, 0:1], wtmpP[:, :], AXL.X, AL.add,
                apply_absolute_value=True),
                wait=[pz_load[k], pz_state["a2chain"]])
            tad = p.v(lambda v, col=col: v.tensor_tensor(
                a2_pp[:, col:col + 1], a2_pp[:, col:col + 1], ascr[:, 0:1],
                AL.add), wait=[trd, pz_state["a2chain"]])
            pz_state["a2chain"] = tad
            pz_cons[k] = trd
            pz_state["a2last"] = tad
    pz_state = {"store": None, "a2chain": tk_az2, "a1last": None,
                "a2last": None}

    tk_yevac = tk_y_dma = None
    tk_dt = None
    tk_xncons = None
    tk_pcons = {}
    tk_pcons2 = {}
    tk_wcons = {}
    tk_evb = {}
    tk_evc = {}
    tk_acons = {}
    tk_brep = {}
    tk_crep = {}
    y_dma_toks = []
    for tc in range(R):
        tk_xd = p.dma("sync", xd[:, :], x_dsh[:, tc * TOK:(tc + 1) * TOK],
                      wait=[tk_y_dma], slot="xd")
        tk_mu = bcast(murB[:, :], ag1_dst[tc, 96:97, :], "mu",
                      wait=[tk_ag1, tk_xncons])
        tk_rr = bcast(rrB[:, :], ag1_dst[tc, 97:98, :], "rr",
                      wait=[tk_ag1, tk_xncons])
        tk_zdt = p.dma("gpsimd", zdtB[0:64, :], ag1_dst[tc, 0:64, :],
                       wait=[tk_ag1, p.cur("tt")], slot="zd")
        t1 = p.v(lambda v: v.tensor_tensor(xnf[:, :], xd[:, :], murB[:, :],
                                           AL.subtract),
                 wait=[tk_xd, tk_mu, tk_yevac])
        tk_xn = p.v(lambda v: v.tensor_tensor(xnf[:, :], xnf[:, :], rrB[:, :],
                                              AL.mult), wait=[t1, tk_rr])
        tk_xnb = p.v(lambda v: v.tensor_copy(xnB[:, :], xnf[:, :]), wait=[tk_xn])
        tk_xncons = tk_xnb
        for i in range(4):
            nmp = "0" if i % 2 == 0 else "1"
            psd = ps0 if i % 2 == 0 else ps1
            tmm = mm_group(nmp, psd[:, :],
                           [(wdt2_sb[0:64, :], zdtB[0:64, i * 512:(i + 1) * 512],
                             [tk_zdt, tk_wdt2])])
            ta1 = p.s(lambda s, psd=psd: s.activation(
                aT[0][:, 0:512], psd[:, :], ACT.Abs, bias=bdt2[:, 0:1]),
                wait=[tmm, tk_dt] + ([tk_acons.get(0), tk_acons.get(1)]
                                     if i == 0 else []))
            ta2 = p.s(lambda s: s.activation(
                aT[0][:, 512:1024], aT[0][:, 0:512], ACT.Exp,
                scale=cst[:, C_NEG1:C_NEG1 + 1]), wait=[ta1])
            ta3 = p.s(lambda s: s.activation(
                aT[0][:, 0:512], aT[0][:, 512:1024], ACT.Ln,
                bias=cst[:, C_ONE:C_ONE + 1]), wait=[ta2])
            ta4 = p.s(lambda s, psd=psd: s.activation(
                aT[0][:, 512:1024], psd[:, :], ACT.Relu, bias=bdt2[:, 0:1]),
                wait=[ta3])
            psset(nmp, ta4)
            tk_dt = p.v(lambda v, i=i: v.tensor_tensor(
                dtt[:, i * 512:(i + 1) * 512], aT[0][:, 0:512],
                aT[0][:, 512:1024], AL.add), wait=[ta4])
        tk_id_free = tk_yevac
        def rep_rows(n):
            sl = n % 2
            tk_brep[sl] = p.dma("gpsimd", browB[sl][0:1, :],
                                ag1_dst[tc, 64 + n:65 + n, :],
                                wait=[tk_brep.get(sl + 2), tk_ag1],
                                slot=f"br{sl}")
            tk_crep[sl] = p.dma("gpsimd", crowB[sl][0:1, :],
                                ag1_dst[tc, 80 + n:81 + n, :],
                                wait=[tk_crep.get(sl + 2), tk_ag1],
                                slot=f"cr{sl}")

        def rep_produce(n):
            sl = n % 2
            evb, evc = [], []
            mmb = mmc = None
            mmq = {}
            for i in range(4):
                bname = ["0", "1", "s", "p"][i]
                psd = [ps0, ps1, ps_s, ps_p][i]
                mmb = p.t(lambda t, sl=sl, i=i, psd=psd: t.matmul(
                    psd[:, :], onesr[0:1, :],
                    browB[sl][0:1, i * 512:(i + 1) * 512],
                    start=True, stop=True),
                    wait=[tk_brep[sl], tk_onesr, psget(bname)])
                mmq[i] = mmb
            for half, pst in ((0, psA), (1, psB)):
                ev = p.s(lambda s, sl=sl, half=half, pst=pst, n=n:
                         s.activation(
                    brepB[sl][:, half * 1024:(half + 1) * 1024], pst[:, :],
                    ACT.Identity, scale=iA_sb[:, n:n + 1]),
                    wait=[mmq[2 * half], mmq[2 * half + 1], tk_iA,
                          tk_wcons.get(sl)])
                psset(["0", "1", "s", "p"][2 * half], ev)
                psset(["0", "1", "s", "p"][2 * half + 1], ev)
                evb.append(ev)
            tk_brep[sl + 2] = mmb
            for i in range(4):
                cname = ["0", "1", "s", "p"][i]
                psd = [ps0, ps1, ps_s, ps_p][i]
                mmc = p.t(lambda t, sl=sl, i=i, psd=psd: t.matmul(
                    psd[:, :], onesr[0:1, :],
                    crowB[sl][0:1, i * 512:(i + 1) * 512],
                    start=True, stop=True),
                    wait=[tk_crep[sl], tk_onesr, psget(cname)])
                mmq[i] = mmc
            for half, pst in ((0, psA), (1, psB)):
                ev = p.s(lambda s, sl=sl, half=half, pst=pst: s.activation(
                    crepB[sl][:, half * 1024:(half + 1) * 1024], pst[:, :],
                    ACT.Identity),
                    wait=[mmq[2 * half], mmq[2 * half + 1],
                          tk_pcons2.get(sl)])
                psset(["0", "1", "s", "p"][2 * half], ev)
                psset(["0", "1", "s", "p"][2 * half + 1], ev)
                evc.append(ev)
            tk_crep[sl + 2] = mmc
            tk_evb[sl] = evb
            tk_evc[sl] = evc

        rep_rows(0)
        rep_rows(1)
        rep_produce(0)
        for n in range(N16):
            sl = n % 2
            ta = p.s(lambda s, sl=sl, n=n: s.activation(
                aT[sl][:, :], dtt[:, :], ACT.Exp, scale=A_sb[:, n:n + 1]),
                wait=[tk_dt, tk_A, tk_acons.get(sl)])
            if n + 2 < N16:
                rep_rows(n + 2)
            if n + 1 < N16:
                rep_produce(n + 1)
            p0b_load(tc * 16 + n)
            tw = p.v(lambda v, sl=sl: v.tensor_tensor(
                pB[sl][:, :], xnB[:, :], brepB[sl][:, :], AL.mult),
                wait=[tk_xnb, tk_pcons.get(sl)] + tk_evb[sl])
            tk_wcons[sl] = tw
            tb = p.v(lambda v, sl=sl: v.scalar_tensor_tensor(
                bscan[:, :], aT[sl][:, :], 1.0, pB[sl][:, :], AL.subtract,
                AL.mult), wait=[ta, tw])
            if tc % 4 == 0:
                th = p.v(lambda v, sl=sl: v.tensor_tensor_scan(
                    hB[:, :], aT[sl][:, :], bscan[:, :], 0.0, AL.mult, AL.add),
                    wait=[tb])
            else:
                th = p.v(lambda v, sl=sl, n=n: v.tensor_tensor_scan(
                    hB[:, :], aT[sl][:, :], bscan[:, :], carry0[:, n:n + 1],
                    AL.mult, AL.add), wait=[tb])
            tk_acons[sl] = th
            p.v(lambda v, n=n: v.tensor_copy(carry0[:, n:n + 1],
                                             hB[:, 2047:2048]), wait=[th])
            tp = p.v(lambda v, sl=sl: v.tensor_tensor(
                pB[sl][:, :], hB[:, :], crepB[sl][:, :], AL.mult),
                wait=[th] + tk_evc[sl])
            tk_pcons2[sl] = tp
            last = None
            for i in range(4):
                ww = [tp, tk_id] + ([tk_id_free] if (n == 0 and i == 0) else [])
                last = p.t(lambda t, sl=sl, i=i, n=n: t.matmul(
                    ps_y[:, i * 512:(i + 1) * 512], id_sb[:, :],
                    pB[sl][:, i * 512:(i + 1) * 512], start=(n == 0),
                    stop=(n == N16 - 1)), wait=ww if i == 0 else [])
            tk_pcons[sl] = last
            p0b_consume(tc * 16 + n)
        tk_yevac = p.v(lambda v: v.scalar_tensor_tensor(
            xd[:, :], xnf[:, :], dsk[:, 0:1], ps_y[:, :], AL.mult, AL.add),
            wait=[tk_pcons[0], tk_pcons[1], tk_xn])
        tk_y_dma = p.dma("gpsimd", ya_src[tc, :, :], xd[:, :],
                         wait=[tk_yevac], slot="yst")
        y_dma_toks.append(tk_y_dma)
    psset("y", tk_yevac)
    tk_a1 = p.v(lambda v: v.tensor_scalar(a1_pp[:, :], a1_pp[:, :], 1.0 / D,
                                          None, AL.mult),
                wait=[pz_state["a1last"]])
    tk_a2 = p.v(lambda v: v.tensor_scalar(a2_pp[:, :], a2_pp[:, :],
                                          1.0 / (4 * D), None, AL.mult),
                wait=[pz_state["a2last"]])
    tk_bw_st = p.drain("gpsimd")
    tk_a2a = p.coll("AllToAll", [ya_src.ap()], [ya_dst.ap()],
                    wait=[tk_bw_st])
    AF.pop()
    AB.pop()

    p.barrier()
    AF.push()
    AB.push()
    ytl = [AF.take(2048), AF.take(2048)]
    tk_x2 = []
    ty_prev = {}
    for k in range(DC):
        sl = k % 2
        tyl = p.dma("sync", ytl[sl][:, :], ya_dst[k, :, :],
                    wait=[tk_a2a, ty_prev.get(sl)], slot=f"yt{sl}")
        tadd = p.v(lambda v, k=k, sl=sl: v.tensor_tensor(
            XS[k][:, :], XS[k][:, :], ytl[sl][:, :], AL.add), wait=[tyl])
        ty_prev[sl] = tadd
        tk_x2.append(tadd)
    if stage == 1:
        p.barrier()
        for k in range(DC):
            p.dma("sync", out_c[k * 128:(k + 1) * 128, :], XS[k][:, :],
                  slot="out")
        p.g(lambda g: g.nop(), wait=[p.drain("sync")])
        return nc, p

    evs2 = ln_stats()
    tk_bw2 = p.dma("sync", browD[0:2, :], ROWS[0:2, :], wait=evs2, slot="brw")
    murF = AF.take(2048)
    rrF = AF.take(2048)
    tmpF = AF.take(2048)
    tk_mu = bcast(murF[:, :], browD[0:1, :], "mu", wait=[tk_bw2])
    tk_m2 = bcast(rrF[:, :], browD[1:2, :], "rr", wait=[tk_bw2])
    p.pend["vector"] += [tk_mu, tk_m2]
    p.pend["scalar"] += [tk_mu, tk_m2]
    tk_rr = finalize_reps(TOK, murF[:, :], rrF[:, :], tmpF[:, :], rrF[:, :])
    xn2B = [AB.take(2048) for _ in range(DC)]
    tk_xn2 = []
    for k in range(DC):
        t1 = p.v(lambda v, k=k: v.tensor_tensor(tmpF[:, :], XS[k][:, :],
                                                murF[:, :], AL.subtract),
                 wait=[tk_mu, tk_rr, tk_x2[k]] + ([tk_xn2[-1]] if tk_xn2 else []))
        t2 = p.v(lambda v, k=k: v.tensor_tensor(xn2B[k][:, :], tmpF[:, :],
                                                rrF[:, :], AL.mult),
                 wait=[t1, tk_rr])
        tk_xn2.append(t2)
    wfB = AB.take(8192)
    tk_wf = kload("gpsimd", wfB[:, :], l0w_T.ap(), 8, "wf")
    diB = AB.take(1024)
    tk_di = p.dma("gpsimd", diB[0:64, :], dishaI[:, :], slot="di")
    ppB = AB.take(64)
    tk_ppat = p.dma("sync", ppB[:, 0:64], pool_pat[:, :], slot="pp")
    poolB = AB.take(512)
    for i in range(4):
        ppairs = [(ppB[:, 0:64], xn2B[kc][:, i * 512:(i + 1) * 512],
                   [tk_xn2[kc], tk_ppat]) for kc in range(DC)]
        tkpool = mm_group("p", ps_p[0:64, :], ppairs)
        tkpev = p.v(lambda v: v.tensor_copy(poolB[0:64, :], ps_p[0:64, :]),
                    wait=[tkpool, p.cur("tt")])
        psset("p", tkpev)
        for m in range(DC):
            nmp = "0" if (i * 8 + m) % 2 == 0 else "1"
            psd = ps0 if (i * 8 + m) % 2 == 0 else ps1
            pairs = [(wfB[:, kc * 1024 + m * 128:kc * 1024 + (m + 1) * 128],
                      xn2B[kc][:, i * 512:(i + 1) * 512], [tk_xn2[kc], tk_wf])
                     for kc in range(DC)]
            pairs.append((diB[0:64, m * 128:(m + 1) * 128], poolB[0:64, :],
                          [tkpev, tk_di]))
            tkm = mm_group(nmp, psd[:, :], pairs)
            tke = p.v(lambda v, m=m, i=i, psd=psd: v.tensor_tensor(
                XS[m][:, i * 512:(i + 1) * 512], XS[m][:, i * 512:(i + 1) * 512],
                psd[:, :], AL.add), wait=[tkm])
            psset(nmp, tke)
    tk_x3 = [psget("0"), psget("1")]
    for k in range(DC):
        p.dma("gpsimd", x3t_src[:, k * 3:(k + 1) * 3],
              XS[k][:, 2045:2048], wait=tk_x3, slot="x3t")
    tk_agx3 = p.coll("AllGather", [x3t_src.ap()], [x3t_dst.ap()],
                     wait=[p.drain("gpsimd")])
    AF.pop()
    AB.pop()
    if stage == 2:
        p.barrier()
        for k in range(DC):
            p.dma("sync", out_c[k * 128:(k + 1) * 128, :], XS[k][:, :],
                  slot="out")
        p.g(lambda g: g.nop(), wait=[p.drain("sync")])
        return nc, p

    p.barrier()
    AF.push()
    AB.push()
    evs3 = ln_stats()
    tk_bw3 = p.dma("sync", browD[0:2, :], ROWS[0:2, :], wait=evs3, slot="brw")
    AF.push()
    muR3 = AF.take(2048)
    m2R3 = AF.take(2048)
    scR3 = AF.take(2048)
    rR3 = AF.take(2048)
    rcR3 = AF.take(2048)
    tkmu3 = bcast(muR3[:, :], browD[0:1, :], "mu", wait=[tk_bw3])
    tkm23 = bcast(m2R3[:, :], browD[1:2, :], "rr", wait=[tk_bw3])
    p.pend["vector"] += [tkmu3, tkm23]
    p.pend["scalar"] += [tkmu3, tkm23]
    tk_fin3 = finalize_reps(TOK, muR3[:, :], m2R3[:, :], scR3[:, :],
                            rR3[:, :], rcR3[:, :])
    tk_bw3a = tk_bw3
    tk_bw3b = p.dma("sync", browD[1:2, :], rR3[0:1, :],
                    wait=[tk_fin3, tkm23], slot="brw")
    tk_bw3c = p.dma("sync", browD[2:3, :], rcR3[0:1, :],
                    wait=[tk_fin3, tk_bw3b], slot="brw")
    AF.pop()

    xpv = AF.take(28)
    AB.push()
    xallB = AF.take(DC * 24 // 4 * 4) if False else None
    xall = AF.take(192)
    tk_xall = p.dma("sync", xall[:, :].rearrange("p (r c) -> p r c", r=8),
                    x3t_dst.ap().rearrange("r p c -> p r c"),
                    wait=[tk_agx3, tk_fin3, tk_bw3b, tk_bw3c], slot="xal")
    tk_sel = p.v(lambda v: v.tensor_scalar(xpv[:, 0:24], xall[:, 0:24],
                                           hmsk[:, 0:1], None, AL.mult),
                 wait=[tk_xall, tk_bp])
    for j in range(1, R):
        tk_sel = p.v(lambda v, j=j: v.scalar_tensor_tensor(
            xpv[:, 0:24], xall[:, j * 24:(j + 1) * 24], hmsk[:, j:j + 1],
            xpv[:, 0:24], AL.mult, AL.add), wait=[tk_sel])
    hghB = []
    tk_hgh = []
    hpb = [AB.take(4) for _ in range(DC)]
    hps = [AB.take(4) for _ in range(DC)]
    tcst = []
    for k in range(DC):
        t1 = p.v(lambda v, k=k: v.tensor_copy(hpb[k][:, 0:3],
                                              xpv[:, k * 3:k * 3 + 3]),
                 wait=[tk_sel])
        t2 = p.s(lambda s, k=k: s.activation(hps[k][:, 0:3],
                                             xpv[:, k * 3:k * 3 + 3],
                                             ACT.Square), wait=[tk_sel])
        tcst.append((t1, t2))
    pairs = [(lnl_sb[:, 0:2], hpb[k][:, 0:3], [tcst[k][0], tk_lnl])
             for k in range(DC)]
    pairs += [(lnl_sb[:, 2:4], hps[k][:, 0:3], [tcst[k][1]]) for k in range(DC)]
    tkh = mm_group("s", ps_s[0:2, 0:3], pairs)
    tkhe = p.v(lambda v: v.tensor_copy(ROWS[0:2, 0:3], ps_s[0:2, 0:3]),
               wait=[tkh])
    psset("s", tkhe)
    tkhd = p.dma("sync", browD[3:5, 0:3], ROWS[0:2, 0:3],
                 wait=[tkhe, tk_bw3c], slot="brw")
    mur3h = AF.take(4)
    rcr3h = AF.take(4)
    tmp3 = AF.take(4)
    scr3h = AF.take(4)
    r3h = AF.take(4)
    tkb3 = bcast(mur3h[:, 0:3], browD[3:4, 0:3], "m3h", wait=[tkhd])
    tkb4 = bcast(rcr3h[:, 0:3], browD[4:5, 0:3], "r3h", wait=[tkhd])
    p.pend["vector"] += [tkb3, tkb4]
    p.pend["scalar"] += [tkb3, tkb4]
    tkb4 = finalize_reps(3, mur3h[:, 0:3], rcr3h[:, 0:3], scr3h[:, 0:3],
                         r3h[:, 0:3], rcr3h[:, 0:3])
    for k in range(DC):
        hg = AB.take(4)
        hghB.append(hg)
        t1 = p.v(lambda v, k=k: v.tensor_tensor(
            tmp3[:, 0:3], xpv[:, k * 3:k * 3 + 3], mur3h[:, 0:3], AL.subtract),
            wait=[tkb3] + ([tk_hgh[-1]] if tk_hgh else []))
        t2 = p.v(lambda v: v.tensor_tensor(tmp3[:, 0:3], tmp3[:, 0:3],
                                           rcr3h[:, 0:3], AL.mult),
                 wait=[t1, tkb4])
        t3 = p.v(lambda v, k=k, hg=hg: v.tensor_scalar(
            hg[:, 0:3], tmp3[:, 0:3], g2_pp[:, k:k + 1], be2_pp[:, k:k + 1],
            AL.mult, AL.add), wait=[t2])
        tk_hgh.append(t3)

    wgtB = AB.take(8192)
    AB.push()
    p2fB = [AB.take(TH), AB.take(TH)]
    zeroB = AB.take(TH)
    ecdB = [AB.take(TH), AB.take(TH)]
    spillB = [AB.take(512), AB.take(512)]
    cmldB = [AB.take(TH), AB.take(TH)]
    tk_zero = p.v(lambda v: v.memset(zeroB[:, :], 0.0))

    tk_sv_dma = {}
    tk_y2loc = {}
    tk_p2mm = {}
    tk_spill_inp = {}
    tk_spill_cm = {}
    tk_inpst = {}
    tk_cmst = {}
    tk_ecs = {}
    tk_hcons = None
    tk_econs = None
    tk_conv = {}
    tk_wlast = None
    tk_win = {}
    tk_hg_prev_g = None
    tk_te_h = {}
    for half in range(2):
        toff = half * TH
        AB.push()
        ueB = [AB.take(TH + 3) for _ in range(DC)]
        AB.push()
        winB = [AB.take(1024), AB.take(1024)]
        hgB = [AB.take(512) for _ in range(DC)]
        svB = [AB.take(512), AB.take(512)]
        AF.push()
        mur3 = AF.take(TH)
        r3r = AF.take(TH)
        rcr = AF.take(TH)
        t1F = AF.take(512)
        if half == 0:
            hguards = [p.cur("vv")]
        else:
            hguards = [tk_te_h[(0, 0)], tk_te_h[(0, 1)], tk_te_h[(0, 2)]]
        tkm3 = bcast(mur3[:, :], browD[0:1, toff:toff + TH], "mu",
                     wait=[tk_bw3a] + hguards)
        tkr3 = bcast(r3r[:, :], browD[1:2, toff:toff + TH], "rr",
                     wait=[tk_bw3b] + hguards)
        tkrc = bcast(rcr[:, :], browD[2:3, toff:toff + TH], "rc",
                     wait=[tk_bw3c] + hguards)
        tk_uev = {}
        tk_ue_halo = [None] * DC
        if half == 1:
            for m in range(DC):
                tk_ue_halo[m] = p.v(lambda v, m=m: v.tensor_copy(
                    ueB[m][:, 0:3], ueB[m][:, TH:TH + 3]),
                    wait=[tk_conv[(0, m)]])
        for i in range(2):
            isl = slice(toff + i * 512, toff + (i + 1) * 512)
            hg_guard = p.cur("tt")
            tk_hg = {}
            tk_hg_prev = None
            for k in range(DC):
                tt1 = p.v(lambda v, k=k, isl=isl, i=i: v.tensor_tensor(
                    t1F[:, :], XS[k][:, isl], mur3[:, i * 512:(i + 1) * 512],
                    AL.subtract), wait=[tkm3, tk_hg_prev, hg_guard])
                tt2 = p.v(lambda v, k=k, i=i: v.tensor_tensor(
                    hgB[k][:, :], t1F[:, :], rcr[:, i * 512:(i + 1) * 512],
                    AL.mult), wait=[tt1, tkrc])
                tt3 = p.v(lambda v, k=k: v.tensor_scalar(
                    hgB[k][:, :], hgB[k][:, :], g2_pp[:, k:k + 1],
                    be2_pp[:, k:k + 1], AL.mult, AL.add), wait=[tt2])
                tt4 = p.v(lambda v, k=k, i=i: v.tensor_tensor(
                    t1F[:, :], t1F[:, :], r3r[:, i * 512:(i + 1) * 512],
                    AL.mult), wait=[tt3, tkr3])
                tt5 = p.v(lambda v, k=k, isl=isl: v.tensor_tensor(
                    XS[k][:, isl], XS[k][:, isl], t1F[:, :], AL.add),
                    wait=[tt4])
                tk_hg_prev = tt5
                tk_hg_prev_g = tt5
                tk_hg[k] = tt3
            for uv in range(2):
                for m in range(DC):
                    sl = m % 2
                    tk_win[sl] = kload(
                        "gpsimd", winB[sl][:, :],
                        w_inT.ap()[:, uv * 1024 + m * 128:uv * 1024 + (m + 1) * 128],
                        8, f"wi{sl}", wait=[tk_win.get(sl + 2)])
                    nmp = "0" if (uv * 8 + m) % 2 == 0 else "1"
                    psd = ps0 if (uv * 8 + m) % 2 == 0 else ps1
                    pairs = [(winB[sl][:, kc * 128:(kc + 1) * 128],
                              hgB[kc][:, :], [tk_hg[kc], tk_win[sl]])
                             for kc in range(DC)]
                    tkm = mm_group(nmp, psd[:, :], pairs)
                    tk_win[sl + 2] = tkm
                    if uv == 0:
                        tev = p.s(lambda s, m=m, i=i, psd=psd: s.activation(
                            ueB[m][:, 3 + i * 512:3 + (i + 1) * 512],
                            psd[:, :], ACT.Identity, bias=bin_pp[:, m:m + 1]),
                            wait=[tkm, tk_ue_halo[m]])
                        tk_uev[(i, m)] = tev
                        psset(nmp, tev)
                    else:
                        ssl = m % 2
                        tsv = p.s(lambda s, ssl=ssl, psd=psd, m=m: s.activation(
                            svB[ssl][:, :], psd[:, :], ACT.Sigmoid,
                            bias=bin_pp[:, 8 + m:9 + m]),
                            wait=[tkm, tk_sv_dma.get(ssl)])
                        psset(nmp, tsv)
                        tk_sv_dma[ssl] = p.dma(
                            "sync",
                            sv_spill[m * 128:(m + 1) * 128,
                                     toff + i * 512:toff + (i + 1) * 512],
                            svB[ssl][:, :], wait=[tsv], slot=f"sv{ssl}")
        if half == 0:
            for m in range(DC):
                tkw = kload("gpsimd", winB[0][:, :],
                            w_inT.ap()[:, m * 128:(m + 1) * 128], 8, "wi0",
                            wait=[p.cur("tt")])
                pairs = [(winB[0][:, kc * 128:(kc + 1) * 128],
                          hghB[kc][:, 0:3], [tk_hgh[kc], tkw])
                         for kc in range(DC)]
                tkm = mm_group("0", ps0[:, 0:3], pairs)
                tev = p.s(lambda s, m=m: s.activation(
                    ueB[m][:, 0:3], ps0[:, 0:3], ACT.Identity,
                    scale=hz[:, 0:1], bias=binh_pp[:, m:m + 1]),
                    wait=[tkm, tk_binh, tk_uev[(0, m)], tk_uev[(1, m)]])
                psset("0", tev)
                tk_ue_halo[m] = tev
        AF.pop()
        AB.pop()
        AB.push()
        ucB = [AB.take(TH) for _ in range(DC)]
        AF.push()
        convF = [AF.take(TH), AF.take(TH)]
        for m in range(DC):
            tc0 = p.v(lambda v, m=m: v.tensor_scalar(
                convF[0][:, :], ueB[m][:, 3:3 + TH],
                wcv[:, m * 4 + 3:m * 4 + 4], bcv_pp[:, m:m + 1],
                AL.mult, AL.add),
                wait=[tk_uev[(0, m)], tk_uev[(1, m)], tk_ue_halo[m],
                      tk_conv.get((half - 1, m)), tk_wlast])
            src, dst = 0, 1
            for j in range(3):
                tc0 = p.v(lambda v, m=m, j=j, src=src, dst=dst:
                          v.scalar_tensor_tensor(
                              convF[dst][:, :], ueB[m][:, j:j + TH],
                              wcv[:, m * 4 + j:m * 4 + j + 1],
                              convF[src][:, :], AL.mult, AL.add), wait=[tc0])
                src, dst = dst, src
            tk_conv[(half, m)] = p.v(lambda v, m=m, src=src: v.tensor_copy(
                ucB[m][:, :], convF[src][:, :]), wait=[tc0, p.cur("tt")])
        AF.pop()
        AF.push()
        decayF = [AF.take(TH) for _ in range(DC)]
        hF = AF.take(TH)
        eF = AF.take(TH)
        inldF = [AF.take(TH), AF.take(TH)]
        specs = [("d", wdT), ("B", wBT), ("C", wCT)]
        tk_sp = {}
        for (nm, wt) in specs:
            tkwg = kload("gpsimd", wgtB[:, :], wt.ap(), 8, "wg",
                         wait=[tk_wlast, p.cur("tt")])
            for m in range(DC):
                for i in range(2):
                    nmp = "0" if (m * 2 + i) % 2 == 0 else "1"
                    psd = ps0 if (m * 2 + i) % 2 == 0 else ps1
                    pairs = [(wgtB[:, kc * 1024 + m * 128:kc * 1024 + (m + 1) * 128],
                              ucB[kc][:, i * 512:(i + 1) * 512],
                              [tk_conv[(half, kc)], tkwg]) for kc in range(DC)]
                    tkm = mm_group(nmp, psd[:, :], pairs)
                    if nm == "d":
                        tev = p.s(lambda s, m=m, i=i, psd=psd: s.activation(
                            decayF[m][:, i * 512:(i + 1) * 512], psd[:, :],
                            ACT.Sigmoid, scale=cst[:, C_NEG1:C_NEG1 + 1],
                            bias=nbd_pp[:, m:m + 1]),
                            wait=[tkm, tk_nbd, tk_te_h.get((half - 1, m))])
                    elif nm == "B":
                        sl = (m * 2 + i) % 2
                        tev = p.v(lambda v, m=m, sl=sl, psd=psd, i=i:
                                  v.scalar_tensor_tensor(
                                      spillB[sl][:, 0:512], psd[:, :],
                                      bB_pp[:, m:m + 1],
                                      ucB[m][:, i * 512:(i + 1) * 512],
                                      AL.add, AL.mult),
                                  wait=[tkm, tk_spill_inp.get(sl)])
                        tk_spill_inp[sl] = p.dma(
                            "sync", inp_spill[m * 128:(m + 1) * 128,
                                              i * 512:(i + 1) * 512],
                            spillB[sl][:, 0:512],
                            wait=[tev, tk_spill_inp.get(sl)], slot=f"si{sl}")
                        tk_inpst[(m, i)] = tk_spill_inp[sl]
                    else:
                        sl = (m * 2 + i) % 2
                        tev = p.s(lambda s, m=m, psd=psd, sl=sl: s.activation(
                            spillB[sl][:, 0:512], psd[:, :], ACT.Identity,
                            bias=bC_pp[:, m:m + 1]),
                            wait=[tkm, tk_spill_cm.get(sl)])
                        tk_spill_cm[sl] = p.dma(
                            "sync", cm_spill[m * 128:(m + 1) * 128,
                                             i * 512:(i + 1) * 512],
                            spillB[sl][:, 0:512],
                            wait=[tev, tk_spill_cm.get(sl)], slot=f"sc{sl}")
                        tk_cmst[(m, i)] = tk_spill_cm[sl]
                    psset(nmp, tev)
                    tk_sp[(nm, m, i)] = tev
            tk_wlast = tk_sp[(nm, 7, 1)]
        tk_cmld = {}
        tk_inld = {}
        for m in range(DC):
            sl = m % 2
            tk_inld[sl] = p.dma("gpsimd", inldF[sl][:, :],
                                inp_spill[m * 128:(m + 1) * 128, :],
                                wait=[tk_inpst[(m, 0)], tk_inpst[(m, 1)],
                                      tk_inld.get(sl + 2)], slot=f"ri{sl}")
            tk_cmld[sl] = p.dma("sync", cmldB[sl][:, :],
                                cm_spill[m * 128:(m + 1) * 128, :],
                                wait=[tk_cmst[(m, 0)], tk_cmst[(m, 1)],
                                      tk_cmld.get(sl + 2)], slot=f"cl{sl}")
            init_h = 0.0 if half == 0 else cL1[:, m:m + 1]
            init_e = 1.0 if half == 0 else cE1[:, m:m + 1]
            th = p.v(lambda v, m=m, sl=sl, init_h=init_h: v.tensor_tensor_scan(
                hF[:, :], decayF[m][:, :], inldF[sl][:, :], init_h,
                AL.mult, AL.add),
                wait=[tk_sp[("d", m, 0)], tk_sp[("d", m, 1)], tk_inld[sl],
                      tk_hcons])
            te = p.v(lambda v, m=m, init_e=init_e: v.tensor_tensor_scan(
                eF[:, :], decayF[m][:, :], zeroB[:, :], init_e,
                AL.mult, AL.add), wait=[tk_zero, tk_econs, th])
            p.v(lambda v, m=m: v.tensor_copy(cL1[:, m:m + 1], hF[:, TH - 1:TH]),
                wait=[th])
            p.v(lambda v, m=m: v.tensor_copy(cE1[:, m:m + 1], eF[:, TH - 1:TH]),
                wait=[te])
            tk_te_h[(half, m)] = te
            tp2 = p.v(lambda v, sl=sl: v.tensor_tensor(
                p2fB[sl][:, :], hF[:, :], cmldB[sl][:, :], AL.mult),
                wait=[th, tk_cmld[sl]])
            tk_hcons = tp2
            tec = p.v(lambda v, m=m, sl=sl: v.tensor_tensor(
                ecdB[sl][:, :], eF[:, :], cmldB[sl][:, :], AL.mult),
                wait=[te, tk_cmld[sl], tk_ecs.get(sl)])
            tk_econs = tec
            tk_ecs[sl] = p.dma("sync",
                               ec_spill[m * 128:(m + 1) * 128, toff:toff + TH],
                               ecdB[sl][:, :], wait=[tec], slot=f"ec{sl}")
            tk_cmld[sl + 2] = tec
            tk_inld[sl + 2] = tp2
            for i in range(2):
                nmp = "s" if i == 0 else "p"
                pst = ps_s[0:1, :] if i == 0 else ps_p[0:1, :]
                ww = [tp2, tk_ones] + ([psget(nmp)] if m == 0 else [])
                tky = p.t(lambda t, sl=sl, i=i, m=m, pst=pst: t.matmul(
                    pst, ones_sb[:, :], p2fB[sl][:, i * 512:(i + 1) * 512],
                    start=(m == 0), stop=(m == DC - 1)), wait=ww)
                tk_p2mm[sl] = tky
        for i in range(2):
            nmp = "s" if i == 0 else "p"
            pst = ps_s[0:1, :] if i == 0 else ps_p[0:1, :]
            ev = p.v(lambda v, i=i, toff=toff, pst=pst: v.tensor_copy(
                ROWS[0:1, toff + i * 512:toff + (i + 1) * 512], pst),
                wait=[tk_p2mm[0], tk_p2mm[1]])
            psset(nmp, ev)
            tk_y2loc[(half, i)] = ev
        AF.pop()
        AB.pop()
        AB.pop()
    p.dma("gpsimd", qp_src[:, 0:8], cL1[:, :], wait=[tk_hcons], slot="qps")
    p.dma("gpsimd", qp_src[:, 8:16], cE1[:, :], wait=[tk_econs], slot="qps")
    tk_agqp = p.coll("AllGather", [qp_src.ap()], [qp_dst.ap()],
                     wait=[p.drain("gpsimd")])
    tk_y2 = dict(tk_y2loc)
    tH = p.v(lambda v: v.memset(H_a[:, :], 0.0))
    cur, oth = H_a, H_b
    tql = None
    for j in range(R - 1):
        tql = p.dma("sync", qpt[:, :], qp_dst[j, :, :],
                    wait=[tk_agqp, tH, tql], slot="qpl")
        t0 = p.v(lambda v, j=j: v.tensor_scalar(
            qpt[:, 8:16], qpt[:, 8:16], hmq[:, j:j + 1], hmpa[:, j:j + 1],
            AL.mult, AL.add), wait=[tql, tk_bp])
        t0b = p.v(lambda v, j=j: v.tensor_scalar(
            qpt[:, 0:8], qpt[:, 0:8], hmq[:, j:j + 1], None, AL.mult),
            wait=[tql])
        t1 = p.v(lambda v, cur=cur, oth=oth: v.tensor_tensor(
            oth[:, :], cur[:, :], qpt[:, 8:16], AL.mult), wait=[tH, t0])
        tH = p.v(lambda v, oth=oth: v.tensor_tensor(
            oth[:, :], oth[:, :], qpt[:, 0:8], AL.add), wait=[t1, t0b])
        cur, oth = oth, cur
        tql = tH
    Hfin = cur
    tk_z2c = {}
    for half in range(2):
        toff = half * TH
        tk_zmm = {}
        for m in range(DC):
            sl = m % 2
            tld = p.dma("sync", ecdB[sl][:, :],
                        ec_spill[m * 128:(m + 1) * 128, toff:toff + TH],
                        wait=[tk_ecs.get(0), tk_ecs.get(1), tk_z2c.get(sl)],
                        slot=f"ec{sl}")
            tz = p.v(lambda v, sl=sl, m=m: v.tensor_scalar(
                p2fB[sl][:, :], ecdB[sl][:, :], Hfin[:, m:m + 1], None,
                AL.mult), wait=[tH, tld, tk_zmm.get(sl), tk_p2mm.get(sl)])
            tk_z2c[sl] = tz
            for i in range(2):
                nmp = "s" if i == 0 else "p"
                pst = ps_s[0:1, :] if i == 0 else ps_p[0:1, :]
                ww = [tz, tk_ones] + ([psget(nmp)] if m == 0 else [])
                tky = p.t(lambda t, sl=sl, i=i, m=m, pst=pst: t.matmul(
                    pst, ones_sb[:, :], p2fB[sl][:, i * 512:(i + 1) * 512],
                    start=(m == 0), stop=(m == DC - 1)), wait=ww)
                tk_zmm[sl] = tky
        for i in range(2):
            nmp = "s" if i == 0 else "p"
            pst = ps_s[0:1, :] if i == 0 else ps_p[0:1, :]
            ev = p.v(lambda v, i=i, toff=toff, pst=pst: v.tensor_tensor(
                ROWS[0:1, toff + i * 512:toff + (i + 1) * 512],
                ROWS[0:1, toff + i * 512:toff + (i + 1) * 512],
                pst, AL.add), wait=[tk_zmm[0], tk_zmm[1]])
            psset(nmp, ev)
            tk_y2[(half, i)] = ev
    AB.pop()
    tk_y2b = p.dma("sync", browD[5:6, :], ROWS[0:1, :],
                   wait=list(tk_y2.values()), slot="brw")
    AB.push()
    y2rB = AB.take(2048)
    gatB = [AB.take(2048) for _ in range(DC)]
    svld = [AB.take(2048), AB.take(2048)]
    tk_y2r = p.dma("gpsimd", y2rB[:, :], browD[5:6, :].partition_broadcast(128),
                   wait=[tk_y2b], slot="y2r")
    tk_svld = {}
    tk_gat = []
    for k in range(DC):
        sl = k % 2
        tk_svld[sl] = p.dma("sync", svld[sl][:, :],
                            sv_spill[k * 128:(k + 1) * 128, :],
                            wait=[tk_sv_dma.get(0), tk_sv_dma.get(1),
                                  tk_svld.get(sl + 2)], slot=f"sv{sl}")
        tg = p.v(lambda v, k=k, sl=sl: v.tensor_tensor(
            gatB[k][:, :], svld[sl][:, :], y2rB[:, :], AL.mult),
            wait=[tk_svld[sl], tk_y2r])
        tk_svld[sl + 2] = tg
        tk_gat.append(tg)
    tkwo = kload("gpsimd", wgtB[:, :], woT.ap(), 8, "wg",
                 wait=[tk_wlast, p.cur("tt")])
    for m in range(DC):
        for i in range(4):
            nmp = "0" if (m * 4 + i) % 2 == 0 else "1"
            psd = ps0 if (m * 4 + i) % 2 == 0 else ps1
            pairs = [(wgtB[:, kc * 1024 + m * 128:kc * 1024 + (m + 1) * 128],
                      gatB[kc][:, i * 512:(i + 1) * 512], [tk_gat[kc], tkwo])
                     for kc in range(DC)]
            tkm = mm_group(nmp, psd[:, :], pairs)
            tev = p.v(lambda v, m=m, i=i, psd=psd: v.scalar_tensor_tensor(
                XS[m][:, i * 512:(i + 1) * 512], psd[:, :], bo_pp[:, m:m + 1],
                XS[m][:, i * 512:(i + 1) * 512], AL.add, AL.add), wait=[tkm])
            psset(nmp, tev)
    AB.pop()
    AF.pop()
    AB.pop()
    if stage == 3:
        p.barrier()
        for k in range(DC):
            p.dma("sync", out_c[k * 128:(k + 1) * 128, :], XS[k][:, :],
                  slot="out")
        p.g(lambda g: g.nop(), wait=[p.drain("sync")])
        return nc, p

    L1SL = [("sync", "b10"), ("sync", "b11"), ("sync", "b12"),
            ("sync", "b13"), ("scalar", "b14"), ("scalar", "b15"),
            ("scalar", "b16"), ("scalar", "b17")]
    L2SL = [("sync", "b20"), ("sync", "b21"), ("sync", "b22"),
            ("sync", "b23"), ("scalar", "b24"), ("scalar", "b25"),
            ("scalar", "b26"), ("scalar", "b27")]
    ONM = ["0", "1", "s", "p"]
    OPS = [ps0, ps1, ps_s, ps_p]
    LOADS1 = [(it, mb, kc) for it in range(4) for mb in range(8)
              for kc in range(DC)]
    LOADS2 = [(it, m2b, kc2) for it in range(4) for m2b in range(2)
              for kc2 in range(32)]
    tk_l1seq = {}
    tk_l2seq = {}
    tk_b1cons = {}
    tk_b2cons = {}

    def emit_load1(idx):
        if idx >= len(LOADS1):
            return
        it_, mb_, kc_ = LOADS1[idx]
        til = idx % 8
        eng, slot = L1SL[til]
        tk_l1seq[idx] = p.dma(
            eng, bw1B[til][:, :],
            bw1_dram.ap()[kc_ * 128:(kc_ + 1) * 128,
                          mb_ * 512:(mb_ + 1) * 512],
            wait=[tk_bw_st, tk_b1cons.get(til)], slot=slot)

    def emit_load2(idx):
        if idx >= len(LOADS2):
            return
        it_, m2b_, kc2_ = LOADS2[idx]
        til = idx % 8
        eng, slot = L2SL[til]
        tk_l2seq[idx] = p.dma(
            eng, bw2B[til][:, :],
            bw2_dram.ap()[kc2_ * 128:(kc2_ + 1) * 128,
                          m2b_ * 512:(m2b_ + 1) * 512],
            wait=[tk_bw_st, tk_b2cons.get(til)], slot=slot)

    p.barrier()
    AF.push()
    AB.push()
    bw1B = [AB.take(512) for _ in range(8)]
    bw2B = [AB.take(512) for _ in range(8)]
    for idx in range(8):
        emit_load1(idx)
        emit_load2(idx)
    evs4 = ln_stats()
    xn4B = [AB.take(512) for _ in range(DC)]
    h1B = [AB.take(512) for _ in range(32)]
    tk_bw4 = p.dma("sync", browD[0:2, :], ROWS[0:2, :], wait=evs4, slot="brw")
    murE = AF.take(2048)
    rrE = AF.take(2048)
    scrE = AF.take(2048)
    tmpE = AF.take(512)
    x4bF = AF.take(512)
    tk_mu = bcast(murE[:, :], browD[0:1, :], "mu", wait=[tk_bw4])
    tk_m2 = bcast(rrE[:, :], browD[1:2, :], "rr", wait=[tk_bw4])
    p.pend["vector"] += [tk_mu, tk_m2]
    p.pend["scalar"] += [tk_mu, tk_m2]
    tk_rr = finalize_reps(TOK, murE[:, :], rrE[:, :], scrE[:, :], rrE[:, :])
    tk_h1 = {}
    for it in range(4):
        isl = slice(it * 512, (it + 1) * 512)
        curT = p.cur("tt")
        tk_xn4 = {}
        for k in range(DC):
            t1 = p.v(lambda v, k=k, isl=isl: v.tensor_tensor(
                tmpE[:, :], XS[k][:, isl], murE[:, isl], AL.subtract),
                wait=[tk_mu, curT, tk_xn4.get(k - 1)])
            t2 = p.v(lambda v, k=k, isl=isl: v.tensor_tensor(
                xn4B[k][:, :], tmpE[:, :], rrE[:, isl], AL.mult),
                wait=[t1, tk_rr])
            tk_xn4[k] = t2
        for mb in range(8):
            tk_mmj = {}
            for kc in range(DC):
                idx = (it * 8 + mb) * DC + kc
                til = idx % 8
                for j in range(4):
                    extra = ([psget(f"y{j}"), psget("y")] if kc == 0 else [])
                    tk_mmj[j] = p.t(lambda t, til=til, j=j, kc=kc: t.matmul(
                        ps_y[:, j * 512:(j + 1) * 512],
                        bw1B[til][:, j * 128:(j + 1) * 128],
                        xn4B[kc][:, :], start=(kc == 0), stop=(kc == DC - 1)),
                        wait=[tk_l1seq[idx], tk_xn4[kc]] + extra)
                tk_b1cons[til] = tk_mmj[3]
                emit_load1(idx + 8)
            for j in range(4):
                m = mb * 4 + j
                tev = p.s(lambda s, m=m, j=j: s.activation(
                    h1B[m][:, :], ps_y[:, j * 512:(j + 1) * 512], ACT.Gelu,
                    bias=b1p[:, m:m + 1], scale=a1_pp[:, m:m + 1]),
                    wait=[tk_mmj[j], tk_a1, curT])
                psset(f"y{j}", tev)
                tk_h1[(it, m)] = tev
        for m2b in range(2):
            tk_m2j = {}
            for kc2 in range(32):
                idx = (it * 2 + m2b) * 32 + kc2
                til = idx % 8
                for j in range(4):
                    extra = ([psget(ONM[j])] if kc2 == 0 else [])
                    tk_m2j[j] = p.t(lambda t, til=til, j=j, kc2=kc2: t.matmul(
                        OPS[j][:, :], bw2B[til][:, j * 128:(j + 1) * 128],
                        h1B[kc2][:, :], start=(kc2 == 0), stop=(kc2 == 31)),
                        wait=[tk_l2seq[idx], tk_h1[(it, kc2)]] + extra)
                tk_b2cons[til] = tk_m2j[3]
                emit_load2(idx + 8)
            for j in range(4):
                m2 = m2b * 4 + j
                tb = p.v(lambda v, m2=m2, isl=isl: v.tensor_scalar(
                    x4bF[:, :], XS[m2][:, isl], b2p[:, m2:m2 + 1], None,
                    AL.add))
                tev = p.v(lambda v, m2=m2, isl=isl, j=j:
                          v.scalar_tensor_tensor(
                    XS[m2][:, isl], OPS[j][:, :], a2_pp[:, m2:m2 + 1],
                    x4bF[:, :], AL.mult, AL.add),
                    wait=[tk_m2j[j], tb, tk_a2])
                psset(ONM[j], tev)
                p.dma("sync", out_c[m2 * 128:(m2 + 1) * 128, isl],
                      XS[m2][:, isl], wait=[tev], slot="out")
    p.g(lambda g: g.nop(), wait=[p.drain("sync")])
    AF.pop()
    AB.pop()
    return nc, p


_NC_CACHE = {}


def _get_program(stage):
    if stage not in _NC_CACHE:
        nc, p = build_program(stage)
        with nc.Block() as block:
            p.build(block)
        _NC_CACHE[stage] = nc
    return _NC_CACHE[stage]


def make_in_maps(inputs):
    z = inputs
    x = np.ascontiguousarray(z["L0_attn_x"], dtype=np.float32)
    Xf = x.reshape(T, D)
    Wstack = np.concatenate([z["L0_attn_W_dt1"], z["L0_attn_W_B"],
                             z["L0_attn_W_C"]], axis=0)
    pool_pat = np.zeros((128, 64), dtype=BF)
    for pp in range(128):
        pool_pat[pp, pp % 64] = 1.0
    lnl = np.zeros((128, 4), dtype=BF)
    lnl[:, 0] = 1.0 / D
    lnl[:, 3] = 1.0 / D
    consts = np.zeros((1, NCONST), dtype=np.float32)
    consts[0, C_EPS] = 1e-5
    consts[0, C_NEG1] = -1.0
    consts[0, C_NEGH] = -0.5
    consts[0, C_ONE] = 1.0

    def pp8(b):
        return np.ascontiguousarray(
            np.asarray(b, dtype=np.float32).reshape(8, 128).T)

    shared = {
        "wstackT": np.ascontiguousarray(Wstack.T, dtype=np.float32),
        "l0w_T": np.ascontiguousarray(z["L0_ffn_weight"].T, dtype=np.float32),
        "dishaI": np.ascontiguousarray(z["L0_ffn_disha"], dtype=np.float32),
        "pool_pat": pool_pat,
        "w_inT": np.ascontiguousarray(z["L1_attn_w_in"].T, dtype=np.float32),
        "wdT": np.ascontiguousarray(z["L1_attn_w_delta"].T, dtype=np.float32),
        "wBT": np.ascontiguousarray(z["L1_attn_w_B"].T, dtype=np.float32),
        "wCT": np.ascontiguousarray(z["L1_attn_w_C"].T, dtype=np.float32),
        "woT": np.ascontiguousarray(z["L1_attn_w_out"].T, dtype=np.float32),
        "b_in_ppI": np.ascontiguousarray(
            np.asarray(z["L1_attn_b_in"], dtype=np.float32).reshape(16, 128).T),
        "b_d_ppI": pp8(z["L1_attn_b_delta"]),
        "b_B_ppI": pp8(z["L1_attn_b_B"]),
        "b_C_ppI": pp8(z["L1_attn_b_C"]),
        "b_o_ppI": pp8(z["L1_attn_b_out"]),
        "b_cv_ppI": pp8(z["L1_attn_b_conv"]),
        "wcv_ppI": np.ascontiguousarray(
            np.asarray(z["L1_attn_w_conv"], dtype=np.float32)[:, 0, :]
            .reshape(8, 128, 4).transpose(1, 0, 2).reshape(128, 32)),
        "g_ppI": pp8(z["L1_attn_ln_g"]),
        "be_ppI": pp8(z["L1_attn_ln_b"]),
        "w1T": np.ascontiguousarray(z["L1_ffn_w1"].T, dtype=np.float32),
        "w2T": np.ascontiguousarray(z["L1_ffn_w2"].T, dtype=np.float32),
        "w1N": np.ascontiguousarray(z["L1_ffn_w1"], dtype=np.float32),
        "w2N": np.ascontiguousarray(z["L1_ffn_w2"], dtype=np.float32),
        "b1_ppI": np.ascontiguousarray(
            np.asarray(z["L1_ffn_b1"], dtype=np.float32).reshape(32, 128).T),
        "b2_ppI": pp8(z["L1_ffn_b2"]),
        "lnlI": lnl,
        "ones_bfI": np.ones((128, 1), dtype=BF),
        "id_bfI": np.eye(128, dtype=BF),
        "constr": consts,
    }
    in_maps = []
    for c in range(R):
        m = dict(shared)
        m["x_tok"] = np.ascontiguousarray(
            Xf[c * TOK:(c + 1) * TOK, :].T, dtype=np.float32)
        m["x_dsh"] = np.ascontiguousarray(
            Xf[:, c * 128:(c + 1) * 128].T, dtype=np.float32)
        m["wdt2T"] = np.ascontiguousarray(
            z["L0_attn_W_dt2"][c * 128:(c + 1) * 128, :].T, dtype=np.float32)
        m["b_dt2c"] = np.ascontiguousarray(
            z["L0_attn_b_dt2"][c * 128:(c + 1) * 128].reshape(128, 1),
            dtype=np.float32)
        m["logAc"] = np.ascontiguousarray(
            z["L0_attn_log_A"][c * 128:(c + 1) * 128, :], dtype=np.float32)
        m["dskipc"] = np.ascontiguousarray(
            z["L0_attn_D_skip"][c * 128:(c + 1) * 128].reshape(128, 1),
            dtype=np.float32)
        seq_first = (c % 4 == 0)
        hm = np.zeros((128, 8), dtype=np.float32)
        if not seq_first:
            hm[:, c - 1] = 1.0
        m["halomskI"] = hm
        m["halozI"] = np.full((128, 1), 0.0 if seq_first else 1.0,
                              dtype=np.float32)
        q = np.zeros((128, 8), dtype=np.float32)
        lo = (c // 4) * 4
        q[:, lo:c] = 1.0
        m["hmqI"] = q
        m["hmpaI"] = np.ascontiguousarray(1.0 - q)
        in_maps.append(m)
    return in_maps


def kernel(**inputs):
    in_maps = make_in_maps(inputs)
    nc = _get_program(STAGE)
    res = run_bass_kernel_spmd(nc, in_maps, core_ids=list(range(R)))
    out = np.empty((T, D), dtype=np.float32)
    for c in range(R):
        out[c * TOK:(c + 1) * TOK, :] = np.asarray(res.results[c]["out_c"]).T
    return out.reshape(2, 8192, D)


# version 09
# speedup vs baseline: 1.3993x; 1.0030x over previous
import numpy as np
import ml_dtypes
import concourse.bass as bass
import concourse.mybir as mybir
from concourse.bass_utils import run_bass_kernel_spmd
from concourse.aot_env import allow_ici_all_to_all

F32 = mybir.dt.float32
BF16 = mybir.dt.bfloat16
AL = mybir.AluOpType
ACT = mybir.ActivationFunctionType
AXL = mybir.AxisListType
BF = ml_dtypes.bfloat16

R = 8
D = 1024
N16 = 16
TOK = 2048
T = R * TOK
DC = 8
TH = 1024
STAGE = 4

C_EPS, C_NEG1, C_NEGH, C_ONE = 0, 1, 2, 3
NCONST = 8
ENGS = ("sync", "vector", "scalar", "tensor", "gpsimd")


class Prog:

    def __init__(self, nc):
        self.nc = nc
        self.q = {e: [] for e in ENGS}
        self.sems = {}
        self.cnt = {}
        for s in ("vv", "ss", "tt", "gg", "cc", "hs_sync", "hs_gpsimd"):
            self.sems[s] = nc.alloc_semaphore(s)
            self.cnt[s] = 0
        self.pend = {e: [] for e in ENGS}
        self.psfree = {}
        self.slot_eng = {}

    def cur(self, sem):
        return (sem, self.cnt[sem]) if self.cnt[sem] > 0 else None

    def _emit(self, eng, fn, waits, inc):
        waits = [w for w in waits if w is not None] + self.pend[eng]
        self.pend[eng] = []
        for (s, v) in waits:
            assert v <= self.cnt[s], f"future token {s}={v} now {self.cnt[s]}"
        tok = None
        if inc is not None:
            s, amt = inc
            self.cnt[s] += amt
            tok = (s, self.cnt[s])
        self.q[eng].append((tuple(waits), fn, inc))
        return tok

    def v(self, fn, wait=()):
        return self._emit("vector", fn, wait, ("vv", 1))

    def s(self, fn, wait=()):
        return self._emit("scalar", fn, wait, ("ss", 1))

    def t(self, fn, wait=()):
        return self._emit("tensor", fn, wait, ("tt", 1))

    def g(self, fn, wait=()):
        return self._emit("gpsimd", fn, wait, ("gg", 1))

    def dma(self, eng, out, in_, wait=(), slot=None):
        assert slot is not None, "dma needs a slot"
        name = "D_" + slot
        if name not in self.sems:
            self.sems[name] = self.nc.alloc_semaphore(name)
            self.cnt[name] = 0
            self.slot_eng[name] = eng
        assert self.slot_eng[name] == eng, f"slot {slot} used from two engines"
        return self._emit(eng, lambda e, o=out, i=in_: e.dma_start(out=o, in_=i),
                          wait, (name, 16))

    def drain(self, eng):
        waits = [(n, self.cnt[n]) for n in self.cnt
                 if n.startswith("D_") and self.slot_eng.get(n) == eng
                 and self.cnt[n] > 0]
        hs = "hs_" + eng
        if hs not in self.sems:
            self.sems[hs] = self.nc.alloc_semaphore(hs)
            self.cnt[hs] = 0
        return self._emit(eng, lambda e: e.nop(), waits, (hs, 1))

    def barrier(self):
        toks = []
        for eng in ENGS:
            if any(n.startswith("D_") and self.slot_eng.get(n) == eng
                   and self.cnt[n] > 0 for n in self.cnt):
                toks.append(self.drain(eng))
        toks += [(s, self.cnt[s]) for s in ("vv", "ss", "tt", "gg", "cc")
                 if self.cnt[s] > 0]
        for e in ENGS:
            self.pend[e] = list(toks)

    def coll(self, kind, ins, outs, wait=()):
        def fn(g, kind=kind, ins=ins, outs=outs):
            with allow_ici_all_to_all():
                return g.collective_compute(
                    kind, AL.bypass, replica_groups=[list(range(R))],
                    ins=[a.opt() for a in ins], outs=[a.opt() for a in outs])
        return self._emit("gpsimd", fn, wait, ("cc", 1))

    def build(self, block):
        hw = {e: {} for e in ENGS}

        def runner(eng):
            def run(engobj):
                h = hw[eng]
                for waits, fn, inc in self.q[eng]:
                    for (s, v) in waits:
                        if h.get(s, 0) < v:
                            engobj.wait_ge(self.sems[s], v)
                            h[s] = v
                    inst = fn(engobj)
                    if inc is not None:
                        s, amt = inc
                        inst.then_inc(self.sems[s], amt)
            return run
        block.sync(runner("sync"))
        block.vector(runner("vector"))
        block.scalar(runner("scalar"))
        block.tensor(runner("tensor"))
        block.gpsimd(runner("gpsimd"))


class Arena:
    def __init__(self, nc, name, cols, dtype):
        self.t = nc.alloc_sbuf_tensor(name, [128, cols], dtype)
        self.cols, self.off, self.marks = cols, 0, []

    def take(self, cols):
        assert self.off + cols <= self.cols, \
            f"{self.t.name} overflow {self.off}+{cols}>{self.cols}"
        ap = self.t[:, self.off:self.off + cols]
        self.off += cols
        return ap

    def push(self):
        self.marks.append(self.off)

    def pop(self):
        self.off = self.marks.pop()


def build_program(stage=STAGE):
    nc = bass.Bass("TRN2", target_bir_lowering=False,
                   detect_race_conditions=False)

    def din(name, shape, dtype=F32):
        return nc.dram_tensor(name, list(shape), dtype, kind="ExternalInput")

    x_tok = din("x_tok", [D, TOK])
    x_dsh = din("x_dsh", [128, T])
    wstackT = din("wstackT", [D, 96])
    wdt2T = din("wdt2T", [64, 128])
    b_dt2c = din("b_dt2c", [128, 1])
    logAc = din("logAc", [128, N16])
    dskipc = din("dskipc", [128, 1])
    l0w_T = din("l0w_T", [D, D])
    dishaI = din("dishaI", [64, D])
    pool_pat = din("pool_pat", [128, 64], BF16)
    w_inT = din("w_inT", [D, 2 * D])
    wdT = din("wdT", [D, D])
    wBT = din("wBT", [D, D])
    wCT = din("wCT", [D, D])
    woT = din("woT", [D, D])
    b_in_ppI = din("b_in_ppI", [128, 16])
    b_d_ppI = din("b_d_ppI", [128, 8])
    b_B_ppI = din("b_B_ppI", [128, 8])
    b_C_ppI = din("b_C_ppI", [128, 8])
    b_o_ppI = din("b_o_ppI", [128, 8])
    b_cv_ppI = din("b_cv_ppI", [128, 8])
    wcv_ppI = din("wcv_ppI", [128, 32])
    g_ppI = din("g_ppI", [128, 8])
    be_ppI = din("be_ppI", [128, 8])
    w1T = din("w1T", [D, 4 * D])
    w2T = din("w2T", [4 * D, D])
    w1N = din("w1N", [4 * D, D])
    w2N = din("w2N", [D, 4 * D])
    b1_ppI = din("b1_ppI", [128, 32])
    b2_ppI = din("b2_ppI", [128, 8])
    lnlI = din("lnlI", [128, 4], BF16)
    ones_bfI = din("ones_bfI", [128, 1], BF16)
    id_bfI = din("id_bfI", [128, 128], BF16)
    constr = din("constr", [1, NCONST])
    halomskI = din("halomskI", [128, 8])
    halozI = din("halozI", [128, 1])
    hmqI = din("hmqI", [128, 8])
    hmpaI = din("hmpaI", [128, 8])

    out_c = nc.dram_tensor("out_c", [D, TOK], F32, kind="ExternalOutput")

    ag1_src = nc.dram_tensor("ag1_src", [98, TOK], F32)
    ag1_dst = nc.dram_tensor("ag1_dst", [R, 98, TOK], F32)
    ya_src = nc.dram_tensor("ya_src", [R, 128, TOK], F32)
    ya_dst = nc.dram_tensor("ya_dst", [R, 128, TOK], F32)
    x3t_src = nc.dram_tensor("x3t_src", [128, 24], F32)
    x3t_dst = nc.dram_tensor("x3t_dst", [R, 128, 24], F32)
    qp_src = nc.dram_tensor("qp_src", [128, 16], F32)
    qp_dst = nc.dram_tensor("qp_dst", [R, 128, 16], F32)
    browD = nc.dram_tensor("browD", [8, TOK], F32)
    sv_spill = nc.dram_tensor("sv_spill", [D, TOK], BF16)
    inp_spill = nc.dram_tensor("inp_spill", [D, TH], BF16)
    cm_spill = nc.dram_tensor("cm_spill", [D, TH], BF16)
    ec_spill = nc.dram_tensor("ec_spill", [D, TOK], BF16)
    bw1_dram = nc.dram_tensor("bw1_dram", [D, 4 * D], BF16)
    bw2_dram = nc.dram_tensor("bw2_dram", [4 * D, D], BF16)

    XS = [nc.alloc_sbuf_tensor(f"xs{k}", [128, TOK], F32) for k in range(DC)]
    AF = Arena(nc, "AF", 12800, F32)
    AB = Arena(nc, "AB", 34560, BF16)
    SM = Arena(nc, "SM", 2000, F32)
    ROWS = nc.alloc_sbuf_tensor("ROWS", [8, TOK], F32)
    id_sb = nc.alloc_sbuf_tensor("id_sb", [128, 128], BF16)
    lnl_sb = nc.alloc_sbuf_tensor("lnl_sb", [128, 4], BF16)
    ones_sb = nc.alloc_sbuf_tensor("ones_sb", [128, 1], BF16)

    ps_y = nc.alloc_psum_tensor("ps_y", [128, 2048], F32)
    psA = nc.alloc_psum_tensor("psA", [128, 1024], F32)
    psB = nc.alloc_psum_tensor("psB", [128, 1024], F32)
    ps0 = psA[:, 0:512]
    ps1 = psA[:, 512:1024]
    ps_s = psB[:, 0:512]
    ps_p = psB[:, 512:1024]

    p = Prog(nc)

    def psget(name):
        return p.psfree.get(name)

    def psset(name, tok):
        p.psfree[name] = tok

    cst = SM.take(NCONST)
    A_sb = SM.take(N16)
    iA_sb = SM.take(N16)
    dsk = SM.take(1)
    bdt2 = SM.take(1)
    carry0 = SM.take(N16)
    bin_pp = SM.take(16)
    binh_pp = SM.take(8)
    bd_pp = SM.take(8)
    nbd_pp = SM.take(8)
    bB_pp = SM.take(8)
    bC_pp = SM.take(8)
    bo_pp = SM.take(8)
    bcv_pp = SM.take(8)
    wcv = SM.take(32)
    g2_pp = SM.take(8)
    be2_pp = SM.take(8)
    b1p = SM.take(32)
    b2p = SM.take(8)
    a1_pp = SM.take(32)
    a2_pp = SM.take(8)
    ascr = SM.take(1)
    cL1 = SM.take(8)
    cE1 = SM.take(8)
    H_a = SM.take(8)
    H_b = SM.take(8)
    qpt = SM.take(16)
    s96n = SM.take(1)
    hmsk = SM.take(8)
    hz = SM.take(1)
    hmq = SM.take(8)
    hmpa = SM.take(8)

    p.dma("sync", cst[:, :], constr[0:1, :].partition_broadcast(128), slot="ini")
    p.dma("sync", A_sb[:, :], logAc[:, :], slot="ini")
    for (dst, src) in ((dsk, dskipc), (bdt2, b_dt2c), (bin_pp, b_in_ppI),
                       (bd_pp, b_d_ppI), (bB_pp, b_B_ppI), (bC_pp, b_C_ppI),
                       (bo_pp, b_o_ppI), (bcv_pp, b_cv_ppI), (wcv, wcv_ppI),
                       (g2_pp, g_ppI), (be2_pp, be_ppI), (b1p, b1_ppI),
                       (b2p, b2_ppI), (hmsk, halomskI), (hz, halozI),
                       (hmq, hmqI), (hmpa, hmpaI)):
        p.dma("sync", dst[:, :], src[:, :], slot="ini")
    p.dma("sync", id_sb[:, :], id_bfI[:, :], slot="ini")
    p.dma("sync", lnl_sb[:, :], lnlI[:, :], slot="ini")
    p.dma("sync", ones_sb[:, :], ones_bfI[:, :], slot="ini")
    tk_ini = p.drain("sync")
    tk_c = tk_la = tk_bp = tk_id = tk_lnl = tk_ones = tk_ini
    tk_A = p.s(lambda s: s.activation(A_sb[:, :], A_sb[:, :], ACT.Exp),
               wait=[tk_la, tk_c])
    tk_A = p.v(lambda v: v.tensor_scalar(A_sb[:, :], A_sb[:, :], -1.0, None,
                                         AL.mult), wait=[tk_A])
    tk_iA = p.v(lambda v: v.tensor_scalar(iA_sb[:, :], A_sb[:, :], 1e-8, None,
                                          AL.add), wait=[tk_A])
    tk_iA = p.v(lambda v: v.reciprocal(iA_sb[:, :], iA_sb[:, :]), wait=[tk_iA])
    tk_nbd = p.v(lambda v: v.tensor_scalar(nbd_pp[:, :], bd_pp[:, :], -1.0, None,
                                           AL.mult), wait=[tk_bp])
    tk_binh = p.v(lambda v: v.tensor_scalar(binh_pp[:, :], bin_pp[:, 0:8],
                                            hz[:, 0:1], None, AL.mult),
                  wait=[tk_bp])

    def mm_group(psname, psum, pairs, extra_free=None, start0=True,
                 stop_last=True):
        tok = None
        n = len(pairs)
        free = psget(psname)
        for i, (lh, rh, w) in enumerate(pairs):
            ww = list(w)
            if i == 0:
                ww += [free, extra_free]
            tok = p.t(lambda t, lh=lh, rh=rh, i=i, psum=psum, n=n: t.matmul(
                psum, lh, rh, start=(start0 and i == 0),
                stop=(stop_last and i == n - 1)), wait=ww)
        return tok

    def bcast(dst, dram_row_ap, slot, wait=()):
        return p.dma("gpsimd", dst, dram_row_ap.partition_broadcast(128),
                     wait=wait, slot=slot)

    def kload(eng, out2d, dram_ap, nchunk, slot, wait=()):
        in3 = dram_ap.rearrange("(c p) m -> p c m", p=128)
        out3 = out2d.rearrange("p (c m) -> p c m", c=nchunk)
        return p.dma(eng, out3, in3, wait=wait, slot=slot)

    def ln_stats(width=TOK):
        AB.push()
        xb = [AB.take(512) for _ in range(DC)]
        sq = [AB.take(512) for _ in range(DC)]
        evs = []
        xs_ready = p.cur("vv")
        prevg = {}
        for i in range(width // 512):
            tcp = []
            for k in range(DC):
                t1 = p.v(lambda v, k=k, i=i, xb=xb: v.tensor_copy(
                    xb[k][:, :], XS[k][:, i * 512:(i + 1) * 512]),
                    wait=[prevg.get(k)])
                t2 = p.s(lambda s, k=k, i=i, sq=sq: s.activation(
                    sq[k][:, :], XS[k][:, i * 512:(i + 1) * 512], ACT.Square),
                    wait=[xs_ready, prevg.get(k)])
                tcp.append((t1, t2))
            pairs = [(lnl_sb[:, 0:2], xb[k][:, :], [tcp[k][0], tk_lnl])
                     for k in range(DC)]
            pairs += [(lnl_sb[:, 2:4], sq[k][:, :], [tcp[k][1]])
                      for k in range(DC)]
            tok = mm_group("s", ps_s[0:2, :], pairs)
            ev = p.v(lambda v, i=i: v.tensor_copy(
                ROWS[0:2, i * 512:(i + 1) * 512], ps_s[0:2, :]), wait=[tok])
            psset("s", ev)
            evs.append(ev)
            for k in range(DC):
                prevg[k] = tok
        for e in ("vector", "scalar", "gpsimd", "sync"):
            p.pend[e] = p.pend[e] + [evs[-1]]
        AB.pop()
        return evs

    def finalize_reps(width, muR, m2R, scrF, rR, rcR=None):
        t1 = p.s(lambda s: s.activation(scrF, muR, ACT.Square))
        t2 = p.v(lambda v: v.tensor_tensor(m2R, m2R, scrF, AL.subtract),
                 wait=[t1])
        t3 = p.s(lambda s: s.activation(scrF, m2R, ACT.Ln,
                                        bias=cst[:, C_EPS:C_EPS + 1]), wait=[t2])
        t4 = p.s(lambda s: s.activation(rR, scrF, ACT.Exp,
                                        scale=cst[:, C_NEGH:C_NEGH + 1]),
                 wait=[t3])
        if rcR is None:
            return t4
        t5 = p.s(lambda s: s.activation(scrF, rR, ACT.Square), wait=[t4])
        t6 = p.v(lambda v: v.tensor_tensor(scrF, scrF, m2R, AL.mult), wait=[t5])
        t7 = p.s(lambda s: s.activation(scrF, scrF, ACT.Ln,
                                        bias=cst[:, C_EPS:C_EPS + 1]), wait=[t6])
        t8 = p.s(lambda s: s.activation(scrF, scrF, ACT.Exp,
                                        scale=cst[:, C_NEGH:C_NEGH + 1]),
                 wait=[t7])
        t9 = p.v(lambda v: v.tensor_tensor(rcR, rR, scrF, AL.mult), wait=[t8])
        return t9

    AB.push()
    wst = AB.take(8 * 96)
    tk_wst = kload("gpsimd", wst[:, :], wstackT.ap(), 8, "wst")
    for k in range(DC):
        p.dma("sync", XS[k][:, :], x_tok[k * 128:(k + 1) * 128, :],
              slot="xt")
    p.barrier()
    evs1 = ln_stats()
    tk_st1 = p.dma("sync", browD[0:2, :], ROWS[0:2, :], wait=evs1, slot="brw")
    s96pairs = [(wst[:, kc * 96:(kc + 1) * 96], ones_sb[:, :], [tk_wst, tk_ones])
                for kc in range(DC)]
    tk_s96 = mm_group("1", ps1[0:96, 0:1], s96pairs)
    tk_s96 = p.v(lambda v: v.tensor_scalar(s96n[0:96, :], ps1[0:96, 0:1], -1.0,
                                           None, AL.mult), wait=[tk_s96])
    psset("1", tk_s96)
    AF.push()
    mur96 = AF.take(TOK)
    rr96 = AF.take(TOK)
    scr96 = AF.take(TOK)
    tk_mu96 = bcast(mur96[:, :], browD[0:1, :], "mu", wait=[tk_st1])
    tk_m296 = bcast(rr96[:, :], browD[1:2, :], "rr", wait=[tk_st1])
    p.pend["vector"] += [tk_mu96, tk_m296]
    p.pend["scalar"] += [tk_mu96, tk_m296]
    tk_fin1 = finalize_reps(TOK, mur96[:, :], rr96[:, :], scr96[:, :],
                            rr96[:, :])
    tk_rr96 = tk_fin1
    xbz = [AB.take(TOK) for _ in range(DC)]
    tkxb = [p.v(lambda v, k=k: v.tensor_copy(xbz[k][:, :], XS[k][:, :]))
            for k in range(DC)]
    z96 = AF.take(TOK)
    tk_ag_in = []
    for i in range(TOK // 512):
        pairs = [(wst[:, kc * 96:(kc + 1) * 96],
                  xbz[kc][:, i * 512:(i + 1) * 512], [tkxb[kc], tk_wst])
                 for kc in range(DC)]
        tokm = mm_group("0", ps0[0:96, :], pairs)
        tf = p.v(lambda v, i=i: v.scalar_tensor_tensor(
            z96[0:96, i * 512:(i + 1) * 512], mur96[0:96, i * 512:(i + 1) * 512],
            s96n[0:96, 0:1], ps0[0:96, :], AL.mult, AL.add),
            wait=[tokm, tk_mu96, tk_fin1, tk_s96])
        psset("0", tf)
        evp = p.v(lambda v, i=i: v.tensor_tensor(
            z96[0:96, i * 512:(i + 1) * 512], z96[0:96, i * 512:(i + 1) * 512],
            rr96[0:96, i * 512:(i + 1) * 512], AL.mult), wait=[tf, tk_rr96])
        tk_ag_in.append(evp)
    p.dma("gpsimd", ag1_src[0:96, :], z96[0:96, :], wait=tk_ag_in, slot="agf")
    p.dma("gpsimd", ag1_src[96:97, :], mur96[0:1, :],
          wait=[tk_fin1] + tk_ag_in, slot="agf")
    p.dma("gpsimd", ag1_src[97:98, :], rr96[0:1, :],
          wait=[tk_fin1] + tk_ag_in, slot="agf")
    tk_ag1 = p.coll("AllGather", [ag1_src.ap()], [ag1_dst.ap()],
                    wait=[p.drain("gpsimd")])
    AF.pop()
    AB.pop()

    p.barrier()
    AF.push()
    AB.push()
    wdt2_sb = AB.take(128)
    tk_wdt2 = p.dma("gpsimd", wdt2_sb[0:64, :], wdt2T[:, :], slot="wd2")
    xd = AF.take(2048)
    dtt = AF.take(2048)
    xnf = AF.take(2048)
    aT = [AF.take(2048), AF.take(2048)]
    bscan = AF.take(2048)
    murB = AB.take(2048)
    rrB = AB.take(2048)
    zdtB = AB.take(2048)
    xnB = AB.take(2048)
    browB = [AB.take(2048), AB.take(2048)]
    crowB = [AB.take(2048), AB.take(2048)]
    brepB = [AB.take(2048), AB.take(2048)]
    crepB = [AB.take(2048), AB.take(2048)]
    onesr = AB.take(128)
    hB = AB.take(2048)
    pB = [AB.take(2048), AB.take(2048)]
    wtmpP = AB.take(1024)
    sgnP = AB.take(1024)
    tk_onesr = p.v(lambda v: v.memset(onesr[:, :], 1.0))

    P0B = []
    for kc in range(DC):
        for q in range(4):
            P0B.append(("s", w1T[kc * 128:(kc + 1) * 128,
                                 q * 1024:(q + 1) * 1024],
                        bw1_dram[kc * 128:(kc + 1) * 128,
                                 q * 1024:(q + 1) * 1024]))
            P0B.append(("a1", w1N[(kc * 4 + q) * 128:(kc * 4 + q + 1) * 128,
                                  :], kc * 4 + q))
    for kc2 in range(32):
        P0B.append(("s", w2T[kc2 * 128:(kc2 + 1) * 128, :],
                    bw2_dram[kc2 * 128:(kc2 + 1) * 128, :]))
        kco, q = kc2 // 4, kc2 % 4
        P0B.append(("a2", w2N[kco * 128:(kco + 1) * 128,
                              q * 1024:(q + 1) * 1024], kco))
    tk_az2 = p.v(lambda v: v.memset(a2_pp[:, :], 0.0))
    pz_cons = {}
    pz_load = {}

    def p0b_load(k):
        src = P0B[k][1]
        pz_load[k] = p.dma("gpsimd", wtmpP[:, :], src,
                           wait=[pz_cons.get(k - 1)], slot="wbl")

    def p0b_consume(k):
        kind = P0B[k][0]
        if kind == "s":
            dst = P0B[k][2]
            tsg = p.s(lambda s: s.activation(sgnP[:, :], wtmpP[:, :],
                                             ACT.Sign),
                      wait=[pz_load[k], pz_state["store"]])
            pz_state["store"] = p.dma("gpsimd", dst, sgnP[:, :],
                                      wait=[tsg, pz_state["store"]],
                                      slot="wst1")
            pz_cons[k] = tsg
        elif kind == "a1":
            col = P0B[k][2]
            trd = p.v(lambda v, col=col: v.tensor_reduce(
                a1_pp[:, col:col + 1], wtmpP[:, :], AXL.X, AL.add,
                apply_absolute_value=True), wait=[pz_load[k]])
            pz_cons[k] = trd
            pz_state["a1last"] = trd
        else:
            col = P0B[k][2]
            trd = p.v(lambda v: v.tensor_reduce(
                ascr[:, 0:1], wtmpP[:, :], AXL.X, AL.add,
                apply_absolute_value=True),
                wait=[pz_load[k], pz_state["a2chain"]])
            tad = p.v(lambda v, col=col: v.tensor_tensor(
                a2_pp[:, col:col + 1], a2_pp[:, col:col + 1], ascr[:, 0:1],
                AL.add), wait=[trd, pz_state["a2chain"]])
            pz_state["a2chain"] = tad
            pz_cons[k] = trd
            pz_state["a2last"] = tad
    pz_state = {"store": None, "a2chain": tk_az2, "a1last": None,
                "a2last": None}

    tk_yevac = tk_y_dma = None
    tk_dt = None
    tk_xncons = None
    tk_pcons = {}
    tk_pcons2 = {}
    tk_wcons = {}
    tk_evb = {}
    tk_evc = {}
    tk_acons = {}
    tk_brep = {}
    tk_crep = {}
    y_dma_toks = []
    for tc in range(R):
        tk_xd = p.dma("sync", xd[:, :], x_dsh[:, tc * TOK:(tc + 1) * TOK],
                      wait=[tk_y_dma], slot="xd")
        tk_mu = bcast(murB[:, :], ag1_dst[tc, 96:97, :], "mu",
                      wait=[tk_ag1, tk_xncons])
        tk_rr = bcast(rrB[:, :], ag1_dst[tc, 97:98, :], "rr",
                      wait=[tk_ag1, tk_xncons])
        tk_zdt = p.dma("gpsimd", zdtB[0:64, :], ag1_dst[tc, 0:64, :],
                       wait=[tk_ag1, p.cur("tt")], slot="zd")
        t1 = p.v(lambda v: v.tensor_tensor(xnf[:, :], xd[:, :], murB[:, :],
                                           AL.subtract),
                 wait=[tk_xd, tk_mu, tk_yevac])
        tk_xn = p.v(lambda v: v.tensor_tensor(xnf[:, :], xnf[:, :], rrB[:, :],
                                              AL.mult), wait=[t1, tk_rr])
        tk_xnb = p.v(lambda v: v.tensor_copy(xnB[:, :], xnf[:, :]), wait=[tk_xn])
        tk_xncons = tk_xnb
        for i in range(4):
            nmp = "0" if i % 2 == 0 else "1"
            psd = ps0 if i % 2 == 0 else ps1
            tmm = mm_group(nmp, psd[:, :],
                           [(wdt2_sb[0:64, :], zdtB[0:64, i * 512:(i + 1) * 512],
                             [tk_zdt, tk_wdt2])])
            ta1 = p.s(lambda s, psd=psd: s.activation(
                aT[0][:, 0:512], psd[:, :], ACT.Abs, bias=bdt2[:, 0:1]),
                wait=[tmm, tk_dt] + ([tk_acons.get(0), tk_acons.get(1)]
                                     if i == 0 else []))
            ta2 = p.s(lambda s: s.activation(
                aT[0][:, 512:1024], aT[0][:, 0:512], ACT.Exp,
                scale=cst[:, C_NEG1:C_NEG1 + 1]), wait=[ta1])
            ta3 = p.s(lambda s: s.activation(
                aT[0][:, 0:512], aT[0][:, 512:1024], ACT.Ln,
                bias=cst[:, C_ONE:C_ONE + 1]), wait=[ta2])
            ta4 = p.s(lambda s, psd=psd: s.activation(
                aT[0][:, 512:1024], psd[:, :], ACT.Relu, bias=bdt2[:, 0:1]),
                wait=[ta3])
            psset(nmp, ta4)
            tk_dt = p.v(lambda v, i=i: v.tensor_tensor(
                dtt[:, i * 512:(i + 1) * 512], aT[0][:, 0:512],
                aT[0][:, 512:1024], AL.add), wait=[ta4])
        tk_id_free = tk_yevac
        def rep_rows(n):
            sl = n % 2
            tk_brep[sl] = p.dma("gpsimd", browB[sl][0:1, :],
                                ag1_dst[tc, 64 + n:65 + n, :],
                                wait=[tk_brep.get(sl + 2), tk_ag1],
                                slot=f"br{sl}")
            tk_crep[sl] = p.dma("gpsimd", crowB[sl][0:1, :],
                                ag1_dst[tc, 80 + n:81 + n, :],
                                wait=[tk_crep.get(sl + 2), tk_ag1],
                                slot=f"cr{sl}")

        def rep_produce(n):
            sl = n % 2
            evb, evc = [], []
            mmb = mmc = None
            mmq = {}
            for i in range(4):
                bname = ["0", "1", "s", "p"][i]
                psd = [ps0, ps1, ps_s, ps_p][i]
                mmb = p.t(lambda t, sl=sl, i=i, psd=psd: t.matmul(
                    psd[:, :], onesr[0:1, :],
                    browB[sl][0:1, i * 512:(i + 1) * 512],
                    start=True, stop=True),
                    wait=[tk_brep[sl], tk_onesr, psget(bname)])
                mmq[i] = mmb
            for half, pst in ((0, psA), (1, psB)):
                ev = p.s(lambda s, sl=sl, half=half, pst=pst, n=n:
                         s.activation(
                    brepB[sl][:, half * 1024:(half + 1) * 1024], pst[:, :],
                    ACT.Identity, scale=iA_sb[:, n:n + 1]),
                    wait=[mmq[2 * half], mmq[2 * half + 1], tk_iA,
                          tk_wcons.get(sl)])
                psset(["0", "1", "s", "p"][2 * half], ev)
                psset(["0", "1", "s", "p"][2 * half + 1], ev)
                evb.append(ev)
            tk_brep[sl + 2] = mmb
            for i in range(4):
                cname = ["0", "1", "s", "p"][i]
                psd = [ps0, ps1, ps_s, ps_p][i]
                mmc = p.t(lambda t, sl=sl, i=i, psd=psd: t.matmul(
                    psd[:, :], onesr[0:1, :],
                    crowB[sl][0:1, i * 512:(i + 1) * 512],
                    start=True, stop=True),
                    wait=[tk_crep[sl], tk_onesr, psget(cname)])
                mmq[i] = mmc
            for half, pst in ((0, psA), (1, psB)):
                ev = p.s(lambda s, sl=sl, half=half, pst=pst: s.activation(
                    crepB[sl][:, half * 1024:(half + 1) * 1024], pst[:, :],
                    ACT.Identity),
                    wait=[mmq[2 * half], mmq[2 * half + 1],
                          tk_pcons2.get(sl)])
                psset(["0", "1", "s", "p"][2 * half], ev)
                psset(["0", "1", "s", "p"][2 * half + 1], ev)
                evc.append(ev)
            tk_crep[sl + 2] = mmc
            tk_evb[sl] = evb
            tk_evc[sl] = evc

        rep_rows(0)
        rep_rows(1)
        rep_produce(0)
        for n in range(N16):
            sl = n % 2
            ta = p.s(lambda s, sl=sl, n=n: s.activation(
                aT[sl][:, :], dtt[:, :], ACT.Exp, scale=A_sb[:, n:n + 1]),
                wait=[tk_dt, tk_A, tk_acons.get(sl)])
            if n + 2 < N16:
                rep_rows(n + 2)
            if n + 1 < N16:
                rep_produce(n + 1)
            p0b_load(tc * 16 + n)
            tw = p.v(lambda v, sl=sl: v.tensor_tensor(
                pB[sl][:, :], xnB[:, :], brepB[sl][:, :], AL.mult),
                wait=[tk_xnb, tk_pcons.get(sl)] + tk_evb[sl])
            tk_wcons[sl] = tw
            tb = p.v(lambda v, sl=sl: v.scalar_tensor_tensor(
                bscan[:, :], aT[sl][:, :], 1.0, pB[sl][:, :], AL.subtract,
                AL.mult), wait=[ta, tw])
            if tc % 4 == 0:
                th = p.v(lambda v, sl=sl: v.tensor_tensor_scan(
                    hB[:, :], aT[sl][:, :], bscan[:, :], 0.0, AL.mult, AL.add),
                    wait=[tb])
            else:
                th = p.v(lambda v, sl=sl, n=n: v.tensor_tensor_scan(
                    hB[:, :], aT[sl][:, :], bscan[:, :], carry0[:, n:n + 1],
                    AL.mult, AL.add), wait=[tb])
            tk_acons[sl] = th
            p.v(lambda v, n=n: v.tensor_copy(carry0[:, n:n + 1],
                                             hB[:, 2047:2048]), wait=[th])
            tp = p.v(lambda v, sl=sl: v.tensor_tensor(
                pB[sl][:, :], hB[:, :], crepB[sl][:, :], AL.mult),
                wait=[th] + tk_evc[sl])
            tk_pcons2[sl] = tp
            last = None
            for i in range(4):
                ww = [tp, tk_id] + ([tk_id_free] if (n == 0 and i == 0) else [])
                last = p.t(lambda t, sl=sl, i=i, n=n: t.matmul(
                    ps_y[:, i * 512:(i + 1) * 512], id_sb[:, :],
                    pB[sl][:, i * 512:(i + 1) * 512], start=(n == 0),
                    stop=(n == N16 - 1)), wait=ww if i == 0 else [])
            tk_pcons[sl] = last
            p0b_consume(tc * 16 + n)
        tk_yevac = p.v(lambda v: v.scalar_tensor_tensor(
            xd[:, :], xnf[:, :], dsk[:, 0:1], ps_y[:, :], AL.mult, AL.add),
            wait=[tk_pcons[0], tk_pcons[1], tk_xn])
        tk_y_dma = p.dma("gpsimd", ya_src[tc, :, :], xd[:, :],
                         wait=[tk_yevac], slot="yst")
        y_dma_toks.append(tk_y_dma)
    psset("y", tk_yevac)
    tk_a1 = p.v(lambda v: v.tensor_scalar(a1_pp[:, :], a1_pp[:, :], 1.0 / D,
                                          None, AL.mult),
                wait=[pz_state["a1last"]])
    tk_a2 = p.v(lambda v: v.tensor_scalar(a2_pp[:, :], a2_pp[:, :],
                                          1.0 / (4 * D), None, AL.mult),
                wait=[pz_state["a2last"]])
    tk_bw_st = p.drain("gpsimd")
    tk_a2a = p.coll("AllToAll", [ya_src.ap()], [ya_dst.ap()],
                    wait=[tk_bw_st])
    AF.pop()
    AB.pop()

    p.barrier()
    AF.push()
    AB.push()
    ytl = [AF.take(2048), AF.take(2048)]
    tk_x2 = []
    ty_prev = {}
    for k in range(DC):
        sl = k % 2
        tyl = p.dma("sync", ytl[sl][:, :], ya_dst[k, :, :],
                    wait=[tk_a2a, ty_prev.get(sl)], slot=f"yt{sl}")
        tadd = p.v(lambda v, k=k, sl=sl: v.tensor_tensor(
            XS[k][:, :], XS[k][:, :], ytl[sl][:, :], AL.add), wait=[tyl])
        ty_prev[sl] = tadd
        tk_x2.append(tadd)
    if stage == 1:
        p.barrier()
        for k in range(DC):
            p.dma("sync", out_c[k * 128:(k + 1) * 128, :], XS[k][:, :],
                  slot="out")
        p.g(lambda g: g.nop(), wait=[p.drain("sync")])
        return nc, p

    evs2 = ln_stats()
    tk_bw2 = p.dma("sync", browD[0:2, :], ROWS[0:2, :], wait=evs2, slot="brw")
    murF = AF.take(2048)
    rrF = AF.take(2048)
    tmpF = AF.take(2048)
    tk_mu = bcast(murF[:, :], browD[0:1, :], "mu", wait=[tk_bw2])
    tk_m2 = bcast(rrF[:, :], browD[1:2, :], "rr", wait=[tk_bw2])
    p.pend["vector"] += [tk_mu, tk_m2]
    p.pend["scalar"] += [tk_mu, tk_m2]
    tk_rr = finalize_reps(TOK, murF[:, :], rrF[:, :], tmpF[:, :], rrF[:, :])
    xn2B = [AB.take(2048) for _ in range(DC)]
    tk_xn2 = []
    for k in range(DC):
        t1 = p.v(lambda v, k=k: v.tensor_tensor(tmpF[:, :], XS[k][:, :],
                                                murF[:, :], AL.subtract),
                 wait=[tk_mu, tk_rr, tk_x2[k]] + ([tk_xn2[-1]] if tk_xn2 else []))
        t2 = p.v(lambda v, k=k: v.tensor_tensor(xn2B[k][:, :], tmpF[:, :],
                                                rrF[:, :], AL.mult),
                 wait=[t1, tk_rr])
        tk_xn2.append(t2)
    wfB = AB.take(8192)
    tk_wf = kload("gpsimd", wfB[:, :], l0w_T.ap(), 8, "wf")
    diB = AB.take(1024)
    tk_di = p.dma("gpsimd", diB[0:64, :], dishaI[:, :], slot="di")
    ppB = AB.take(64)
    tk_ppat = p.dma("sync", ppB[:, 0:64], pool_pat[:, :], slot="pp")
    poolB = AB.take(512)
    for i in range(4):
        ppairs = [(ppB[:, 0:64], xn2B[kc][:, i * 512:(i + 1) * 512],
                   [tk_xn2[kc], tk_ppat]) for kc in range(DC)]
        tkpool = mm_group("p", ps_p[0:64, :], ppairs)
        tkpev = p.v(lambda v: v.tensor_copy(poolB[0:64, :], ps_p[0:64, :]),
                    wait=[tkpool, p.cur("tt")])
        psset("p", tkpev)
        for m in range(DC):
            nmp = "0" if (i * 8 + m) % 2 == 0 else "1"
            psd = ps0 if (i * 8 + m) % 2 == 0 else ps1
            pairs = [(wfB[:, kc * 1024 + m * 128:kc * 1024 + (m + 1) * 128],
                      xn2B[kc][:, i * 512:(i + 1) * 512], [tk_xn2[kc], tk_wf])
                     for kc in range(DC)]
            pairs.append((diB[0:64, m * 128:(m + 1) * 128], poolB[0:64, :],
                          [tkpev, tk_di]))
            tkm = mm_group(nmp, psd[:, :], pairs)
            tke = p.v(lambda v, m=m, i=i, psd=psd: v.tensor_tensor(
                XS[m][:, i * 512:(i + 1) * 512], XS[m][:, i * 512:(i + 1) * 512],
                psd[:, :], AL.add), wait=[tkm])
            psset(nmp, tke)
    tk_x3 = [psget("0"), psget("1")]
    for k in range(DC):
        p.dma("gpsimd", x3t_src[:, k * 3:(k + 1) * 3],
              XS[k][:, 2045:2048], wait=tk_x3, slot="x3t")
    tk_agx3 = p.coll("AllGather", [x3t_src.ap()], [x3t_dst.ap()],
                     wait=[p.drain("gpsimd")])
    AF.pop()
    AB.pop()
    if stage == 2:
        p.barrier()
        for k in range(DC):
            p.dma("sync", out_c[k * 128:(k + 1) * 128, :], XS[k][:, :],
                  slot="out")
        p.g(lambda g: g.nop(), wait=[p.drain("sync")])
        return nc, p

    p.barrier()
    AF.push()
    AB.push()
    evs3 = ln_stats()
    tk_bw3 = p.dma("sync", browD[0:2, :], ROWS[0:2, :], wait=evs3, slot="brw")
    AF.push()
    muR3 = AF.take(2048)
    m2R3 = AF.take(2048)
    scR3 = AF.take(2048)
    rR3 = AF.take(2048)
    rcR3 = AF.take(2048)
    tkmu3 = bcast(muR3[:, :], browD[0:1, :], "mu", wait=[tk_bw3])
    tkm23 = bcast(m2R3[:, :], browD[1:2, :], "rr", wait=[tk_bw3])
    p.pend["vector"] += [tkmu3, tkm23]
    p.pend["scalar"] += [tkmu3, tkm23]
    tk_fin3 = finalize_reps(TOK, muR3[:, :], m2R3[:, :], scR3[:, :],
                            rR3[:, :], rcR3[:, :])
    tk_bw3a = tk_bw3
    tk_bw3b = p.dma("sync", browD[1:2, :], rR3[0:1, :],
                    wait=[tk_fin3, tkm23], slot="brw")
    tk_bw3c = p.dma("sync", browD[2:3, :], rcR3[0:1, :],
                    wait=[tk_fin3, tk_bw3b], slot="brw")
    AF.pop()

    xpv = AF.take(28)
    AB.push()
    xallB = AF.take(DC * 24 // 4 * 4) if False else None
    xall = AF.take(192)
    tk_xall = p.dma("sync", xall[:, :].rearrange("p (r c) -> p r c", r=8),
                    x3t_dst.ap().rearrange("r p c -> p r c"),
                    wait=[tk_agx3, tk_fin3, tk_bw3b, tk_bw3c], slot="xal")
    tk_sel = p.v(lambda v: v.tensor_scalar(xpv[:, 0:24], xall[:, 0:24],
                                           hmsk[:, 0:1], None, AL.mult),
                 wait=[tk_xall, tk_bp])
    for j in range(1, R):
        tk_sel = p.v(lambda v, j=j: v.scalar_tensor_tensor(
            xpv[:, 0:24], xall[:, j * 24:(j + 1) * 24], hmsk[:, j:j + 1],
            xpv[:, 0:24], AL.mult, AL.add), wait=[tk_sel])
    hghB = []
    tk_hgh = []
    hpb = [AB.take(4) for _ in range(DC)]
    hps = [AB.take(4) for _ in range(DC)]
    tcst = []
    for k in range(DC):
        t1 = p.v(lambda v, k=k: v.tensor_copy(hpb[k][:, 0:3],
                                              xpv[:, k * 3:k * 3 + 3]),
                 wait=[tk_sel])
        t2 = p.s(lambda s, k=k: s.activation(hps[k][:, 0:3],
                                             xpv[:, k * 3:k * 3 + 3],
                                             ACT.Square), wait=[tk_sel])
        tcst.append((t1, t2))
    pairs = [(lnl_sb[:, 0:2], hpb[k][:, 0:3], [tcst[k][0], tk_lnl])
             for k in range(DC)]
    pairs += [(lnl_sb[:, 2:4], hps[k][:, 0:3], [tcst[k][1]]) for k in range(DC)]
    tkh = mm_group("s", ps_s[0:2, 0:3], pairs)
    tkhe = p.v(lambda v: v.tensor_copy(ROWS[0:2, 0:3], ps_s[0:2, 0:3]),
               wait=[tkh])
    psset("s", tkhe)
    tkhd = p.dma("sync", browD[3:5, 0:3], ROWS[0:2, 0:3],
                 wait=[tkhe, tk_bw3c], slot="brw")
    mur3h = AF.take(4)
    rcr3h = AF.take(4)
    tmp3 = AF.take(4)
    scr3h = AF.take(4)
    r3h = AF.take(4)
    tkb3 = bcast(mur3h[:, 0:3], browD[3:4, 0:3], "m3h", wait=[tkhd])
    tkb4 = bcast(rcr3h[:, 0:3], browD[4:5, 0:3], "r3h", wait=[tkhd])
    p.pend["vector"] += [tkb3, tkb4]
    p.pend["scalar"] += [tkb3, tkb4]
    tkb4 = finalize_reps(3, mur3h[:, 0:3], rcr3h[:, 0:3], scr3h[:, 0:3],
                         r3h[:, 0:3], rcr3h[:, 0:3])
    for k in range(DC):
        hg = AB.take(4)
        hghB.append(hg)
        t1 = p.v(lambda v, k=k: v.tensor_tensor(
            tmp3[:, 0:3], xpv[:, k * 3:k * 3 + 3], mur3h[:, 0:3], AL.subtract),
            wait=[tkb3] + ([tk_hgh[-1]] if tk_hgh else []))
        t2 = p.v(lambda v: v.tensor_tensor(tmp3[:, 0:3], tmp3[:, 0:3],
                                           rcr3h[:, 0:3], AL.mult),
                 wait=[t1, tkb4])
        t3 = p.v(lambda v, k=k, hg=hg: v.tensor_scalar(
            hg[:, 0:3], tmp3[:, 0:3], g2_pp[:, k:k + 1], be2_pp[:, k:k + 1],
            AL.mult, AL.add), wait=[t2])
        tk_hgh.append(t3)

    wgtB = AB.take(8192)
    AB.push()
    p2fB = [AB.take(TH), AB.take(TH)]
    zeroB = AB.take(TH)
    ecdB = [AB.take(TH), AB.take(TH)]
    spillB = [AB.take(512), AB.take(512)]
    cmldB = [AB.take(TH), AB.take(TH)]
    tk_zero = p.v(lambda v: v.memset(zeroB[:, :], 0.0))

    tk_sv_dma = {}
    tk_y2loc = {}
    tk_p2mm = {}
    tk_spill_inp = {}
    tk_spill_cm = {}
    tk_inpst = {}
    tk_cmst = {}
    tk_ecs = {}
    tk_hcons = None
    tk_econs = None
    tk_conv = {}
    tk_wlast = None
    tk_win = {}
    tk_hg_prev_g = None
    tk_te_h = {}
    for half in range(2):
        toff = half * TH
        AB.push()
        ueB = [AB.take(TH + 3) for _ in range(DC)]
        AB.push()
        winB = [AB.take(1024), AB.take(1024)]
        hgB = [AB.take(512) for _ in range(DC)]
        svB = [AB.take(512), AB.take(512)]
        AF.push()
        mur3 = AF.take(TH)
        r3r = AF.take(TH)
        rcr = AF.take(TH)
        t1F = AF.take(512)
        if half == 0:
            hguards = [p.cur("vv")]
        else:
            hguards = [tk_te_h[(0, 0)], tk_te_h[(0, 1)], tk_te_h[(0, 2)]]
        tkm3 = bcast(mur3[:, :], browD[0:1, toff:toff + TH], "mu",
                     wait=[tk_bw3a] + hguards)
        tkr3 = bcast(r3r[:, :], browD[1:2, toff:toff + TH], "rr",
                     wait=[tk_bw3b] + hguards)
        tkrc = bcast(rcr[:, :], browD[2:3, toff:toff + TH], "rc",
                     wait=[tk_bw3c] + hguards)
        tk_uev = {}
        tk_ue_halo = [None] * DC
        if half == 1:
            for m in range(DC):
                tk_ue_halo[m] = p.v(lambda v, m=m: v.tensor_copy(
                    ueB[m][:, 0:3], ueB[m][:, TH:TH + 3]),
                    wait=[tk_conv[(0, m)]])
        for i in range(2):
            isl = slice(toff + i * 512, toff + (i + 1) * 512)
            hg_guard = p.cur("tt")
            tk_hg = {}
            tk_hg_prev = None
            for k in range(DC):
                tt1 = p.v(lambda v, k=k, isl=isl, i=i: v.tensor_tensor(
                    t1F[:, :], XS[k][:, isl], mur3[:, i * 512:(i + 1) * 512],
                    AL.subtract), wait=[tkm3, tk_hg_prev, hg_guard])
                tt2 = p.v(lambda v, k=k, i=i: v.tensor_tensor(
                    hgB[k][:, :], t1F[:, :], rcr[:, i * 512:(i + 1) * 512],
                    AL.mult), wait=[tt1, tkrc])
                tt3 = p.v(lambda v, k=k: v.tensor_scalar(
                    hgB[k][:, :], hgB[k][:, :], g2_pp[:, k:k + 1],
                    be2_pp[:, k:k + 1], AL.mult, AL.add), wait=[tt2])
                tt4 = p.v(lambda v, k=k, i=i: v.tensor_tensor(
                    t1F[:, :], t1F[:, :], r3r[:, i * 512:(i + 1) * 512],
                    AL.mult), wait=[tt3, tkr3])
                tt5 = p.v(lambda v, k=k, isl=isl: v.tensor_tensor(
                    XS[k][:, isl], XS[k][:, isl], t1F[:, :], AL.add),
                    wait=[tt4])
                tk_hg_prev = tt5
                tk_hg_prev_g = tt5
                tk_hg[k] = tt3
            for uv in range(2):
                for m in range(DC):
                    sl = m % 2
                    tk_win[sl] = kload(
                        "gpsimd", winB[sl][:, :],
                        w_inT.ap()[:, uv * 1024 + m * 128:uv * 1024 + (m + 1) * 128],
                        8, f"wi{sl}", wait=[tk_win.get(sl + 2)])
                    nmp = "0" if (uv * 8 + m) % 2 == 0 else "1"
                    psd = ps0 if (uv * 8 + m) % 2 == 0 else ps1
                    pairs = [(winB[sl][:, kc * 128:(kc + 1) * 128],
                              hgB[kc][:, :], [tk_hg[kc], tk_win[sl]])
                             for kc in range(DC)]
                    tkm = mm_group(nmp, psd[:, :], pairs)
                    tk_win[sl + 2] = tkm
                    if uv == 0:
                        tev = p.s(lambda s, m=m, i=i, psd=psd: s.activation(
                            ueB[m][:, 3 + i * 512:3 + (i + 1) * 512],
                            psd[:, :], ACT.Identity, bias=bin_pp[:, m:m + 1]),
                            wait=[tkm, tk_ue_halo[m]])
                        tk_uev[(i, m)] = tev
                        psset(nmp, tev)
                    else:
                        ssl = m % 2
                        tsv = p.s(lambda s, ssl=ssl, psd=psd, m=m: s.activation(
                            svB[ssl][:, :], psd[:, :], ACT.Sigmoid,
                            bias=bin_pp[:, 8 + m:9 + m]),
                            wait=[tkm, tk_sv_dma.get(ssl)])
                        psset(nmp, tsv)
                        tk_sv_dma[ssl] = p.dma(
                            "sync",
                            sv_spill[m * 128:(m + 1) * 128,
                                     toff + i * 512:toff + (i + 1) * 512],
                            svB[ssl][:, :], wait=[tsv], slot=f"sv{ssl}")
        if half == 0:
            for m in range(DC):
                tkw = kload("gpsimd", winB[0][:, :],
                            w_inT.ap()[:, m * 128:(m + 1) * 128], 8, "wi0",
                            wait=[p.cur("tt")])
                pairs = [(winB[0][:, kc * 128:(kc + 1) * 128],
                          hghB[kc][:, 0:3], [tk_hgh[kc], tkw])
                         for kc in range(DC)]
                tkm = mm_group("0", ps0[:, 0:3], pairs)
                tev = p.s(lambda s, m=m: s.activation(
                    ueB[m][:, 0:3], ps0[:, 0:3], ACT.Identity,
                    scale=hz[:, 0:1], bias=binh_pp[:, m:m + 1]),
                    wait=[tkm, tk_binh, tk_uev[(0, m)], tk_uev[(1, m)]])
                psset("0", tev)
                tk_ue_halo[m] = tev
        AF.pop()
        AB.pop()
        AB.push()
        ucB = [AB.take(TH) for _ in range(DC)]
        AF.push()
        convF = [AF.take(TH), AF.take(TH)]
        for m in range(DC):
            tc0 = p.v(lambda v, m=m: v.tensor_scalar(
                convF[0][:, :], ueB[m][:, 3:3 + TH],
                wcv[:, m * 4 + 3:m * 4 + 4], bcv_pp[:, m:m + 1],
                AL.mult, AL.add),
                wait=[tk_uev[(0, m)], tk_uev[(1, m)], tk_ue_halo[m],
                      tk_conv.get((half - 1, m)), tk_wlast])
            src, dst = 0, 1
            for j in range(3):
                tc0 = p.v(lambda v, m=m, j=j, src=src, dst=dst:
                          v.scalar_tensor_tensor(
                              convF[dst][:, :], ueB[m][:, j:j + TH],
                              wcv[:, m * 4 + j:m * 4 + j + 1],
                              convF[src][:, :], AL.mult, AL.add), wait=[tc0])
                src, dst = dst, src
            tk_conv[(half, m)] = p.v(lambda v, m=m, src=src: v.tensor_copy(
                ucB[m][:, :], convF[src][:, :]), wait=[tc0, p.cur("tt")])
        AF.pop()
        AF.push()
        decayF = [AF.take(TH) for _ in range(DC)]
        hF = AF.take(TH)
        eF = AF.take(TH)
        inldF = [AF.take(TH), AF.take(TH)]
        specs = [("d", wdT), ("B", wBT), ("C", wCT)]
        tk_sp = {}
        for (nm, wt) in specs:
            tkwg = kload("gpsimd", wgtB[:, :], wt.ap(), 8, "wg",
                         wait=[tk_wlast, p.cur("tt")])
            for m in range(DC):
                for i in range(2):
                    nmp = "0" if (m * 2 + i) % 2 == 0 else "1"
                    psd = ps0 if (m * 2 + i) % 2 == 0 else ps1
                    pairs = [(wgtB[:, kc * 1024 + m * 128:kc * 1024 + (m + 1) * 128],
                              ucB[kc][:, i * 512:(i + 1) * 512],
                              [tk_conv[(half, kc)], tkwg]) for kc in range(DC)]
                    tkm = mm_group(nmp, psd[:, :], pairs)
                    if nm == "d":
                        tev = p.s(lambda s, m=m, i=i, psd=psd: s.activation(
                            decayF[m][:, i * 512:(i + 1) * 512], psd[:, :],
                            ACT.Sigmoid, scale=cst[:, C_NEG1:C_NEG1 + 1],
                            bias=nbd_pp[:, m:m + 1]),
                            wait=[tkm, tk_nbd, tk_te_h.get((half - 1, m))])
                    elif nm == "B":
                        sl = (m * 2 + i) % 2
                        tev = p.v(lambda v, m=m, sl=sl, psd=psd, i=i:
                                  v.scalar_tensor_tensor(
                                      spillB[sl][:, 0:512], psd[:, :],
                                      bB_pp[:, m:m + 1],
                                      ucB[m][:, i * 512:(i + 1) * 512],
                                      AL.add, AL.mult),
                                  wait=[tkm, tk_spill_inp.get(sl)])
                        tk_spill_inp[sl] = p.dma(
                            "sync", inp_spill[m * 128:(m + 1) * 128,
                                              i * 512:(i + 1) * 512],
                            spillB[sl][:, 0:512],
                            wait=[tev, tk_spill_inp.get(sl)], slot=f"si{sl}")
                        tk_inpst[(m, i)] = tk_spill_inp[sl]
                    else:
                        sl = (m * 2 + i) % 2
                        tev = p.s(lambda s, m=m, psd=psd, sl=sl: s.activation(
                            spillB[sl][:, 0:512], psd[:, :], ACT.Identity,
                            bias=bC_pp[:, m:m + 1]),
                            wait=[tkm, tk_spill_cm.get(sl)])
                        tk_spill_cm[sl] = p.dma(
                            "sync", cm_spill[m * 128:(m + 1) * 128,
                                             i * 512:(i + 1) * 512],
                            spillB[sl][:, 0:512],
                            wait=[tev, tk_spill_cm.get(sl)], slot=f"sc{sl}")
                        tk_cmst[(m, i)] = tk_spill_cm[sl]
                    psset(nmp, tev)
                    tk_sp[(nm, m, i)] = tev
            tk_wlast = tk_sp[(nm, 7, 1)]
        tk_cmld = {}
        tk_inld = {}
        for m in range(DC):
            sl = m % 2
            tk_inld[sl] = p.dma("gpsimd", inldF[sl][:, :],
                                inp_spill[m * 128:(m + 1) * 128, :],
                                wait=[tk_inpst[(m, 0)], tk_inpst[(m, 1)],
                                      tk_inld.get(sl + 2)], slot=f"ri{sl}")
            tk_cmld[sl] = p.dma("sync", cmldB[sl][:, :],
                                cm_spill[m * 128:(m + 1) * 128, :],
                                wait=[tk_cmst[(m, 0)], tk_cmst[(m, 1)],
                                      tk_cmld.get(sl + 2)], slot=f"cl{sl}")
            init_h = 0.0 if half == 0 else cL1[:, m:m + 1]
            init_e = 1.0 if half == 0 else cE1[:, m:m + 1]
            th = p.v(lambda v, m=m, sl=sl, init_h=init_h: v.tensor_tensor_scan(
                hF[:, :], decayF[m][:, :], inldF[sl][:, :], init_h,
                AL.mult, AL.add),
                wait=[tk_sp[("d", m, 0)], tk_sp[("d", m, 1)], tk_inld[sl],
                      tk_hcons])
            te = p.v(lambda v, m=m, init_e=init_e: v.tensor_tensor_scan(
                eF[:, :], decayF[m][:, :], zeroB[:, :], init_e,
                AL.mult, AL.add), wait=[tk_zero, tk_econs, th])
            p.v(lambda v, m=m: v.tensor_copy(cL1[:, m:m + 1], hF[:, TH - 1:TH]),
                wait=[th])
            p.v(lambda v, m=m: v.tensor_copy(cE1[:, m:m + 1], eF[:, TH - 1:TH]),
                wait=[te])
            tk_te_h[(half, m)] = te
            tp2 = p.v(lambda v, sl=sl: v.tensor_tensor(
                p2fB[sl][:, :], hF[:, :], cmldB[sl][:, :], AL.mult),
                wait=[th, tk_cmld[sl]])
            tk_hcons = tp2
            tec = p.v(lambda v, m=m, sl=sl: v.tensor_tensor(
                ecdB[sl][:, :], eF[:, :], cmldB[sl][:, :], AL.mult),
                wait=[te, tk_cmld[sl], tk_ecs.get(sl)])
            tk_econs = tec
            tk_ecs[sl] = p.dma("sync",
                               ec_spill[m * 128:(m + 1) * 128, toff:toff + TH],
                               ecdB[sl][:, :], wait=[tec], slot=f"ec{sl}")
            tk_cmld[sl + 2] = tec
            tk_inld[sl + 2] = tp2
            for i in range(2):
                nmp = "s" if i == 0 else "p"
                pst = ps_s[0:1, :] if i == 0 else ps_p[0:1, :]
                ww = [tp2, tk_ones] + ([psget(nmp)] if m == 0 else [])
                tky = p.t(lambda t, sl=sl, i=i, m=m, pst=pst: t.matmul(
                    pst, ones_sb[:, :], p2fB[sl][:, i * 512:(i + 1) * 512],
                    start=(m == 0), stop=(m == DC - 1)), wait=ww)
                tk_p2mm[sl] = tky
        for i in range(2):
            nmp = "s" if i == 0 else "p"
            pst = ps_s[0:1, :] if i == 0 else ps_p[0:1, :]
            ev = p.v(lambda v, i=i, toff=toff, pst=pst: v.tensor_copy(
                ROWS[0:1, toff + i * 512:toff + (i + 1) * 512], pst),
                wait=[tk_p2mm[0], tk_p2mm[1]])
            psset(nmp, ev)
            tk_y2loc[(half, i)] = ev
        AF.pop()
        AB.pop()
        AB.pop()
    p.dma("gpsimd", qp_src[:, 0:8], cL1[:, :], wait=[tk_hcons], slot="qps")
    p.dma("gpsimd", qp_src[:, 8:16], cE1[:, :], wait=[tk_econs], slot="qps")
    tk_agqp = p.coll("AllGather", [qp_src.ap()], [qp_dst.ap()],
                     wait=[p.drain("gpsimd")])
    tk_y2 = dict(tk_y2loc)
    tH = p.v(lambda v: v.memset(H_a[:, :], 0.0))
    cur, oth = H_a, H_b
    tql = None
    for j in range(R - 1):
        tql = p.dma("sync", qpt[:, :], qp_dst[j, :, :],
                    wait=[tk_agqp, tH, tql], slot="qpl")
        t0 = p.v(lambda v, j=j: v.tensor_scalar(
            qpt[:, 8:16], qpt[:, 8:16], hmq[:, j:j + 1], hmpa[:, j:j + 1],
            AL.mult, AL.add), wait=[tql, tk_bp])
        t0b = p.v(lambda v, j=j: v.tensor_scalar(
            qpt[:, 0:8], qpt[:, 0:8], hmq[:, j:j + 1], None, AL.mult),
            wait=[tql])
        t1 = p.v(lambda v, cur=cur, oth=oth: v.tensor_tensor(
            oth[:, :], cur[:, :], qpt[:, 8:16], AL.mult), wait=[tH, t0])
        tH = p.v(lambda v, oth=oth: v.tensor_tensor(
            oth[:, :], oth[:, :], qpt[:, 0:8], AL.add), wait=[t1, t0b])
        cur, oth = oth, cur
        tql = tH
    Hfin = cur
    tk_z2c = {}
    for half in range(2):
        toff = half * TH
        tk_zmm = {}
        for m in range(DC):
            sl = m % 2
            tld = p.dma("sync", ecdB[sl][:, :],
                        ec_spill[m * 128:(m + 1) * 128, toff:toff + TH],
                        wait=[tk_ecs.get(0), tk_ecs.get(1), tk_z2c.get(sl)],
                        slot=f"ec{sl}")
            tz = p.v(lambda v, sl=sl, m=m: v.tensor_scalar(
                p2fB[sl][:, :], ecdB[sl][:, :], Hfin[:, m:m + 1], None,
                AL.mult), wait=[tH, tld, tk_zmm.get(sl), tk_p2mm.get(sl)])
            tk_z2c[sl] = tz
            for i in range(2):
                nmp = "s" if i == 0 else "p"
                pst = ps_s[0:1, :] if i == 0 else ps_p[0:1, :]
                ww = [tz, tk_ones] + ([psget(nmp)] if m == 0 else [])
                tky = p.t(lambda t, sl=sl, i=i, m=m, pst=pst: t.matmul(
                    pst, ones_sb[:, :], p2fB[sl][:, i * 512:(i + 1) * 512],
                    start=(m == 0), stop=(m == DC - 1)), wait=ww)
                tk_zmm[sl] = tky
        for i in range(2):
            nmp = "s" if i == 0 else "p"
            pst = ps_s[0:1, :] if i == 0 else ps_p[0:1, :]
            ev = p.v(lambda v, i=i, toff=toff, pst=pst: v.tensor_tensor(
                ROWS[0:1, toff + i * 512:toff + (i + 1) * 512],
                ROWS[0:1, toff + i * 512:toff + (i + 1) * 512],
                pst, AL.add), wait=[tk_zmm[0], tk_zmm[1]])
            psset(nmp, ev)
            tk_y2[(half, i)] = ev
    AB.pop()
    tk_y2b = p.dma("sync", browD[5:6, :], ROWS[0:1, :],
                   wait=list(tk_y2.values()), slot="brw")
    AB.push()
    y2rB = AB.take(2048)
    gatB = [AB.take(2048) for _ in range(DC)]
    svld = [AB.take(2048), AB.take(2048)]
    tk_y2r = p.dma("gpsimd", y2rB[:, :], browD[5:6, :].partition_broadcast(128),
                   wait=[tk_y2b], slot="y2r")
    tk_svld = {}
    tk_gat = []
    for k in range(DC):
        sl = k % 2
        tk_svld[sl] = p.dma("sync", svld[sl][:, :],
                            sv_spill[k * 128:(k + 1) * 128, :],
                            wait=[tk_sv_dma.get(0), tk_sv_dma.get(1),
                                  tk_svld.get(sl + 2)], slot=f"sv{sl}")
        tg = p.v(lambda v, k=k, sl=sl: v.tensor_tensor(
            gatB[k][:, :], svld[sl][:, :], y2rB[:, :], AL.mult),
            wait=[tk_svld[sl], tk_y2r])
        tk_svld[sl + 2] = tg
        tk_gat.append(tg)
    tkwo = kload("gpsimd", wgtB[:, :], woT.ap(), 8, "wg",
                 wait=[tk_wlast, p.cur("tt")])
    for m in range(DC):
        for i in range(4):
            nmp = "0" if (m * 4 + i) % 2 == 0 else "1"
            psd = ps0 if (m * 4 + i) % 2 == 0 else ps1
            pairs = [(wgtB[:, kc * 1024 + m * 128:kc * 1024 + (m + 1) * 128],
                      gatB[kc][:, i * 512:(i + 1) * 512], [tk_gat[kc], tkwo])
                     for kc in range(DC)]
            tkm = mm_group(nmp, psd[:, :], pairs)
            tev = p.v(lambda v, m=m, i=i, psd=psd: v.scalar_tensor_tensor(
                XS[m][:, i * 512:(i + 1) * 512], psd[:, :], bo_pp[:, m:m + 1],
                XS[m][:, i * 512:(i + 1) * 512], AL.add, AL.add), wait=[tkm])
            psset(nmp, tev)
    AB.pop()
    AF.pop()
    AB.pop()
    if stage == 3:
        p.barrier()
        for k in range(DC):
            p.dma("sync", out_c[k * 128:(k + 1) * 128, :], XS[k][:, :],
                  slot="out")
        p.g(lambda g: g.nop(), wait=[p.drain("sync")])
        return nc, p

    L1SL = [("sync", "b10"), ("sync", "b11"), ("sync", "b12"),
            ("scalar", "b14"), ("scalar", "b15"), ("scalar", "b16")]
    L2SL = [("sync", "b20"), ("sync", "b21"), ("sync", "b22"),
            ("scalar", "b24"), ("scalar", "b25"), ("scalar", "b26")]
    BNM8 = ["y0", "y1", "y2", "y3", "0", "1", "s", "p"]
    BAP8 = [ps_y[:, 0:512], ps_y[:, 512:1024], ps_y[:, 1024:1536],
            ps_y[:, 1536:2048], ps0, ps1, ps_s, ps_p]
    LOADS1 = [(it, mbp, kc) for it in range(4) for mbp in range(4)
              for kc in range(DC)]
    LOADS2 = [(it, kc2) for it in range(4) for kc2 in range(32)]
    tk_l1seq = {}
    tk_l2seq = {}
    tk_b1cons = {}
    tk_b2cons = {}

    def emit_load1(idx):
        if idx >= len(LOADS1):
            return
        it_, mbp_, kc_ = LOADS1[idx]
        til = idx % 6
        eng, slot = L1SL[til]
        tk_l1seq[idx] = p.dma(
            eng, bw1B[til][:, :],
            bw1_dram.ap()[kc_ * 128:(kc_ + 1) * 128,
                          mbp_ * 1024:(mbp_ + 1) * 1024],
            wait=[tk_bw_st, tk_b1cons.get(til)], slot=slot)

    def emit_load2(idx):
        if idx >= len(LOADS2):
            return
        it_, kc2_ = LOADS2[idx]
        til = idx % 6
        eng, slot = L2SL[til]
        tk_l2seq[idx] = p.dma(
            eng, bw2B[til][:, :],
            bw2_dram.ap()[kc2_ * 128:(kc2_ + 1) * 128, :],
            wait=[tk_bw_st, tk_b2cons.get(til)], slot=slot)

    p.barrier()
    AF.push()
    AB.push()
    bw1B = [AB.take(1024) for _ in range(6)]
    bw2B = [AB.take(1024) for _ in range(6)]
    for idx in range(6):
        emit_load1(idx)
        emit_load2(idx)
    evs4 = ln_stats()
    xn4B = [AB.take(512) for _ in range(DC)]
    h1B = [AB.take(512) for _ in range(32)]
    tk_bw4 = p.dma("sync", browD[0:2, :], ROWS[0:2, :], wait=evs4, slot="brw")
    murE = AF.take(2048)
    rrE = AF.take(2048)
    scrE = AF.take(2048)
    tmpE = AF.take(512)
    x4bF = AF.take(512)
    tk_mu = bcast(murE[:, :], browD[0:1, :], "mu", wait=[tk_bw4])
    tk_m2 = bcast(rrE[:, :], browD[1:2, :], "rr", wait=[tk_bw4])
    p.pend["vector"] += [tk_mu, tk_m2]
    p.pend["scalar"] += [tk_mu, tk_m2]
    tk_rr = finalize_reps(TOK, murE[:, :], rrE[:, :], scrE[:, :], rrE[:, :])
    tk_h1 = {}
    for it in range(4):
        isl = slice(it * 512, (it + 1) * 512)
        curT = p.cur("tt")
        tk_xn4 = {}
        for k in range(DC):
            t1 = p.v(lambda v, k=k, isl=isl: v.tensor_tensor(
                tmpE[:, :], XS[k][:, isl], murE[:, isl], AL.subtract),
                wait=[tk_mu, curT, tk_xn4.get(k - 1)])
            t2 = p.v(lambda v, k=k, isl=isl: v.tensor_tensor(
                xn4B[k][:, :], tmpE[:, :], rrE[:, isl], AL.mult),
                wait=[t1, tk_rr])
            tk_xn4[k] = t2
        for mbp in range(4):
            tk_mmj = {}
            for kc in range(DC):
                idx = (it * 4 + mbp) * DC + kc
                til = idx % 6
                for j in range(8):
                    extra = ([psget(BNM8[j])] if kc == 0 else [])
                    tk_mmj[j] = p.t(lambda t, til=til, j=j, kc=kc: t.matmul(
                        BAP8[j][:, :],
                        bw1B[til][:, j * 128:(j + 1) * 128],
                        xn4B[kc][:, :], start=(kc == 0), stop=(kc == DC - 1)),
                        wait=[tk_l1seq[idx], tk_xn4[kc]] + extra)
                tk_b1cons[til] = tk_mmj[7]
                emit_load1(idx + 6)
            for j in range(8):
                m = mbp * 8 + j
                tev = p.s(lambda s, m=m, j=j: s.activation(
                    h1B[m][:, :], BAP8[j][:, :], ACT.Gelu,
                    bias=b1p[:, m:m + 1], scale=a1_pp[:, m:m + 1]),
                    wait=[tk_mmj[j], tk_a1, curT])
                psset(BNM8[j], tev)
                tk_h1[(it, m)] = tev
        tk_m2j = {}
        for kc2 in range(32):
            idx = it * 32 + kc2
            til = idx % 6
            for j in range(8):
                extra = ([psget(BNM8[j])] if kc2 == 0 else [])
                tk_m2j[j] = p.t(lambda t, til=til, j=j, kc2=kc2: t.matmul(
                    BAP8[j][:, :], bw2B[til][:, j * 128:(j + 1) * 128],
                    h1B[kc2][:, :], start=(kc2 == 0), stop=(kc2 == 31)),
                    wait=[tk_l2seq[idx], tk_h1[(it, kc2)]] + extra)
            tk_b2cons[til] = tk_m2j[7]
            emit_load2(idx + 6)
        for j in range(8):
            m2 = j
            tb = p.v(lambda v, m2=m2, isl=isl: v.tensor_scalar(
                x4bF[:, :], XS[m2][:, isl], b2p[:, m2:m2 + 1], None,
                AL.add))
            tev = p.v(lambda v, m2=m2, isl=isl, j=j:
                      v.scalar_tensor_tensor(
                XS[m2][:, isl], BAP8[j][:, :], a2_pp[:, m2:m2 + 1],
                x4bF[:, :], AL.mult, AL.add),
                wait=[tk_m2j[j], tb, tk_a2])
            psset(BNM8[j], tev)
            p.dma("sync", out_c[m2 * 128:(m2 + 1) * 128, isl],
                  XS[m2][:, isl], wait=[tev], slot="out")
    p.g(lambda g: g.nop(), wait=[p.drain("sync")])
    AF.pop()
    AB.pop()
    return nc, p


_NC_CACHE = {}


def _get_program(stage):
    if stage not in _NC_CACHE:
        nc, p = build_program(stage)
        with nc.Block() as block:
            p.build(block)
        _NC_CACHE[stage] = nc
    return _NC_CACHE[stage]


def make_in_maps(inputs):
    z = inputs
    x = np.ascontiguousarray(z["L0_attn_x"], dtype=np.float32)
    Xf = x.reshape(T, D)
    Wstack = np.concatenate([z["L0_attn_W_dt1"], z["L0_attn_W_B"],
                             z["L0_attn_W_C"]], axis=0)
    pool_pat = np.zeros((128, 64), dtype=BF)
    for pp in range(128):
        pool_pat[pp, pp % 64] = 1.0
    lnl = np.zeros((128, 4), dtype=BF)
    lnl[:, 0] = 1.0 / D
    lnl[:, 3] = 1.0 / D
    consts = np.zeros((1, NCONST), dtype=np.float32)
    consts[0, C_EPS] = 1e-5
    consts[0, C_NEG1] = -1.0
    consts[0, C_NEGH] = -0.5
    consts[0, C_ONE] = 1.0

    def pp8(b):
        return np.ascontiguousarray(
            np.asarray(b, dtype=np.float32).reshape(8, 128).T)

    shared = {
        "wstackT": np.ascontiguousarray(Wstack.T, dtype=np.float32),
        "l0w_T": np.ascontiguousarray(z["L0_ffn_weight"].T, dtype=np.float32),
        "dishaI": np.ascontiguousarray(z["L0_ffn_disha"], dtype=np.float32),
        "pool_pat": pool_pat,
        "w_inT": np.ascontiguousarray(z["L1_attn_w_in"].T, dtype=np.float32),
        "wdT": np.ascontiguousarray(z["L1_attn_w_delta"].T, dtype=np.float32),
        "wBT": np.ascontiguousarray(z["L1_attn_w_B"].T, dtype=np.float32),
        "wCT": np.ascontiguousarray(z["L1_attn_w_C"].T, dtype=np.float32),
        "woT": np.ascontiguousarray(z["L1_attn_w_out"].T, dtype=np.float32),
        "b_in_ppI": np.ascontiguousarray(
            np.asarray(z["L1_attn_b_in"], dtype=np.float32).reshape(16, 128).T),
        "b_d_ppI": pp8(z["L1_attn_b_delta"]),
        "b_B_ppI": pp8(z["L1_attn_b_B"]),
        "b_C_ppI": pp8(z["L1_attn_b_C"]),
        "b_o_ppI": pp8(z["L1_attn_b_out"]),
        "b_cv_ppI": pp8(z["L1_attn_b_conv"]),
        "wcv_ppI": np.ascontiguousarray(
            np.asarray(z["L1_attn_w_conv"], dtype=np.float32)[:, 0, :]
            .reshape(8, 128, 4).transpose(1, 0, 2).reshape(128, 32)),
        "g_ppI": pp8(z["L1_attn_ln_g"]),
        "be_ppI": pp8(z["L1_attn_ln_b"]),
        "w1T": np.ascontiguousarray(z["L1_ffn_w1"].T, dtype=np.float32),
        "w2T": np.ascontiguousarray(z["L1_ffn_w2"].T, dtype=np.float32),
        "w1N": np.ascontiguousarray(z["L1_ffn_w1"], dtype=np.float32),
        "w2N": np.ascontiguousarray(z["L1_ffn_w2"], dtype=np.float32),
        "b1_ppI": np.ascontiguousarray(
            np.asarray(z["L1_ffn_b1"], dtype=np.float32).reshape(32, 128).T),
        "b2_ppI": pp8(z["L1_ffn_b2"]),
        "lnlI": lnl,
        "ones_bfI": np.ones((128, 1), dtype=BF),
        "id_bfI": np.eye(128, dtype=BF),
        "constr": consts,
    }
    in_maps = []
    for c in range(R):
        m = dict(shared)
        m["x_tok"] = np.ascontiguousarray(
            Xf[c * TOK:(c + 1) * TOK, :].T, dtype=np.float32)
        m["x_dsh"] = np.ascontiguousarray(
            Xf[:, c * 128:(c + 1) * 128].T, dtype=np.float32)
        m["wdt2T"] = np.ascontiguousarray(
            z["L0_attn_W_dt2"][c * 128:(c + 1) * 128, :].T, dtype=np.float32)
        m["b_dt2c"] = np.ascontiguousarray(
            z["L0_attn_b_dt2"][c * 128:(c + 1) * 128].reshape(128, 1),
            dtype=np.float32)
        m["logAc"] = np.ascontiguousarray(
            z["L0_attn_log_A"][c * 128:(c + 1) * 128, :], dtype=np.float32)
        m["dskipc"] = np.ascontiguousarray(
            z["L0_attn_D_skip"][c * 128:(c + 1) * 128].reshape(128, 1),
            dtype=np.float32)
        seq_first = (c % 4 == 0)
        hm = np.zeros((128, 8), dtype=np.float32)
        if not seq_first:
            hm[:, c - 1] = 1.0
        m["halomskI"] = hm
        m["halozI"] = np.full((128, 1), 0.0 if seq_first else 1.0,
                              dtype=np.float32)
        q = np.zeros((128, 8), dtype=np.float32)
        lo = (c // 4) * 4
        q[:, lo:c] = 1.0
        m["hmqI"] = q
        m["hmpaI"] = np.ascontiguousarray(1.0 - q)
        in_maps.append(m)
    return in_maps


def kernel(**inputs):
    in_maps = make_in_maps(inputs)
    nc = _get_program(STAGE)
    res = run_bass_kernel_spmd(nc, in_maps, core_ids=list(range(R)))
    out = np.empty((T, D), dtype=np.float32)
    for c in range(R):
        out[c * TOK:(c + 1) * TOK, :] = np.asarray(res.results[c]["out_c"]).T
    return out.reshape(2, 8192, D)


# version 10
# speedup vs baseline: 1.4190x; 1.0141x over previous
import numpy as np
import ml_dtypes
import concourse.bass as bass
import concourse.mybir as mybir
from concourse.bass_utils import run_bass_kernel_spmd
from concourse.aot_env import allow_ici_all_to_all

F32 = mybir.dt.float32
BF16 = mybir.dt.bfloat16
AL = mybir.AluOpType
ACT = mybir.ActivationFunctionType
AXL = mybir.AxisListType
BF = ml_dtypes.bfloat16

R = 8
D = 1024
N16 = 16
TOK = 2048
T = R * TOK
DC = 8
TH = 1024
STAGE = 4

C_EPS, C_NEG1, C_NEGH, C_ONE = 0, 1, 2, 3
NCONST = 8
ENGS = ("sync", "vector", "scalar", "tensor", "gpsimd")


class Prog:

    def __init__(self, nc):
        self.nc = nc
        self.q = {e: [] for e in ENGS}
        self.sems = {}
        self.cnt = {}
        for s in ("vv", "ss", "tt", "gg", "cc", "hs_sync", "hs_gpsimd"):
            self.sems[s] = nc.alloc_semaphore(s)
            self.cnt[s] = 0
        self.pend = {e: [] for e in ENGS}
        self.psfree = {}
        self.slot_eng = {}

    def cur(self, sem):
        return (sem, self.cnt[sem]) if self.cnt[sem] > 0 else None

    def _emit(self, eng, fn, waits, inc):
        waits = [w for w in waits if w is not None] + self.pend[eng]
        self.pend[eng] = []
        for (s, v) in waits:
            assert v <= self.cnt[s], f"future token {s}={v} now {self.cnt[s]}"
        tok = None
        if inc is not None:
            s, amt = inc
            self.cnt[s] += amt
            tok = (s, self.cnt[s])
        self.q[eng].append((tuple(waits), fn, inc))
        return tok

    def v(self, fn, wait=()):
        return self._emit("vector", fn, wait, ("vv", 1))

    def s(self, fn, wait=()):
        return self._emit("scalar", fn, wait, ("ss", 1))

    def t(self, fn, wait=()):
        return self._emit("tensor", fn, wait, ("tt", 1))

    def g(self, fn, wait=()):
        return self._emit("gpsimd", fn, wait, ("gg", 1))

    def dma(self, eng, out, in_, wait=(), slot=None):
        assert slot is not None, "dma needs a slot"
        name = "D_" + slot
        if name not in self.sems:
            self.sems[name] = self.nc.alloc_semaphore(name)
            self.cnt[name] = 0
            self.slot_eng[name] = eng
        assert self.slot_eng[name] == eng, f"slot {slot} used from two engines"
        return self._emit(eng, lambda e, o=out, i=in_: e.dma_start(out=o, in_=i),
                          wait, (name, 16))

    def drain(self, eng):
        waits = [(n, self.cnt[n]) for n in self.cnt
                 if n.startswith("D_") and self.slot_eng.get(n) == eng
                 and self.cnt[n] > 0]
        hs = "hs_" + eng
        if hs not in self.sems:
            self.sems[hs] = self.nc.alloc_semaphore(hs)
            self.cnt[hs] = 0
        return self._emit(eng, lambda e: e.nop(), waits, (hs, 1))

    def barrier(self):
        toks = []
        for eng in ENGS:
            if any(n.startswith("D_") and self.slot_eng.get(n) == eng
                   and self.cnt[n] > 0 for n in self.cnt):
                toks.append(self.drain(eng))
        toks += [(s, self.cnt[s]) for s in ("vv", "ss", "tt", "gg", "cc")
                 if self.cnt[s] > 0]
        for e in ENGS:
            self.pend[e] = list(toks)

    def coll(self, kind, ins, outs, wait=()):
        def fn(g, kind=kind, ins=ins, outs=outs):
            with allow_ici_all_to_all():
                return g.collective_compute(
                    kind, AL.bypass, replica_groups=[list(range(R))],
                    ins=[a.opt() for a in ins], outs=[a.opt() for a in outs])
        return self._emit("gpsimd", fn, wait, ("cc", 1))

    def build(self, block):
        hw = {e: {} for e in ENGS}

        def runner(eng):
            def run(engobj):
                h = hw[eng]
                for waits, fn, inc in self.q[eng]:
                    for (s, v) in waits:
                        if h.get(s, 0) < v:
                            engobj.wait_ge(self.sems[s], v)
                            h[s] = v
                    inst = fn(engobj)
                    if inc is not None:
                        s, amt = inc
                        inst.then_inc(self.sems[s], amt)
            return run
        block.sync(runner("sync"))
        block.vector(runner("vector"))
        block.scalar(runner("scalar"))
        block.tensor(runner("tensor"))
        block.gpsimd(runner("gpsimd"))


class Arena:
    def __init__(self, nc, name, cols, dtype):
        self.t = nc.alloc_sbuf_tensor(name, [128, cols], dtype)
        self.cols, self.off, self.marks = cols, 0, []

    def take(self, cols):
        assert self.off + cols <= self.cols, \
            f"{self.t.name} overflow {self.off}+{cols}>{self.cols}"
        ap = self.t[:, self.off:self.off + cols]
        self.off += cols
        return ap

    def push(self):
        self.marks.append(self.off)

    def pop(self):
        self.off = self.marks.pop()


def build_program(stage=STAGE):
    nc = bass.Bass("TRN2", target_bir_lowering=False,
                   detect_race_conditions=False)

    def din(name, shape, dtype=F32):
        return nc.dram_tensor(name, list(shape), dtype, kind="ExternalInput")

    x_tok = din("x_tok", [D, TOK])
    x_dsh = din("x_dsh", [128, T])
    wstackT = din("wstackT", [D, 96])
    wdt2T = din("wdt2T", [64, 128])
    b_dt2c = din("b_dt2c", [128, 1])
    logAc = din("logAc", [128, N16])
    dskipc = din("dskipc", [128, 1])
    l0w_T = din("l0w_T", [D, D])
    dishaI = din("dishaI", [64, D])
    pool_pat = din("pool_pat", [128, 64], BF16)
    w_inT = din("w_inT", [D, 2 * D])
    wdT = din("wdT", [D, D])
    wBT = din("wBT", [D, D])
    wCT = din("wCT", [D, D])
    woT = din("woT", [D, D])
    b_in_ppI = din("b_in_ppI", [128, 16])
    b_d_ppI = din("b_d_ppI", [128, 8])
    b_B_ppI = din("b_B_ppI", [128, 8])
    b_C_ppI = din("b_C_ppI", [128, 8])
    b_o_ppI = din("b_o_ppI", [128, 8])
    b_cv_ppI = din("b_cv_ppI", [128, 8])
    wcv_ppI = din("wcv_ppI", [128, 32])
    g_ppI = din("g_ppI", [128, 8])
    be_ppI = din("be_ppI", [128, 8])
    w1T = din("w1T", [D, 4 * D])
    w2T = din("w2T", [4 * D, D])
    w1N = din("w1N", [4 * D, D])
    w2N = din("w2N", [D, 4 * D])
    b1_ppI = din("b1_ppI", [128, 32])
    b2_ppI = din("b2_ppI", [128, 8])
    lnlI = din("lnlI", [128, 4], BF16)
    ones_bfI = din("ones_bfI", [128, 1], BF16)
    id_bfI = din("id_bfI", [128, 128], BF16)
    constr = din("constr", [1, NCONST])
    halomskI = din("halomskI", [128, 8])
    halozI = din("halozI", [128, 1])
    hmqI = din("hmqI", [128, 8])
    hmpaI = din("hmpaI", [128, 8])

    out_c = nc.dram_tensor("out_c", [D, TOK], F32, kind="ExternalOutput")

    ag1_src = nc.dram_tensor("ag1_src", [98, TOK], F32)
    ag1_dst = nc.dram_tensor("ag1_dst", [R, 98, TOK], F32)
    ya_src = nc.dram_tensor("ya_src", [R, 128, TOK], BF16)
    ya_dst = nc.dram_tensor("ya_dst", [R, 128, TOK], BF16)
    x3t_src = nc.dram_tensor("x3t_src", [128, 24], F32)
    x3t_dst = nc.dram_tensor("x3t_dst", [R, 128, 24], F32)
    qp_src = nc.dram_tensor("qp_src", [128, 16], F32)
    qp_dst = nc.dram_tensor("qp_dst", [R, 128, 16], F32)
    browD = nc.dram_tensor("browD", [8, TOK], F32)
    sv_spill = nc.dram_tensor("sv_spill", [D, TOK], BF16)
    inp_spill = nc.dram_tensor("inp_spill", [D, TH], BF16)
    cm_spill = nc.dram_tensor("cm_spill", [D, TH], BF16)
    ec_spill = nc.dram_tensor("ec_spill", [D, TOK], BF16)
    bw1_dram = nc.dram_tensor("bw1_dram", [D, 4 * D], BF16)
    bw2_dram = nc.dram_tensor("bw2_dram", [4 * D, D], BF16)

    XS = [nc.alloc_sbuf_tensor(f"xs{k}", [128, TOK], F32) for k in range(DC)]
    AF = Arena(nc, "AF", 12800, F32)
    AB = Arena(nc, "AB", 34560, BF16)
    SM = Arena(nc, "SM", 2000, F32)
    ROWS = nc.alloc_sbuf_tensor("ROWS", [8, TOK], F32)
    id_sb = nc.alloc_sbuf_tensor("id_sb", [128, 128], BF16)
    lnl_sb = nc.alloc_sbuf_tensor("lnl_sb", [128, 4], BF16)
    ones_sb = nc.alloc_sbuf_tensor("ones_sb", [128, 1], BF16)

    ps_y = nc.alloc_psum_tensor("ps_y", [128, 2048], F32)
    psA = nc.alloc_psum_tensor("psA", [128, 1024], F32)
    psB = nc.alloc_psum_tensor("psB", [128, 1024], F32)
    ps0 = psA[:, 0:512]
    ps1 = psA[:, 512:1024]
    ps_s = psB[:, 0:512]
    ps_p = psB[:, 512:1024]

    p = Prog(nc)

    def psget(name):
        return p.psfree.get(name)

    def psset(name, tok):
        p.psfree[name] = tok

    cst = SM.take(NCONST)
    A_sb = SM.take(N16)
    iA_sb = SM.take(N16)
    dsk = SM.take(1)
    bdt2 = SM.take(1)
    carry0 = SM.take(N16)
    bin_pp = SM.take(16)
    binh_pp = SM.take(8)
    bd_pp = SM.take(8)
    nbd_pp = SM.take(8)
    bB_pp = SM.take(8)
    bC_pp = SM.take(8)
    bo_pp = SM.take(8)
    bcv_pp = SM.take(8)
    wcv = SM.take(32)
    g2_pp = SM.take(8)
    be2_pp = SM.take(8)
    b1p = SM.take(32)
    b2p = SM.take(8)
    a1_pp = SM.take(32)
    a2_pp = SM.take(8)
    ascr = SM.take(1)
    cL1 = SM.take(8)
    cE1 = SM.take(8)
    H_a = SM.take(8)
    H_b = SM.take(8)
    qpt = SM.take(16)
    s96n = SM.take(1)
    hmsk = SM.take(8)
    hz = SM.take(1)
    hmq = SM.take(8)
    hmpa = SM.take(8)

    p.dma("sync", cst[:, :], constr[0:1, :].partition_broadcast(128), slot="ini")
    p.dma("sync", A_sb[:, :], logAc[:, :], slot="ini")
    for (dst, src) in ((dsk, dskipc), (bdt2, b_dt2c), (bin_pp, b_in_ppI),
                       (bd_pp, b_d_ppI), (bB_pp, b_B_ppI), (bC_pp, b_C_ppI),
                       (bo_pp, b_o_ppI), (bcv_pp, b_cv_ppI), (wcv, wcv_ppI),
                       (g2_pp, g_ppI), (be2_pp, be_ppI), (b1p, b1_ppI),
                       (b2p, b2_ppI), (hmsk, halomskI), (hz, halozI),
                       (hmq, hmqI), (hmpa, hmpaI)):
        p.dma("sync", dst[:, :], src[:, :], slot="ini")
    p.dma("sync", id_sb[:, :], id_bfI[:, :], slot="ini")
    p.dma("sync", lnl_sb[:, :], lnlI[:, :], slot="ini")
    p.dma("sync", ones_sb[:, :], ones_bfI[:, :], slot="ini")
    tk_ini = p.drain("sync")
    tk_c = tk_la = tk_bp = tk_id = tk_lnl = tk_ones = tk_ini
    tk_A = p.s(lambda s: s.activation(A_sb[:, :], A_sb[:, :], ACT.Exp),
               wait=[tk_la, tk_c])
    tk_A = p.v(lambda v: v.tensor_scalar(A_sb[:, :], A_sb[:, :], -1.0, None,
                                         AL.mult), wait=[tk_A])
    tk_iA = p.v(lambda v: v.tensor_scalar(iA_sb[:, :], A_sb[:, :], 1e-8, None,
                                          AL.add), wait=[tk_A])
    tk_iA = p.v(lambda v: v.reciprocal(iA_sb[:, :], iA_sb[:, :]), wait=[tk_iA])
    tk_nbd = p.v(lambda v: v.tensor_scalar(nbd_pp[:, :], bd_pp[:, :], -1.0, None,
                                           AL.mult), wait=[tk_bp])
    tk_binh = p.v(lambda v: v.tensor_scalar(binh_pp[:, :], bin_pp[:, 0:8],
                                            hz[:, 0:1], None, AL.mult),
                  wait=[tk_bp])

    def mm_group(psname, psum, pairs, extra_free=None, start0=True,
                 stop_last=True):
        tok = None
        n = len(pairs)
        free = psget(psname)
        for i, (lh, rh, w) in enumerate(pairs):
            ww = list(w)
            if i == 0:
                ww += [free, extra_free]
            tok = p.t(lambda t, lh=lh, rh=rh, i=i, psum=psum, n=n: t.matmul(
                psum, lh, rh, start=(start0 and i == 0),
                stop=(stop_last and i == n - 1)), wait=ww)
        return tok

    def bcast(dst, dram_row_ap, slot, wait=()):
        return p.dma("gpsimd", dst, dram_row_ap.partition_broadcast(128),
                     wait=wait, slot=slot)

    def kload(eng, out2d, dram_ap, nchunk, slot, wait=()):
        in3 = dram_ap.rearrange("(c p) m -> p c m", p=128)
        out3 = out2d.rearrange("p (c m) -> p c m", c=nchunk)
        return p.dma(eng, out3, in3, wait=wait, slot=slot)

    def ln_stats(width=TOK):
        AB.push()
        xb = [AB.take(512) for _ in range(DC)]
        sq = [AB.take(512) for _ in range(DC)]
        evs = []
        xs_ready = p.cur("vv")
        prevg = {}
        for i in range(width // 512):
            tcp = []
            for k in range(DC):
                t1 = p.v(lambda v, k=k, i=i, xb=xb: v.tensor_copy(
                    xb[k][:, :], XS[k][:, i * 512:(i + 1) * 512]),
                    wait=[prevg.get(k)])
                t2 = p.s(lambda s, k=k, i=i, sq=sq: s.activation(
                    sq[k][:, :], XS[k][:, i * 512:(i + 1) * 512], ACT.Square),
                    wait=[xs_ready, prevg.get(k)])
                tcp.append((t1, t2))
            pairs = [(lnl_sb[:, 0:2], xb[k][:, :], [tcp[k][0], tk_lnl])
                     for k in range(DC)]
            pairs += [(lnl_sb[:, 2:4], sq[k][:, :], [tcp[k][1]])
                      for k in range(DC)]
            tok = mm_group("s", ps_s[0:2, :], pairs)
            ev = p.v(lambda v, i=i: v.tensor_copy(
                ROWS[0:2, i * 512:(i + 1) * 512], ps_s[0:2, :]), wait=[tok])
            psset("s", ev)
            evs.append(ev)
            for k in range(DC):
                prevg[k] = tok
        for e in ("vector", "scalar", "gpsimd", "sync"):
            p.pend[e] = p.pend[e] + [evs[-1]]
        AB.pop()
        return evs

    def finalize_reps(width, muR, m2R, scrF, rR, rcR=None):
        t1 = p.s(lambda s: s.activation(scrF, muR, ACT.Square))
        t2 = p.v(lambda v: v.tensor_tensor(m2R, m2R, scrF, AL.subtract),
                 wait=[t1])
        t3 = p.s(lambda s: s.activation(scrF, m2R, ACT.Ln,
                                        bias=cst[:, C_EPS:C_EPS + 1]), wait=[t2])
        t4 = p.s(lambda s: s.activation(rR, scrF, ACT.Exp,
                                        scale=cst[:, C_NEGH:C_NEGH + 1]),
                 wait=[t3])
        if rcR is None:
            return t4
        t5 = p.s(lambda s: s.activation(scrF, rR, ACT.Square), wait=[t4])
        t6 = p.v(lambda v: v.tensor_tensor(scrF, scrF, m2R, AL.mult), wait=[t5])
        t7 = p.s(lambda s: s.activation(scrF, scrF, ACT.Ln,
                                        bias=cst[:, C_EPS:C_EPS + 1]), wait=[t6])
        t8 = p.s(lambda s: s.activation(scrF, scrF, ACT.Exp,
                                        scale=cst[:, C_NEGH:C_NEGH + 1]),
                 wait=[t7])
        t9 = p.v(lambda v: v.tensor_tensor(rcR, rR, scrF, AL.mult), wait=[t8])
        return t9

    AB.push()
    wst = AB.take(8 * 96)
    tk_wst = kload("gpsimd", wst[:, :], wstackT.ap(), 8, "wst")
    for k in range(DC):
        p.dma("sync", XS[k][:, :], x_tok[k * 128:(k + 1) * 128, :],
              slot="xt")
    p.barrier()
    evs1 = ln_stats()
    tk_st1 = p.dma("sync", browD[0:2, :], ROWS[0:2, :], wait=evs1, slot="brw")
    s96pairs = [(wst[:, kc * 96:(kc + 1) * 96], ones_sb[:, :], [tk_wst, tk_ones])
                for kc in range(DC)]
    tk_s96 = mm_group("1", ps1[0:96, 0:1], s96pairs)
    tk_s96 = p.v(lambda v: v.tensor_scalar(s96n[0:96, :], ps1[0:96, 0:1], -1.0,
                                           None, AL.mult), wait=[tk_s96])
    psset("1", tk_s96)
    AF.push()
    mur96 = AF.take(TOK)
    rr96 = AF.take(TOK)
    scr96 = AF.take(TOK)
    tk_mu96 = bcast(mur96[:, :], browD[0:1, :], "mu", wait=[tk_st1])
    tk_m296 = bcast(rr96[:, :], browD[1:2, :], "rr", wait=[tk_st1])
    p.pend["vector"] += [tk_mu96, tk_m296]
    p.pend["scalar"] += [tk_mu96, tk_m296]
    tk_fin1 = finalize_reps(TOK, mur96[:, :], rr96[:, :], scr96[:, :],
                            rr96[:, :])
    tk_rr96 = tk_fin1
    xbz = [AB.take(TOK) for _ in range(DC)]
    tkxb = [p.v(lambda v, k=k: v.tensor_copy(xbz[k][:, :], XS[k][:, :]))
            for k in range(DC)]
    z96 = AF.take(TOK)
    tk_ag_in = []
    for i in range(TOK // 512):
        pairs = [(wst[:, kc * 96:(kc + 1) * 96],
                  xbz[kc][:, i * 512:(i + 1) * 512], [tkxb[kc], tk_wst])
                 for kc in range(DC)]
        tokm = mm_group("0", ps0[0:96, :], pairs)
        tf = p.v(lambda v, i=i: v.scalar_tensor_tensor(
            z96[0:96, i * 512:(i + 1) * 512], mur96[0:96, i * 512:(i + 1) * 512],
            s96n[0:96, 0:1], ps0[0:96, :], AL.mult, AL.add),
            wait=[tokm, tk_mu96, tk_fin1, tk_s96])
        psset("0", tf)
        evp = p.v(lambda v, i=i: v.tensor_tensor(
            z96[0:96, i * 512:(i + 1) * 512], z96[0:96, i * 512:(i + 1) * 512],
            rr96[0:96, i * 512:(i + 1) * 512], AL.mult), wait=[tf, tk_rr96])
        tk_ag_in.append(evp)
    p.dma("gpsimd", ag1_src[0:96, :], z96[0:96, :], wait=tk_ag_in, slot="agf")
    p.dma("gpsimd", ag1_src[96:97, :], mur96[0:1, :],
          wait=[tk_fin1] + tk_ag_in, slot="agf")
    p.dma("gpsimd", ag1_src[97:98, :], rr96[0:1, :],
          wait=[tk_fin1] + tk_ag_in, slot="agf")
    tk_ag1 = p.coll("AllGather", [ag1_src.ap()], [ag1_dst.ap()],
                    wait=[p.drain("gpsimd")])
    AF.pop()
    AB.pop()

    p.barrier()
    AF.push()
    AB.push()
    wdt2_sb = AB.take(128)
    tk_wdt2 = p.dma("gpsimd", wdt2_sb[0:64, :], wdt2T[:, :], slot="wd2")
    xd = AF.take(2048)
    dtt = AF.take(2048)
    xnf = AF.take(2048)
    aT = [AF.take(2048), AF.take(2048)]
    bscan = AF.take(2048)
    murB = AB.take(2048)
    rrB = AB.take(2048)
    zdtB = AB.take(2048)
    xnB = AB.take(2048)
    browB = [AB.take(2048), AB.take(2048)]
    crowB = [AB.take(2048), AB.take(2048)]
    brepB = [AB.take(2048), AB.take(2048)]
    crepB = [AB.take(2048), AB.take(2048)]
    onesr = AB.take(128)
    hB = AB.take(2048)
    pB = [AB.take(2048), AB.take(2048)]
    wtmpP = AB.take(1024)
    sgnP = AB.take(1024)
    tk_onesr = p.v(lambda v: v.memset(onesr[:, :], 1.0))

    P0B = []
    for kc in range(DC):
        for q in range(4):
            P0B.append(("s", w1T[kc * 128:(kc + 1) * 128,
                                 q * 1024:(q + 1) * 1024],
                        bw1_dram[kc * 128:(kc + 1) * 128,
                                 q * 1024:(q + 1) * 1024]))
            P0B.append(("a1", w1N[(kc * 4 + q) * 128:(kc * 4 + q + 1) * 128,
                                  :], kc * 4 + q))
    for kc2 in range(32):
        P0B.append(("s", w2T[kc2 * 128:(kc2 + 1) * 128, :],
                    bw2_dram[kc2 * 128:(kc2 + 1) * 128, :]))
        kco, q = kc2 // 4, kc2 % 4
        P0B.append(("a2", w2N[kco * 128:(kco + 1) * 128,
                              q * 1024:(q + 1) * 1024], kco))
    tk_az2 = p.v(lambda v: v.memset(a2_pp[:, :], 0.0))
    pz_cons = {}
    pz_load = {}

    def p0b_load(k):
        src = P0B[k][1]
        pz_load[k] = p.dma("gpsimd", wtmpP[:, :], src,
                           wait=[pz_cons.get(k - 1)], slot="wbl")

    def p0b_consume(k):
        kind = P0B[k][0]
        if kind == "s":
            dst = P0B[k][2]
            tsg = p.s(lambda s: s.activation(sgnP[:, :], wtmpP[:, :],
                                             ACT.Sign),
                      wait=[pz_load[k], pz_state["store"]])
            pz_state["store"] = p.dma("gpsimd", dst, sgnP[:, :],
                                      wait=[tsg, pz_state["store"]],
                                      slot="wst1")
            pz_cons[k] = tsg
        elif kind == "a1":
            col = P0B[k][2]
            trd = p.v(lambda v, col=col: v.tensor_reduce(
                a1_pp[:, col:col + 1], wtmpP[:, :], AXL.X, AL.add,
                apply_absolute_value=True), wait=[pz_load[k]])
            pz_cons[k] = trd
            pz_state["a1last"] = trd
        else:
            col = P0B[k][2]
            trd = p.v(lambda v: v.tensor_reduce(
                ascr[:, 0:1], wtmpP[:, :], AXL.X, AL.add,
                apply_absolute_value=True),
                wait=[pz_load[k], pz_state["a2chain"]])
            tad = p.v(lambda v, col=col: v.tensor_tensor(
                a2_pp[:, col:col + 1], a2_pp[:, col:col + 1], ascr[:, 0:1],
                AL.add), wait=[trd, pz_state["a2chain"]])
            pz_state["a2chain"] = tad
            pz_cons[k] = trd
            pz_state["a2last"] = tad
    pz_state = {"store": None, "a2chain": tk_az2, "a1last": None,
                "a2last": None}

    tk_yevac = tk_y_dma = None
    tk_dt = None
    tk_xncons = None
    tk_pcons = {}
    tk_pcons2 = {}
    tk_wcons = {}
    tk_evb = {}
    tk_evc = {}
    tk_acons = {}
    tk_brep = {}
    tk_crep = {}
    y_dma_toks = []
    for tc in range(R):
        tk_xd = p.dma("sync", xd[:, :], x_dsh[:, tc * TOK:(tc + 1) * TOK],
                      wait=[tk_y_dma], slot="xd")
        tk_mu = bcast(murB[:, :], ag1_dst[tc, 96:97, :], "mu",
                      wait=[tk_ag1, tk_xncons])
        tk_rr = bcast(rrB[:, :], ag1_dst[tc, 97:98, :], "rr",
                      wait=[tk_ag1, tk_xncons])
        tk_zdt = p.dma("gpsimd", zdtB[0:64, :], ag1_dst[tc, 0:64, :],
                       wait=[tk_ag1, p.cur("tt")], slot="zd")
        t1 = p.v(lambda v: v.tensor_tensor(xnf[:, :], xd[:, :], murB[:, :],
                                           AL.subtract),
                 wait=[tk_xd, tk_mu, tk_yevac])
        tk_xn = p.v(lambda v: v.tensor_tensor(xnf[:, :], xnf[:, :], rrB[:, :],
                                              AL.mult), wait=[t1, tk_rr])
        tk_xnb = p.v(lambda v: v.tensor_copy(xnB[:, :], xnf[:, :]), wait=[tk_xn])
        tk_xncons = tk_xnb
        for i in range(4):
            nmp = "0" if i % 2 == 0 else "1"
            psd = ps0 if i % 2 == 0 else ps1
            tmm = mm_group(nmp, psd[:, :],
                           [(wdt2_sb[0:64, :], zdtB[0:64, i * 512:(i + 1) * 512],
                             [tk_zdt, tk_wdt2])])
            ta1 = p.s(lambda s, psd=psd: s.activation(
                aT[0][:, 0:512], psd[:, :], ACT.Abs, bias=bdt2[:, 0:1]),
                wait=[tmm, tk_dt] + ([tk_acons.get(0), tk_acons.get(1)]
                                     if i == 0 else []))
            ta2 = p.s(lambda s: s.activation(
                aT[0][:, 512:1024], aT[0][:, 0:512], ACT.Exp,
                scale=cst[:, C_NEG1:C_NEG1 + 1]), wait=[ta1])
            ta3 = p.s(lambda s: s.activation(
                aT[0][:, 0:512], aT[0][:, 512:1024], ACT.Ln,
                bias=cst[:, C_ONE:C_ONE + 1]), wait=[ta2])
            ta4 = p.s(lambda s, psd=psd: s.activation(
                aT[0][:, 512:1024], psd[:, :], ACT.Relu, bias=bdt2[:, 0:1]),
                wait=[ta3])
            psset(nmp, ta4)
            tk_dt = p.v(lambda v, i=i: v.tensor_tensor(
                dtt[:, i * 512:(i + 1) * 512], aT[0][:, 0:512],
                aT[0][:, 512:1024], AL.add), wait=[ta4])
        tk_id_free = tk_yevac
        def rep_rows(n):
            sl = n % 2
            tk_brep[sl] = p.dma("gpsimd", browB[sl][0:1, :],
                                ag1_dst[tc, 64 + n:65 + n, :],
                                wait=[tk_brep.get(sl + 2), tk_ag1],
                                slot=f"br{sl}")
            tk_crep[sl] = p.dma("gpsimd", crowB[sl][0:1, :],
                                ag1_dst[tc, 80 + n:81 + n, :],
                                wait=[tk_crep.get(sl + 2), tk_ag1],
                                slot=f"cr{sl}")

        def rep_produce(n):
            sl = n % 2
            evb, evc = [], []
            mmb = mmc = None
            mmq = {}
            for i in range(4):
                bname = ["0", "1", "s", "p"][i]
                psd = [ps0, ps1, ps_s, ps_p][i]
                mmb = p.t(lambda t, sl=sl, i=i, psd=psd: t.matmul(
                    psd[:, :], onesr[0:1, :],
                    browB[sl][0:1, i * 512:(i + 1) * 512],
                    start=True, stop=True),
                    wait=[tk_brep[sl], tk_onesr, psget(bname)])
                mmq[i] = mmb
            for half, pst in ((0, psA), (1, psB)):
                ev = p.s(lambda s, sl=sl, half=half, pst=pst, n=n:
                         s.activation(
                    brepB[sl][:, half * 1024:(half + 1) * 1024], pst[:, :],
                    ACT.Identity, scale=iA_sb[:, n:n + 1]),
                    wait=[mmq[2 * half], mmq[2 * half + 1], tk_iA,
                          tk_wcons.get(sl)])
                psset(["0", "1", "s", "p"][2 * half], ev)
                psset(["0", "1", "s", "p"][2 * half + 1], ev)
                evb.append(ev)
            tk_brep[sl + 2] = mmb
            for i in range(4):
                cname = ["0", "1", "s", "p"][i]
                psd = [ps0, ps1, ps_s, ps_p][i]
                mmc = p.t(lambda t, sl=sl, i=i, psd=psd: t.matmul(
                    psd[:, :], onesr[0:1, :],
                    crowB[sl][0:1, i * 512:(i + 1) * 512],
                    start=True, stop=True),
                    wait=[tk_crep[sl], tk_onesr, psget(cname)])
                mmq[i] = mmc
            for half, pst in ((0, psA), (1, psB)):
                ev = p.s(lambda s, sl=sl, half=half, pst=pst: s.activation(
                    crepB[sl][:, half * 1024:(half + 1) * 1024], pst[:, :],
                    ACT.Identity),
                    wait=[mmq[2 * half], mmq[2 * half + 1],
                          tk_pcons2.get(sl)])
                psset(["0", "1", "s", "p"][2 * half], ev)
                psset(["0", "1", "s", "p"][2 * half + 1], ev)
                evc.append(ev)
            tk_crep[sl + 2] = mmc
            tk_evb[sl] = evb
            tk_evc[sl] = evc

        rep_rows(0)
        rep_rows(1)
        rep_produce(0)
        for n in range(N16):
            sl = n % 2
            ta = p.s(lambda s, sl=sl, n=n: s.activation(
                aT[sl][:, :], dtt[:, :], ACT.Exp, scale=A_sb[:, n:n + 1]),
                wait=[tk_dt, tk_A, tk_acons.get(sl)])
            if n + 2 < N16:
                rep_rows(n + 2)
            if n + 1 < N16:
                rep_produce(n + 1)
            p0b_load(tc * 16 + n)
            tw = p.v(lambda v, sl=sl: v.tensor_tensor(
                pB[sl][:, :], xnB[:, :], brepB[sl][:, :], AL.mult),
                wait=[tk_xnb, tk_pcons.get(sl)] + tk_evb[sl])
            tk_wcons[sl] = tw
            tb = p.v(lambda v, sl=sl: v.scalar_tensor_tensor(
                bscan[:, :], aT[sl][:, :], 1.0, pB[sl][:, :], AL.subtract,
                AL.mult), wait=[ta, tw])
            if tc % 4 == 0:
                th = p.v(lambda v, sl=sl: v.tensor_tensor_scan(
                    hB[:, :], aT[sl][:, :], bscan[:, :], 0.0, AL.mult, AL.add),
                    wait=[tb])
            else:
                th = p.v(lambda v, sl=sl, n=n: v.tensor_tensor_scan(
                    hB[:, :], aT[sl][:, :], bscan[:, :], carry0[:, n:n + 1],
                    AL.mult, AL.add), wait=[tb])
            tk_acons[sl] = th
            p.v(lambda v, n=n: v.tensor_copy(carry0[:, n:n + 1],
                                             hB[:, 2047:2048]), wait=[th])
            tp = p.v(lambda v, sl=sl: v.tensor_tensor(
                pB[sl][:, :], hB[:, :], crepB[sl][:, :], AL.mult),
                wait=[th] + tk_evc[sl])
            tk_pcons2[sl] = tp
            last = None
            for i in range(4):
                ww = [tp, tk_id] + ([tk_id_free] if (n == 0 and i == 0) else [])
                last = p.t(lambda t, sl=sl, i=i, n=n: t.matmul(
                    ps_y[:, i * 512:(i + 1) * 512], id_sb[:, :],
                    pB[sl][:, i * 512:(i + 1) * 512], start=(n == 0),
                    stop=(n == N16 - 1)), wait=ww if i == 0 else [])
            tk_pcons[sl] = last
            p0b_consume(tc * 16 + n)
        tk_yevac = p.v(lambda v: v.scalar_tensor_tensor(
            xd[:, :], xnf[:, :], dsk[:, 0:1], ps_y[:, :], AL.mult, AL.add),
            wait=[tk_pcons[0], tk_pcons[1], tk_xn])
        tk_y_dma = p.dma("gpsimd", ya_src[tc, :, :], xd[:, :],
                         wait=[tk_yevac], slot="yst")
        y_dma_toks.append(tk_y_dma)
    psset("y", tk_yevac)
    tk_a1 = p.v(lambda v: v.tensor_scalar(a1_pp[:, :], a1_pp[:, :], 1.0 / D,
                                          None, AL.mult),
                wait=[pz_state["a1last"]])
    tk_a2 = p.v(lambda v: v.tensor_scalar(a2_pp[:, :], a2_pp[:, :],
                                          1.0 / (4 * D), None, AL.mult),
                wait=[pz_state["a2last"]])
    tk_bw_st = p.drain("gpsimd")
    tk_a2a = p.coll("AllToAll", [ya_src.ap()], [ya_dst.ap()],
                    wait=[tk_bw_st])
    AF.pop()
    AB.pop()

    p.barrier()
    AF.push()
    AB.push()
    ytl = [AB.take(2048), AB.take(2048)]
    tk_x2 = []
    ty_prev = {}
    for k in range(DC):
        sl = k % 2
        tyl = p.dma("sync", ytl[sl][:, :], ya_dst[k, :, :],
                    wait=[tk_a2a, ty_prev.get(sl)], slot=f"yt{sl}")
        tadd = p.v(lambda v, k=k, sl=sl: v.tensor_tensor(
            XS[k][:, :], XS[k][:, :], ytl[sl][:, :], AL.add), wait=[tyl])
        ty_prev[sl] = tadd
        tk_x2.append(tadd)
    if stage == 1:
        p.barrier()
        for k in range(DC):
            p.dma("sync", out_c[k * 128:(k + 1) * 128, :], XS[k][:, :],
                  slot="out")
        p.g(lambda g: g.nop(), wait=[p.drain("sync")])
        return nc, p

    evs2 = ln_stats()
    tk_bw2 = p.dma("sync", browD[0:2, :], ROWS[0:2, :], wait=evs2, slot="brw")
    murF = AF.take(2048)
    rrF = AF.take(2048)
    tmpF = AF.take(2048)
    tk_mu = bcast(murF[:, :], browD[0:1, :], "mu", wait=[tk_bw2])
    tk_m2 = bcast(rrF[:, :], browD[1:2, :], "rr", wait=[tk_bw2])
    p.pend["vector"] += [tk_mu, tk_m2]
    p.pend["scalar"] += [tk_mu, tk_m2]
    tk_rr = finalize_reps(TOK, murF[:, :], rrF[:, :], tmpF[:, :], rrF[:, :])
    xn2B = [AB.take(2048) for _ in range(DC)]
    tk_xn2 = []
    for k in range(DC):
        t1 = p.v(lambda v, k=k: v.tensor_tensor(tmpF[:, :], XS[k][:, :],
                                                murF[:, :], AL.subtract),
                 wait=[tk_mu, tk_rr, tk_x2[k]] + ([tk_xn2[-1]] if tk_xn2 else []))
        t2 = p.v(lambda v, k=k: v.tensor_tensor(xn2B[k][:, :], tmpF[:, :],
                                                rrF[:, :], AL.mult),
                 wait=[t1, tk_rr])
        tk_xn2.append(t2)
    wfB = AB.take(8192)
    tk_wf = kload("gpsimd", wfB[:, :], l0w_T.ap(), 8, "wf")
    diB = AB.take(1024)
    tk_di = p.dma("gpsimd", diB[0:64, :], dishaI[:, :], slot="di")
    ppB = AB.take(64)
    tk_ppat = p.dma("sync", ppB[:, 0:64], pool_pat[:, :], slot="pp")
    poolB = AB.take(512)
    for i in range(4):
        ppairs = [(ppB[:, 0:64], xn2B[kc][:, i * 512:(i + 1) * 512],
                   [tk_xn2[kc], tk_ppat]) for kc in range(DC)]
        tkpool = mm_group("p", ps_p[0:64, :], ppairs)
        tkpev = p.v(lambda v: v.tensor_copy(poolB[0:64, :], ps_p[0:64, :]),
                    wait=[tkpool, p.cur("tt")])
        psset("p", tkpev)
        for m in range(DC):
            nmp = "0" if (i * 8 + m) % 2 == 0 else "1"
            psd = ps0 if (i * 8 + m) % 2 == 0 else ps1
            pairs = [(wfB[:, kc * 1024 + m * 128:kc * 1024 + (m + 1) * 128],
                      xn2B[kc][:, i * 512:(i + 1) * 512], [tk_xn2[kc], tk_wf])
                     for kc in range(DC)]
            pairs.append((diB[0:64, m * 128:(m + 1) * 128], poolB[0:64, :],
                          [tkpev, tk_di]))
            tkm = mm_group(nmp, psd[:, :], pairs)
            tke = p.v(lambda v, m=m, i=i, psd=psd: v.tensor_tensor(
                XS[m][:, i * 512:(i + 1) * 512], XS[m][:, i * 512:(i + 1) * 512],
                psd[:, :], AL.add), wait=[tkm])
            psset(nmp, tke)
    tk_x3 = [psget("0"), psget("1")]
    for k in range(DC):
        p.dma("gpsimd", x3t_src[:, k * 3:(k + 1) * 3],
              XS[k][:, 2045:2048], wait=tk_x3, slot="x3t")
    tk_agx3 = p.coll("AllGather", [x3t_src.ap()], [x3t_dst.ap()],
                     wait=[p.drain("gpsimd")])
    AF.pop()
    AB.pop()
    if stage == 2:
        p.barrier()
        for k in range(DC):
            p.dma("sync", out_c[k * 128:(k + 1) * 128, :], XS[k][:, :],
                  slot="out")
        p.g(lambda g: g.nop(), wait=[p.drain("sync")])
        return nc, p

    p.barrier()
    AF.push()
    AB.push()
    evs3 = ln_stats()
    tk_bw3 = p.dma("sync", browD[0:2, :], ROWS[0:2, :], wait=evs3, slot="brw")
    AF.push()
    muR3 = AF.take(2048)
    m2R3 = AF.take(2048)
    scR3 = AF.take(2048)
    rR3 = AF.take(2048)
    rcR3 = AF.take(2048)
    tkmu3 = bcast(muR3[:, :], browD[0:1, :], "mu", wait=[tk_bw3])
    tkm23 = bcast(m2R3[:, :], browD[1:2, :], "rr", wait=[tk_bw3])
    p.pend["vector"] += [tkmu3, tkm23]
    p.pend["scalar"] += [tkmu3, tkm23]
    tk_fin3 = finalize_reps(TOK, muR3[:, :], m2R3[:, :], scR3[:, :],
                            rR3[:, :], rcR3[:, :])
    tk_bw3a = tk_bw3
    tk_bw3b = p.dma("sync", browD[1:2, :], rR3[0:1, :],
                    wait=[tk_fin3, tkm23], slot="brw")
    tk_bw3c = p.dma("sync", browD[2:3, :], rcR3[0:1, :],
                    wait=[tk_fin3, tk_bw3b], slot="brw")
    AF.pop()

    xpv = AF.take(28)
    AB.push()
    xallB = AF.take(DC * 24 // 4 * 4) if False else None
    xall = AF.take(192)
    tk_xall = p.dma("sync", xall[:, :].rearrange("p (r c) -> p r c", r=8),
                    x3t_dst.ap().rearrange("r p c -> p r c"),
                    wait=[tk_agx3, tk_fin3, tk_bw3b, tk_bw3c], slot="xal")
    tk_sel = p.v(lambda v: v.tensor_scalar(xpv[:, 0:24], xall[:, 0:24],
                                           hmsk[:, 0:1], None, AL.mult),
                 wait=[tk_xall, tk_bp])
    for j in range(1, R):
        tk_sel = p.v(lambda v, j=j: v.scalar_tensor_tensor(
            xpv[:, 0:24], xall[:, j * 24:(j + 1) * 24], hmsk[:, j:j + 1],
            xpv[:, 0:24], AL.mult, AL.add), wait=[tk_sel])
    hghB = []
    tk_hgh = []
    hpb = [AB.take(4) for _ in range(DC)]
    hps = [AB.take(4) for _ in range(DC)]
    tcst = []
    for k in range(DC):
        t1 = p.v(lambda v, k=k: v.tensor_copy(hpb[k][:, 0:3],
                                              xpv[:, k * 3:k * 3 + 3]),
                 wait=[tk_sel])
        t2 = p.s(lambda s, k=k: s.activation(hps[k][:, 0:3],
                                             xpv[:, k * 3:k * 3 + 3],
                                             ACT.Square), wait=[tk_sel])
        tcst.append((t1, t2))
    pairs = [(lnl_sb[:, 0:2], hpb[k][:, 0:3], [tcst[k][0], tk_lnl])
             for k in range(DC)]
    pairs += [(lnl_sb[:, 2:4], hps[k][:, 0:3], [tcst[k][1]]) for k in range(DC)]
    tkh = mm_group("s", ps_s[0:2, 0:3], pairs)
    tkhe = p.v(lambda v: v.tensor_copy(ROWS[0:2, 0:3], ps_s[0:2, 0:3]),
               wait=[tkh])
    psset("s", tkhe)
    tkhd = p.dma("sync", browD[3:5, 0:3], ROWS[0:2, 0:3],
                 wait=[tkhe, tk_bw3c], slot="brw")
    mur3h = AF.take(4)
    rcr3h = AF.take(4)
    tmp3 = AF.take(4)
    scr3h = AF.take(4)
    r3h = AF.take(4)
    tkb3 = bcast(mur3h[:, 0:3], browD[3:4, 0:3], "m3h", wait=[tkhd])
    tkb4 = bcast(rcr3h[:, 0:3], browD[4:5, 0:3], "r3h", wait=[tkhd])
    p.pend["vector"] += [tkb3, tkb4]
    p.pend["scalar"] += [tkb3, tkb4]
    tkb4 = finalize_reps(3, mur3h[:, 0:3], rcr3h[:, 0:3], scr3h[:, 0:3],
                         r3h[:, 0:3], rcr3h[:, 0:3])
    for k in range(DC):
        hg = AB.take(4)
        hghB.append(hg)
        t1 = p.v(lambda v, k=k: v.tensor_tensor(
            tmp3[:, 0:3], xpv[:, k * 3:k * 3 + 3], mur3h[:, 0:3], AL.subtract),
            wait=[tkb3] + ([tk_hgh[-1]] if tk_hgh else []))
        t2 = p.v(lambda v: v.tensor_tensor(tmp3[:, 0:3], tmp3[:, 0:3],
                                           rcr3h[:, 0:3], AL.mult),
                 wait=[t1, tkb4])
        t3 = p.v(lambda v, k=k, hg=hg: v.tensor_scalar(
            hg[:, 0:3], tmp3[:, 0:3], g2_pp[:, k:k + 1], be2_pp[:, k:k + 1],
            AL.mult, AL.add), wait=[t2])
        tk_hgh.append(t3)

    wgtB = AB.take(8192)
    AB.push()
    p2fB = [AB.take(TH), AB.take(TH)]
    zeroB = AB.take(TH)
    ecdB = [AB.take(TH), AB.take(TH)]
    spillB = [AB.take(512), AB.take(512)]
    cmldB = [AB.take(TH), AB.take(TH)]
    tk_zero = p.v(lambda v: v.memset(zeroB[:, :], 0.0))

    tk_sv_dma = {}
    tk_y2loc = {}
    tk_p2mm = {}
    tk_spill_inp = {}
    tk_spill_cm = {}
    tk_inpst = {}
    tk_cmst = {}
    tk_ecs = {}
    tk_hcons = None
    tk_econs = None
    tk_conv = {}
    tk_wlast = None
    tk_win = {}
    tk_hg_prev_g = None
    tk_te_h = {}
    for half in range(2):
        toff = half * TH
        AB.push()
        ueB = [AB.take(TH + 3) for _ in range(DC)]
        AB.push()
        winB = [AB.take(1024), AB.take(1024)]
        hgB = [AB.take(512) for _ in range(DC)]
        svB = [AB.take(512), AB.take(512)]
        AF.push()
        mur3 = AF.take(TH)
        r3r = AF.take(TH)
        rcr = AF.take(TH)
        t1F = AF.take(512)
        if half == 0:
            hguards = [p.cur("vv")]
        else:
            hguards = [tk_te_h[(0, 0)], tk_te_h[(0, 1)], tk_te_h[(0, 2)]]
        tkm3 = bcast(mur3[:, :], browD[0:1, toff:toff + TH], "mu",
                     wait=[tk_bw3a] + hguards)
        tkr3 = bcast(r3r[:, :], browD[1:2, toff:toff + TH], "rr",
                     wait=[tk_bw3b] + hguards)
        tkrc = bcast(rcr[:, :], browD[2:3, toff:toff + TH], "rc",
                     wait=[tk_bw3c] + hguards)
        tk_uev = {}
        tk_ue_halo = [None] * DC
        if half == 1:
            for m in range(DC):
                tk_ue_halo[m] = p.v(lambda v, m=m: v.tensor_copy(
                    ueB[m][:, 0:3], ueB[m][:, TH:TH + 3]),
                    wait=[tk_conv[(0, m)]])
        for i in range(2):
            isl = slice(toff + i * 512, toff + (i + 1) * 512)
            hg_guard = p.cur("tt")
            tk_hg = {}
            tk_hg_prev = None
            for k in range(DC):
                tt1 = p.v(lambda v, k=k, isl=isl, i=i: v.tensor_tensor(
                    t1F[:, :], XS[k][:, isl], mur3[:, i * 512:(i + 1) * 512],
                    AL.subtract), wait=[tkm3, tk_hg_prev, hg_guard])
                tt2 = p.v(lambda v, k=k, i=i: v.tensor_tensor(
                    hgB[k][:, :], t1F[:, :], rcr[:, i * 512:(i + 1) * 512],
                    AL.mult), wait=[tt1, tkrc])
                tt3 = p.v(lambda v, k=k: v.tensor_scalar(
                    hgB[k][:, :], hgB[k][:, :], g2_pp[:, k:k + 1],
                    be2_pp[:, k:k + 1], AL.mult, AL.add), wait=[tt2])
                tt4 = p.v(lambda v, k=k, i=i: v.tensor_tensor(
                    t1F[:, :], t1F[:, :], r3r[:, i * 512:(i + 1) * 512],
                    AL.mult), wait=[tt3, tkr3])
                tt5 = p.v(lambda v, k=k, isl=isl: v.tensor_tensor(
                    XS[k][:, isl], XS[k][:, isl], t1F[:, :], AL.add),
                    wait=[tt4])
                tk_hg_prev = tt5
                tk_hg_prev_g = tt5
                tk_hg[k] = tt3
            for uv in range(2):
                for m in range(DC):
                    sl = m % 2
                    tk_win[sl] = kload(
                        "gpsimd", winB[sl][:, :],
                        w_inT.ap()[:, uv * 1024 + m * 128:uv * 1024 + (m + 1) * 128],
                        8, f"wi{sl}", wait=[tk_win.get(sl + 2)])
                    nmp = "0" if (uv * 8 + m) % 2 == 0 else "1"
                    psd = ps0 if (uv * 8 + m) % 2 == 0 else ps1
                    pairs = [(winB[sl][:, kc * 128:(kc + 1) * 128],
                              hgB[kc][:, :], [tk_hg[kc], tk_win[sl]])
                             for kc in range(DC)]
                    tkm = mm_group(nmp, psd[:, :], pairs)
                    tk_win[sl + 2] = tkm
                    if uv == 0:
                        tev = p.s(lambda s, m=m, i=i, psd=psd: s.activation(
                            ueB[m][:, 3 + i * 512:3 + (i + 1) * 512],
                            psd[:, :], ACT.Identity, bias=bin_pp[:, m:m + 1]),
                            wait=[tkm, tk_ue_halo[m]])
                        tk_uev[(i, m)] = tev
                        psset(nmp, tev)
                    else:
                        ssl = m % 2
                        tsv = p.s(lambda s, ssl=ssl, psd=psd, m=m: s.activation(
                            svB[ssl][:, :], psd[:, :], ACT.Sigmoid,
                            bias=bin_pp[:, 8 + m:9 + m]),
                            wait=[tkm, tk_sv_dma.get(ssl)])
                        psset(nmp, tsv)
                        tk_sv_dma[ssl] = p.dma(
                            "sync",
                            sv_spill[m * 128:(m + 1) * 128,
                                     toff + i * 512:toff + (i + 1) * 512],
                            svB[ssl][:, :], wait=[tsv], slot=f"sv{ssl}")
        if half == 0:
            for m in range(DC):
                tkw = kload("gpsimd", winB[0][:, :],
                            w_inT.ap()[:, m * 128:(m + 1) * 128], 8, "wi0",
                            wait=[p.cur("tt")])
                pairs = [(winB[0][:, kc * 128:(kc + 1) * 128],
                          hghB[kc][:, 0:3], [tk_hgh[kc], tkw])
                         for kc in range(DC)]
                tkm = mm_group("0", ps0[:, 0:3], pairs)
                tev = p.s(lambda s, m=m: s.activation(
                    ueB[m][:, 0:3], ps0[:, 0:3], ACT.Identity,
                    scale=hz[:, 0:1], bias=binh_pp[:, m:m + 1]),
                    wait=[tkm, tk_binh, tk_uev[(0, m)], tk_uev[(1, m)]])
                psset("0", tev)
                tk_ue_halo[m] = tev
        AF.pop()
        AB.pop()
        AB.push()
        ucB = [AB.take(TH) for _ in range(DC)]
        AF.push()
        convF = [AF.take(TH), AF.take(TH)]
        for m in range(DC):
            tc0 = p.v(lambda v, m=m: v.tensor_scalar(
                convF[0][:, :], ueB[m][:, 3:3 + TH],
                wcv[:, m * 4 + 3:m * 4 + 4], bcv_pp[:, m:m + 1],
                AL.mult, AL.add),
                wait=[tk_uev[(0, m)], tk_uev[(1, m)], tk_ue_halo[m],
                      tk_conv.get((half - 1, m)), tk_wlast])
            src, dst = 0, 1
            for j in range(3):
                tc0 = p.v(lambda v, m=m, j=j, src=src, dst=dst:
                          v.scalar_tensor_tensor(
                              convF[dst][:, :], ueB[m][:, j:j + TH],
                              wcv[:, m * 4 + j:m * 4 + j + 1],
                              convF[src][:, :], AL.mult, AL.add), wait=[tc0])
                src, dst = dst, src
            tk_conv[(half, m)] = p.v(lambda v, m=m, src=src: v.tensor_copy(
                ucB[m][:, :], convF[src][:, :]), wait=[tc0, p.cur("tt")])
        AF.pop()
        AF.push()
        decayF = [AF.take(TH) for _ in range(DC)]
        hF = AF.take(TH)
        eF = AF.take(TH)
        inldF = [AF.take(TH), AF.take(TH)]
        specs = [("d", wdT), ("B", wBT), ("C", wCT)]
        tk_sp = {}
        for (nm, wt) in specs:
            tkwg = kload("gpsimd", wgtB[:, :], wt.ap(), 8, "wg",
                         wait=[tk_wlast, p.cur("tt")])
            for m in range(DC):
                for i in range(2):
                    nmp = "0" if (m * 2 + i) % 2 == 0 else "1"
                    psd = ps0 if (m * 2 + i) % 2 == 0 else ps1
                    pairs = [(wgtB[:, kc * 1024 + m * 128:kc * 1024 + (m + 1) * 128],
                              ucB[kc][:, i * 512:(i + 1) * 512],
                              [tk_conv[(half, kc)], tkwg]) for kc in range(DC)]
                    tkm = mm_group(nmp, psd[:, :], pairs)
                    if nm == "d":
                        tev = p.s(lambda s, m=m, i=i, psd=psd: s.activation(
                            decayF[m][:, i * 512:(i + 1) * 512], psd[:, :],
                            ACT.Sigmoid, scale=cst[:, C_NEG1:C_NEG1 + 1],
                            bias=nbd_pp[:, m:m + 1]),
                            wait=[tkm, tk_nbd, tk_te_h.get((half - 1, m))])
                    elif nm == "B":
                        sl = (m * 2 + i) % 2
                        tev = p.v(lambda v, m=m, sl=sl, psd=psd, i=i:
                                  v.scalar_tensor_tensor(
                                      spillB[sl][:, 0:512], psd[:, :],
                                      bB_pp[:, m:m + 1],
                                      ucB[m][:, i * 512:(i + 1) * 512],
                                      AL.add, AL.mult),
                                  wait=[tkm, tk_spill_inp.get(sl)])
                        tk_spill_inp[sl] = p.dma(
                            "sync", inp_spill[m * 128:(m + 1) * 128,
                                              i * 512:(i + 1) * 512],
                            spillB[sl][:, 0:512],
                            wait=[tev, tk_spill_inp.get(sl)], slot=f"si{sl}")
                        tk_inpst[(m, i)] = tk_spill_inp[sl]
                    else:
                        sl = (m * 2 + i) % 2
                        tev = p.s(lambda s, m=m, psd=psd, sl=sl: s.activation(
                            spillB[sl][:, 0:512], psd[:, :], ACT.Identity,
                            bias=bC_pp[:, m:m + 1]),
                            wait=[tkm, tk_spill_cm.get(sl)])
                        tk_spill_cm[sl] = p.dma(
                            "sync", cm_spill[m * 128:(m + 1) * 128,
                                             i * 512:(i + 1) * 512],
                            spillB[sl][:, 0:512],
                            wait=[tev, tk_spill_cm.get(sl)], slot=f"sc{sl}")
                        tk_cmst[(m, i)] = tk_spill_cm[sl]
                    psset(nmp, tev)
                    tk_sp[(nm, m, i)] = tev
            tk_wlast = tk_sp[(nm, 7, 1)]
        tk_cmld = {}
        tk_inld = {}
        for m in range(DC):
            sl = m % 2
            tk_inld[sl] = p.dma("gpsimd", inldF[sl][:, :],
                                inp_spill[m * 128:(m + 1) * 128, :],
                                wait=[tk_inpst[(m, 0)], tk_inpst[(m, 1)],
                                      tk_inld.get(sl + 2)], slot=f"ri{sl}")
            tk_cmld[sl] = p.dma("sync", cmldB[sl][:, :],
                                cm_spill[m * 128:(m + 1) * 128, :],
                                wait=[tk_cmst[(m, 0)], tk_cmst[(m, 1)],
                                      tk_cmld.get(sl + 2)], slot=f"cl{sl}")
            init_h = 0.0 if half == 0 else cL1[:, m:m + 1]
            init_e = 1.0 if half == 0 else cE1[:, m:m + 1]
            th = p.v(lambda v, m=m, sl=sl, init_h=init_h: v.tensor_tensor_scan(
                hF[:, :], decayF[m][:, :], inldF[sl][:, :], init_h,
                AL.mult, AL.add),
                wait=[tk_sp[("d", m, 0)], tk_sp[("d", m, 1)], tk_inld[sl],
                      tk_hcons])
            te = p.v(lambda v, m=m, init_e=init_e: v.tensor_tensor_scan(
                eF[:, :], decayF[m][:, :], zeroB[:, :], init_e,
                AL.mult, AL.add), wait=[tk_zero, tk_econs, th])
            p.v(lambda v, m=m: v.tensor_copy(cL1[:, m:m + 1], hF[:, TH - 1:TH]),
                wait=[th])
            p.v(lambda v, m=m: v.tensor_copy(cE1[:, m:m + 1], eF[:, TH - 1:TH]),
                wait=[te])
            tk_te_h[(half, m)] = te
            tp2 = p.v(lambda v, sl=sl: v.tensor_tensor(
                p2fB[sl][:, :], hF[:, :], cmldB[sl][:, :], AL.mult),
                wait=[th, tk_cmld[sl]])
            tk_hcons = tp2
            tec = p.v(lambda v, m=m, sl=sl: v.tensor_tensor(
                ecdB[sl][:, :], eF[:, :], cmldB[sl][:, :], AL.mult),
                wait=[te, tk_cmld[sl], tk_ecs.get(sl)])
            tk_econs = tec
            tk_ecs[sl] = p.dma("sync",
                               ec_spill[m * 128:(m + 1) * 128, toff:toff + TH],
                               ecdB[sl][:, :], wait=[tec], slot=f"ec{sl}")
            tk_cmld[sl + 2] = tec
            tk_inld[sl + 2] = tp2
            for i in range(2):
                nmp = "s" if i == 0 else "p"
                pst = ps_s[0:1, :] if i == 0 else ps_p[0:1, :]
                ww = [tp2, tk_ones] + ([psget(nmp)] if m == 0 else [])
                tky = p.t(lambda t, sl=sl, i=i, m=m, pst=pst: t.matmul(
                    pst, ones_sb[:, :], p2fB[sl][:, i * 512:(i + 1) * 512],
                    start=(m == 0), stop=(m == DC - 1)), wait=ww)
                tk_p2mm[sl] = tky
        for i in range(2):
            nmp = "s" if i == 0 else "p"
            pst = ps_s[0:1, :] if i == 0 else ps_p[0:1, :]
            ev = p.v(lambda v, i=i, toff=toff, pst=pst: v.tensor_copy(
                ROWS[0:1, toff + i * 512:toff + (i + 1) * 512], pst),
                wait=[tk_p2mm[0], tk_p2mm[1]])
            psset(nmp, ev)
            tk_y2loc[(half, i)] = ev
        AF.pop()
        AB.pop()
        AB.pop()
    p.dma("gpsimd", qp_src[:, 0:8], cL1[:, :], wait=[tk_hcons], slot="qps")
    p.dma("gpsimd", qp_src[:, 8:16], cE1[:, :], wait=[tk_econs], slot="qps")
    tk_agqp = p.coll("AllGather", [qp_src.ap()], [qp_dst.ap()],
                     wait=[p.drain("gpsimd")])
    tk_y2 = dict(tk_y2loc)
    tH = p.v(lambda v: v.memset(H_a[:, :], 0.0))
    cur, oth = H_a, H_b
    tql = None
    for j in range(R - 1):
        tql = p.dma("sync", qpt[:, :], qp_dst[j, :, :],
                    wait=[tk_agqp, tH, tql], slot="qpl")
        t0 = p.v(lambda v, j=j: v.tensor_scalar(
            qpt[:, 8:16], qpt[:, 8:16], hmq[:, j:j + 1], hmpa[:, j:j + 1],
            AL.mult, AL.add), wait=[tql, tk_bp])
        t0b = p.v(lambda v, j=j: v.tensor_scalar(
            qpt[:, 0:8], qpt[:, 0:8], hmq[:, j:j + 1], None, AL.mult),
            wait=[tql])
        t1 = p.v(lambda v, cur=cur, oth=oth: v.tensor_tensor(
            oth[:, :], cur[:, :], qpt[:, 8:16], AL.mult), wait=[tH, t0])
        tH = p.v(lambda v, oth=oth: v.tensor_tensor(
            oth[:, :], oth[:, :], qpt[:, 0:8], AL.add), wait=[t1, t0b])
        cur, oth = oth, cur
        tql = tH
    Hfin = cur
    tk_z2c = {}
    for half in range(2):
        toff = half * TH
        tk_zmm = {}
        for m in range(DC):
            sl = m % 2
            tld = p.dma("sync", ecdB[sl][:, :],
                        ec_spill[m * 128:(m + 1) * 128, toff:toff + TH],
                        wait=[tk_ecs.get(0), tk_ecs.get(1), tk_z2c.get(sl)],
                        slot=f"ec{sl}")
            tz = p.v(lambda v, sl=sl, m=m: v.tensor_scalar(
                p2fB[sl][:, :], ecdB[sl][:, :], Hfin[:, m:m + 1], None,
                AL.mult), wait=[tH, tld, tk_zmm.get(sl), tk_p2mm.get(sl)])
            tk_z2c[sl] = tz
            for i in range(2):
                nmp = "s" if i == 0 else "p"
                pst = ps_s[0:1, :] if i == 0 else ps_p[0:1, :]
                ww = [tz, tk_ones] + ([psget(nmp)] if m == 0 else [])
                tky = p.t(lambda t, sl=sl, i=i, m=m, pst=pst: t.matmul(
                    pst, ones_sb[:, :], p2fB[sl][:, i * 512:(i + 1) * 512],
                    start=(m == 0), stop=(m == DC - 1)), wait=ww)
                tk_zmm[sl] = tky
        for i in range(2):
            nmp = "s" if i == 0 else "p"
            pst = ps_s[0:1, :] if i == 0 else ps_p[0:1, :]
            ev = p.v(lambda v, i=i, toff=toff, pst=pst: v.tensor_tensor(
                ROWS[0:1, toff + i * 512:toff + (i + 1) * 512],
                ROWS[0:1, toff + i * 512:toff + (i + 1) * 512],
                pst, AL.add), wait=[tk_zmm[0], tk_zmm[1]])
            psset(nmp, ev)
            tk_y2[(half, i)] = ev
    AB.pop()
    tk_y2b = p.dma("sync", browD[5:6, :], ROWS[0:1, :],
                   wait=list(tk_y2.values()), slot="brw")
    AB.push()
    y2rB = AB.take(2048)
    gatB = [AB.take(2048) for _ in range(DC)]
    svld = [AB.take(2048), AB.take(2048)]
    tk_y2r = p.dma("gpsimd", y2rB[:, :], browD[5:6, :].partition_broadcast(128),
                   wait=[tk_y2b], slot="y2r")
    tk_svld = {}
    tk_gat = []
    for k in range(DC):
        sl = k % 2
        tk_svld[sl] = p.dma("sync", svld[sl][:, :],
                            sv_spill[k * 128:(k + 1) * 128, :],
                            wait=[tk_sv_dma.get(0), tk_sv_dma.get(1),
                                  tk_svld.get(sl + 2)], slot=f"sv{sl}")
        tg = p.v(lambda v, k=k, sl=sl: v.tensor_tensor(
            gatB[k][:, :], svld[sl][:, :], y2rB[:, :], AL.mult),
            wait=[tk_svld[sl], tk_y2r])
        tk_svld[sl + 2] = tg
        tk_gat.append(tg)
    tkwo = kload("gpsimd", wgtB[:, :], woT.ap(), 8, "wg",
                 wait=[tk_wlast, p.cur("tt")])
    for m in range(DC):
        for i in range(4):
            nmp = "0" if (m * 4 + i) % 2 == 0 else "1"
            psd = ps0 if (m * 4 + i) % 2 == 0 else ps1
            pairs = [(wgtB[:, kc * 1024 + m * 128:kc * 1024 + (m + 1) * 128],
                      gatB[kc][:, i * 512:(i + 1) * 512], [tk_gat[kc], tkwo])
                     for kc in range(DC)]
            tkm = mm_group(nmp, psd[:, :], pairs)
            tev = p.v(lambda v, m=m, i=i, psd=psd: v.scalar_tensor_tensor(
                XS[m][:, i * 512:(i + 1) * 512], psd[:, :], bo_pp[:, m:m + 1],
                XS[m][:, i * 512:(i + 1) * 512], AL.add, AL.add), wait=[tkm])
            psset(nmp, tev)
    AB.pop()
    AF.pop()
    AB.pop()
    if stage == 3:
        p.barrier()
        for k in range(DC):
            p.dma("sync", out_c[k * 128:(k + 1) * 128, :], XS[k][:, :],
                  slot="out")
        p.g(lambda g: g.nop(), wait=[p.drain("sync")])
        return nc, p

    L1SL = [("sync", "b10"), ("sync", "b11"), ("sync", "b12"),
            ("scalar", "b14"), ("scalar", "b15"), ("scalar", "b16")]
    L2SL = [("sync", "b20"), ("sync", "b21"), ("sync", "b22"),
            ("scalar", "b24"), ("scalar", "b25"), ("scalar", "b26")]
    BNM8 = ["y0", "y1", "y2", "y3", "0", "1", "s", "p"]
    BAP8 = [ps_y[:, 0:512], ps_y[:, 512:1024], ps_y[:, 1024:1536],
            ps_y[:, 1536:2048], ps0, ps1, ps_s, ps_p]
    LOADS1 = [(it, mbp, kc) for it in range(4) for mbp in range(4)
              for kc in range(DC)]
    LOADS2 = [(it, kc2) for it in range(4) for kc2 in range(32)]
    tk_l1seq = {}
    tk_l2seq = {}
    tk_b1cons = {}
    tk_b2cons = {}

    def emit_load1(idx):
        if idx >= len(LOADS1):
            return
        it_, mbp_, kc_ = LOADS1[idx]
        til = idx % 6
        eng, slot = L1SL[til]
        tk_l1seq[idx] = p.dma(
            eng, bw1B[til][:, :],
            bw1_dram.ap()[kc_ * 128:(kc_ + 1) * 128,
                          mbp_ * 1024:(mbp_ + 1) * 1024],
            wait=[tk_bw_st, tk_b1cons.get(til)], slot=slot)

    def emit_load2(idx):
        if idx >= len(LOADS2):
            return
        it_, kc2_ = LOADS2[idx]
        til = idx % 6
        eng, slot = L2SL[til]
        tk_l2seq[idx] = p.dma(
            eng, bw2B[til][:, :],
            bw2_dram.ap()[kc2_ * 128:(kc2_ + 1) * 128, :],
            wait=[tk_bw_st, tk_b2cons.get(til)], slot=slot)

    p.barrier()
    AF.push()
    AB.push()
    bw1B = [AB.take(1024) for _ in range(6)]
    bw2B = [AB.take(1024) for _ in range(6)]
    for idx in range(6):
        emit_load1(idx)
        emit_load2(idx)
    evs4 = ln_stats()
    xn4B = [AB.take(512) for _ in range(DC)]
    h1B = [AB.take(512) for _ in range(32)]
    tk_bw4 = p.dma("sync", browD[0:2, :], ROWS[0:2, :], wait=evs4, slot="brw")
    murE = AF.take(2048)
    rrE = AF.take(2048)
    scrE = AF.take(2048)
    tmpE = AF.take(512)
    x4bF = AF.take(512)
    tk_mu = bcast(murE[:, :], browD[0:1, :], "mu", wait=[tk_bw4])
    tk_m2 = bcast(rrE[:, :], browD[1:2, :], "rr", wait=[tk_bw4])
    p.pend["vector"] += [tk_mu, tk_m2]
    p.pend["scalar"] += [tk_mu, tk_m2]
    tk_rr = finalize_reps(TOK, murE[:, :], rrE[:, :], scrE[:, :], rrE[:, :])
    tk_h1 = {}
    for it in range(4):
        isl = slice(it * 512, (it + 1) * 512)
        curT = p.cur("tt")
        tk_xn4 = {}
        for k in range(DC):
            t1 = p.v(lambda v, k=k, isl=isl: v.tensor_tensor(
                tmpE[:, :], XS[k][:, isl], murE[:, isl], AL.subtract),
                wait=[tk_mu, curT, tk_xn4.get(k - 1)])
            t2 = p.v(lambda v, k=k, isl=isl: v.tensor_tensor(
                xn4B[k][:, :], tmpE[:, :], rrE[:, isl], AL.mult),
                wait=[t1, tk_rr])
            tk_xn4[k] = t2
        for mbp in range(4):
            tk_mmj = {}
            for kc in range(DC):
                idx = (it * 4 + mbp) * DC + kc
                til = idx % 6
                for j in range(8):
                    extra = ([psget(BNM8[j])] if kc == 0 else [])
                    tk_mmj[j] = p.t(lambda t, til=til, j=j, kc=kc: t.matmul(
                        BAP8[j][:, :],
                        bw1B[til][:, j * 128:(j + 1) * 128],
                        xn4B[kc][:, :], start=(kc == 0), stop=(kc == DC - 1)),
                        wait=[tk_l1seq[idx], tk_xn4[kc]] + extra)
                tk_b1cons[til] = tk_mmj[7]
                emit_load1(idx + 6)
            for j in range(8):
                m = mbp * 8 + j
                tev = p.s(lambda s, m=m, j=j: s.activation(
                    h1B[m][:, :], BAP8[j][:, :], ACT.Gelu,
                    bias=b1p[:, m:m + 1], scale=a1_pp[:, m:m + 1]),
                    wait=[tk_mmj[j], tk_a1, curT])
                psset(BNM8[j], tev)
                tk_h1[(it, m)] = tev
        tk_m2j = {}
        for kc2 in range(32):
            idx = it * 32 + kc2
            til = idx % 6
            for j in range(8):
                extra = ([psget(BNM8[j])] if kc2 == 0 else [])
                tk_m2j[j] = p.t(lambda t, til=til, j=j, kc2=kc2: t.matmul(
                    BAP8[j][:, :], bw2B[til][:, j * 128:(j + 1) * 128],
                    h1B[kc2][:, :], start=(kc2 == 0), stop=(kc2 == 31)),
                    wait=[tk_l2seq[idx], tk_h1[(it, kc2)]] + extra)
            tk_b2cons[til] = tk_m2j[7]
            emit_load2(idx + 6)
        for j in range(8):
            m2 = j
            tb = p.v(lambda v, m2=m2, isl=isl: v.tensor_scalar(
                x4bF[:, :], XS[m2][:, isl], b2p[:, m2:m2 + 1], None,
                AL.add))
            tev = p.v(lambda v, m2=m2, isl=isl, j=j:
                      v.scalar_tensor_tensor(
                XS[m2][:, isl], BAP8[j][:, :], a2_pp[:, m2:m2 + 1],
                x4bF[:, :], AL.mult, AL.add),
                wait=[tk_m2j[j], tb, tk_a2])
            psset(BNM8[j], tev)
            p.dma("sync", out_c[m2 * 128:(m2 + 1) * 128, isl],
                  XS[m2][:, isl], wait=[tev], slot="out")
    p.g(lambda g: g.nop(), wait=[p.drain("sync")])
    AF.pop()
    AB.pop()
    return nc, p


_NC_CACHE = {}


def _get_program(stage):
    if stage not in _NC_CACHE:
        nc, p = build_program(stage)
        with nc.Block() as block:
            p.build(block)
        _NC_CACHE[stage] = nc
    return _NC_CACHE[stage]


def make_in_maps(inputs):
    z = inputs
    x = np.ascontiguousarray(z["L0_attn_x"], dtype=np.float32)
    Xf = x.reshape(T, D)
    Wstack = np.concatenate([z["L0_attn_W_dt1"], z["L0_attn_W_B"],
                             z["L0_attn_W_C"]], axis=0)
    pool_pat = np.zeros((128, 64), dtype=BF)
    for pp in range(128):
        pool_pat[pp, pp % 64] = 1.0
    lnl = np.zeros((128, 4), dtype=BF)
    lnl[:, 0] = 1.0 / D
    lnl[:, 3] = 1.0 / D
    consts = np.zeros((1, NCONST), dtype=np.float32)
    consts[0, C_EPS] = 1e-5
    consts[0, C_NEG1] = -1.0
    consts[0, C_NEGH] = -0.5
    consts[0, C_ONE] = 1.0

    def pp8(b):
        return np.ascontiguousarray(
            np.asarray(b, dtype=np.float32).reshape(8, 128).T)

    shared = {
        "wstackT": np.ascontiguousarray(Wstack.T, dtype=np.float32),
        "l0w_T": np.ascontiguousarray(z["L0_ffn_weight"].T, dtype=np.float32),
        "dishaI": np.ascontiguousarray(z["L0_ffn_disha"], dtype=np.float32),
        "pool_pat": pool_pat,
        "w_inT": np.ascontiguousarray(z["L1_attn_w_in"].T, dtype=np.float32),
        "wdT": np.ascontiguousarray(z["L1_attn_w_delta"].T, dtype=np.float32),
        "wBT": np.ascontiguousarray(z["L1_attn_w_B"].T, dtype=np.float32),
        "wCT": np.ascontiguousarray(z["L1_attn_w_C"].T, dtype=np.float32),
        "woT": np.ascontiguousarray(z["L1_attn_w_out"].T, dtype=np.float32),
        "b_in_ppI": np.ascontiguousarray(
            np.asarray(z["L1_attn_b_in"], dtype=np.float32).reshape(16, 128).T),
        "b_d_ppI": pp8(z["L1_attn_b_delta"]),
        "b_B_ppI": pp8(z["L1_attn_b_B"]),
        "b_C_ppI": pp8(z["L1_attn_b_C"]),
        "b_o_ppI": pp8(z["L1_attn_b_out"]),
        "b_cv_ppI": pp8(z["L1_attn_b_conv"]),
        "wcv_ppI": np.ascontiguousarray(
            np.asarray(z["L1_attn_w_conv"], dtype=np.float32)[:, 0, :]
            .reshape(8, 128, 4).transpose(1, 0, 2).reshape(128, 32)),
        "g_ppI": pp8(z["L1_attn_ln_g"]),
        "be_ppI": pp8(z["L1_attn_ln_b"]),
        "w1T": np.ascontiguousarray(z["L1_ffn_w1"].T, dtype=np.float32),
        "w2T": np.ascontiguousarray(z["L1_ffn_w2"].T, dtype=np.float32),
        "w1N": np.ascontiguousarray(z["L1_ffn_w1"], dtype=np.float32),
        "w2N": np.ascontiguousarray(z["L1_ffn_w2"], dtype=np.float32),
        "b1_ppI": np.ascontiguousarray(
            np.asarray(z["L1_ffn_b1"], dtype=np.float32).reshape(32, 128).T),
        "b2_ppI": pp8(z["L1_ffn_b2"]),
        "lnlI": lnl,
        "ones_bfI": np.ones((128, 1), dtype=BF),
        "id_bfI": np.eye(128, dtype=BF),
        "constr": consts,
    }
    in_maps = []
    for c in range(R):
        m = dict(shared)
        m["x_tok"] = np.ascontiguousarray(
            Xf[c * TOK:(c + 1) * TOK, :].T, dtype=np.float32)
        m["x_dsh"] = np.ascontiguousarray(
            Xf[:, c * 128:(c + 1) * 128].T, dtype=np.float32)
        m["wdt2T"] = np.ascontiguousarray(
            z["L0_attn_W_dt2"][c * 128:(c + 1) * 128, :].T, dtype=np.float32)
        m["b_dt2c"] = np.ascontiguousarray(
            z["L0_attn_b_dt2"][c * 128:(c + 1) * 128].reshape(128, 1),
            dtype=np.float32)
        m["logAc"] = np.ascontiguousarray(
            z["L0_attn_log_A"][c * 128:(c + 1) * 128, :], dtype=np.float32)
        m["dskipc"] = np.ascontiguousarray(
            z["L0_attn_D_skip"][c * 128:(c + 1) * 128].reshape(128, 1),
            dtype=np.float32)
        seq_first = (c % 4 == 0)
        hm = np.zeros((128, 8), dtype=np.float32)
        if not seq_first:
            hm[:, c - 1] = 1.0
        m["halomskI"] = hm
        m["halozI"] = np.full((128, 1), 0.0 if seq_first else 1.0,
                              dtype=np.float32)
        q = np.zeros((128, 8), dtype=np.float32)
        lo = (c // 4) * 4
        q[:, lo:c] = 1.0
        m["hmqI"] = q
        m["hmpaI"] = np.ascontiguousarray(1.0 - q)
        in_maps.append(m)
    return in_maps


def kernel(**inputs):
    in_maps = make_in_maps(inputs)
    nc = _get_program(STAGE)
    res = run_bass_kernel_spmd(nc, in_maps, core_ids=list(range(R)))
    out = np.empty((T, D), dtype=np.float32)
    for c in range(R):
        out[c * TOK:(c + 1) * TOK, :] = np.asarray(res.results[c]["out_c"]).T
    return out.reshape(2, 8192, D)


# version 11
# speedup vs baseline: 1.4424x; 1.0165x over previous
import numpy as np
import ml_dtypes
import concourse.bass as bass
import concourse.mybir as mybir
from concourse.bass_utils import run_bass_kernel_spmd
from concourse.aot_env import allow_ici_all_to_all

F32 = mybir.dt.float32
BF16 = mybir.dt.bfloat16
AL = mybir.AluOpType
ACT = mybir.ActivationFunctionType
AXL = mybir.AxisListType
BF = ml_dtypes.bfloat16

R = 8
D = 1024
N16 = 16
TOK = 2048
T = R * TOK
DC = 8
TH = 1024
STAGE = 4

C_EPS, C_NEG1, C_NEGH, C_ONE = 0, 1, 2, 3
NCONST = 8
ENGS = ("sync", "vector", "scalar", "tensor", "gpsimd")


class Prog:

    def __init__(self, nc):
        self.nc = nc
        self.q = {e: [] for e in ENGS}
        self.sems = {}
        self.cnt = {}
        for s in ("vv", "ss", "tt", "gg", "cc", "hs_sync", "hs_gpsimd"):
            self.sems[s] = nc.alloc_semaphore(s)
            self.cnt[s] = 0
        self.pend = {e: [] for e in ENGS}
        self.psfree = {}
        self.slot_eng = {}

    def cur(self, sem):
        return (sem, self.cnt[sem]) if self.cnt[sem] > 0 else None

    def _emit(self, eng, fn, waits, inc):
        waits = [w for w in waits if w is not None] + self.pend[eng]
        self.pend[eng] = []
        for (s, v) in waits:
            assert v <= self.cnt[s], f"future token {s}={v} now {self.cnt[s]}"
        tok = None
        if inc is not None:
            s, amt = inc
            self.cnt[s] += amt
            tok = (s, self.cnt[s])
        self.q[eng].append((tuple(waits), fn, inc))
        return tok

    def v(self, fn, wait=()):
        return self._emit("vector", fn, wait, ("vv", 1))

    def s(self, fn, wait=()):
        return self._emit("scalar", fn, wait, ("ss", 1))

    def t(self, fn, wait=()):
        return self._emit("tensor", fn, wait, ("tt", 1))

    def g(self, fn, wait=()):
        return self._emit("gpsimd", fn, wait, ("gg", 1))

    def dma(self, eng, out, in_, wait=(), slot=None):
        assert slot is not None, "dma needs a slot"
        name = "D_" + slot
        if name not in self.sems:
            self.sems[name] = self.nc.alloc_semaphore(name)
            self.cnt[name] = 0
            self.slot_eng[name] = eng
        assert self.slot_eng[name] == eng, f"slot {slot} used from two engines"
        return self._emit(eng, lambda e, o=out, i=in_: e.dma_start(out=o, in_=i),
                          wait, (name, 16))

    def drain(self, eng):
        waits = [(n, self.cnt[n]) for n in self.cnt
                 if n.startswith("D_") and self.slot_eng.get(n) == eng
                 and self.cnt[n] > 0]
        hs = "hs_" + eng
        if hs not in self.sems:
            self.sems[hs] = self.nc.alloc_semaphore(hs)
            self.cnt[hs] = 0
        return self._emit(eng, lambda e: e.nop(), waits, (hs, 1))

    def barrier(self):
        toks = []
        for eng in ENGS:
            if any(n.startswith("D_") and self.slot_eng.get(n) == eng
                   and self.cnt[n] > 0 for n in self.cnt):
                toks.append(self.drain(eng))
        toks += [(s, self.cnt[s]) for s in ("vv", "ss", "tt", "gg", "cc")
                 if self.cnt[s] > 0]
        for e in ENGS:
            self.pend[e] = list(toks)

    def coll(self, kind, ins, outs, wait=()):
        def fn(g, kind=kind, ins=ins, outs=outs):
            with allow_ici_all_to_all():
                return g.collective_compute(
                    kind, AL.bypass, replica_groups=[list(range(R))],
                    ins=[a.opt() for a in ins], outs=[a.opt() for a in outs])
        return self._emit("gpsimd", fn, wait, ("cc", 1))

    def build(self, block):
        hw = {e: {} for e in ENGS}

        def runner(eng):
            def run(engobj):
                h = hw[eng]
                for waits, fn, inc in self.q[eng]:
                    for (s, v) in waits:
                        if h.get(s, 0) < v:
                            engobj.wait_ge(self.sems[s], v)
                            h[s] = v
                    inst = fn(engobj)
                    if inc is not None:
                        s, amt = inc
                        inst.then_inc(self.sems[s], amt)
            return run
        block.sync(runner("sync"))
        block.vector(runner("vector"))
        block.scalar(runner("scalar"))
        block.tensor(runner("tensor"))
        block.gpsimd(runner("gpsimd"))


class Arena:
    def __init__(self, nc, name, cols, dtype):
        self.t = nc.alloc_sbuf_tensor(name, [128, cols], dtype)
        self.cols, self.off, self.marks = cols, 0, []

    def take(self, cols):
        assert self.off + cols <= self.cols, \
            f"{self.t.name} overflow {self.off}+{cols}>{self.cols}"
        ap = self.t[:, self.off:self.off + cols]
        self.off += cols
        return ap

    def push(self):
        self.marks.append(self.off)

    def pop(self):
        self.off = self.marks.pop()


def build_program(stage=STAGE):
    nc = bass.Bass("TRN2", target_bir_lowering=False,
                   detect_race_conditions=False)

    def din(name, shape, dtype=F32):
        return nc.dram_tensor(name, list(shape), dtype, kind="ExternalInput")

    x_tok = din("x_tok", [D, TOK])
    x_dsh = din("x_dsh", [128, T])
    wstackT = din("wstackT", [D, 96])
    wdt2T = din("wdt2T", [64, 128])
    b_dt2c = din("b_dt2c", [128, 1])
    logAc = din("logAc", [128, N16])
    dskipc = din("dskipc", [128, 1])
    l0w_T = din("l0w_T", [D, D])
    dishaI = din("dishaI", [64, D])
    pool_pat = din("pool_pat", [128, 64], BF16)
    w_inT = din("w_inT", [D, 2 * D])
    wdT = din("wdT", [D, D])
    wBT = din("wBT", [D, D])
    wCT = din("wCT", [D, D])
    woT = din("woT", [D, D])
    b_in_ppI = din("b_in_ppI", [128, 16])
    b_d_ppI = din("b_d_ppI", [128, 8])
    b_B_ppI = din("b_B_ppI", [128, 8])
    b_C_ppI = din("b_C_ppI", [128, 8])
    b_o_ppI = din("b_o_ppI", [128, 8])
    b_cv_ppI = din("b_cv_ppI", [128, 8])
    wcv_ppI = din("wcv_ppI", [128, 32])
    g_ppI = din("g_ppI", [128, 8])
    be_ppI = din("be_ppI", [128, 8])
    w1T = din("w1T", [D, 4 * D])
    w2T = din("w2T", [4 * D, D])
    w1N = din("w1N", [4 * D, D])
    w2N = din("w2N", [D, 4 * D])
    b1_ppI = din("b1_ppI", [128, 32])
    b2_ppI = din("b2_ppI", [128, 8])
    lnlI = din("lnlI", [128, 4], BF16)
    ones_bfI = din("ones_bfI", [128, 1], BF16)
    id_bfI = din("id_bfI", [128, 128], BF16)
    constr = din("constr", [1, NCONST])
    halomskI = din("halomskI", [128, 8])
    halozI = din("halozI", [128, 1])
    hmqI = din("hmqI", [128, 8])
    hmpaI = din("hmpaI", [128, 8])

    out_c = nc.dram_tensor("out_c", [D, TOK], F32, kind="ExternalOutput")

    ag1_src = nc.dram_tensor("ag1_src", [98, TOK], BF16)
    ag1_dst = nc.dram_tensor("ag1_dst", [R, 98, TOK], BF16)
    ya_src = nc.dram_tensor("ya_src", [R, 128, TOK], BF16)
    ya_dst = nc.dram_tensor("ya_dst", [R, 128, TOK], BF16)
    x3t_src = nc.dram_tensor("x3t_src", [128, 24], F32)
    x3t_dst = nc.dram_tensor("x3t_dst", [R, 128, 24], F32)
    qp_src = nc.dram_tensor("qp_src", [128, 16], F32)
    qp_dst = nc.dram_tensor("qp_dst", [R, 128, 16], F32)
    browD = nc.dram_tensor("browD", [8, TOK], F32)
    sv_spill = nc.dram_tensor("sv_spill", [D, TOK], BF16)
    inp_spill = nc.dram_tensor("inp_spill", [D, TH], BF16)
    cm_spill = nc.dram_tensor("cm_spill", [D, TH], BF16)
    ec_spill = nc.dram_tensor("ec_spill", [D, TOK], BF16)
    bw1_dram = nc.dram_tensor("bw1_dram", [D, 4 * D], BF16)
    bw2_dram = nc.dram_tensor("bw2_dram", [4 * D, D], BF16)

    XS = [nc.alloc_sbuf_tensor(f"xs{k}", [128, TOK], F32) for k in range(DC)]
    AF = Arena(nc, "AF", 12800, F32)
    AB = Arena(nc, "AB", 34560, BF16)
    SM = Arena(nc, "SM", 2000, F32)
    ROWS = nc.alloc_sbuf_tensor("ROWS", [8, TOK], F32)
    id_sb = nc.alloc_sbuf_tensor("id_sb", [128, 128], BF16)
    lnl_sb = nc.alloc_sbuf_tensor("lnl_sb", [128, 4], BF16)
    ones_sb = nc.alloc_sbuf_tensor("ones_sb", [128, 1], BF16)

    ps_y = nc.alloc_psum_tensor("ps_y", [128, 2048], F32)
    psA = nc.alloc_psum_tensor("psA", [128, 1024], F32)
    psB = nc.alloc_psum_tensor("psB", [128, 1024], F32)
    ps0 = psA[:, 0:512]
    ps1 = psA[:, 512:1024]
    ps_s = psB[:, 0:512]
    ps_p = psB[:, 512:1024]

    p = Prog(nc)

    def psget(name):
        return p.psfree.get(name)

    def psset(name, tok):
        p.psfree[name] = tok

    cst = SM.take(NCONST)
    A_sb = SM.take(N16)
    iA_sb = SM.take(N16)
    dsk = SM.take(1)
    bdt2 = SM.take(1)
    carry0 = SM.take(N16)
    bin_pp = SM.take(16)
    binh_pp = SM.take(8)
    bd_pp = SM.take(8)
    nbd_pp = SM.take(8)
    bB_pp = SM.take(8)
    bC_pp = SM.take(8)
    bo_pp = SM.take(8)
    bcv_pp = SM.take(8)
    wcv = SM.take(32)
    g2_pp = SM.take(8)
    be2_pp = SM.take(8)
    b1p = SM.take(32)
    b2p = SM.take(8)
    a1_pp = SM.take(32)
    a2_pp = SM.take(8)
    ascr = SM.take(1)
    cL1 = SM.take(8)
    cE1 = SM.take(8)
    H_a = SM.take(8)
    H_b = SM.take(8)
    qpt = SM.take(16)
    s96n = SM.take(1)
    hmsk = SM.take(8)
    hz = SM.take(1)
    hmq = SM.take(8)
    hmpa = SM.take(8)

    p.dma("sync", cst[:, :], constr[0:1, :].partition_broadcast(128), slot="ini")
    p.dma("sync", A_sb[:, :], logAc[:, :], slot="ini")
    for (dst, src) in ((dsk, dskipc), (bdt2, b_dt2c), (bin_pp, b_in_ppI),
                       (bd_pp, b_d_ppI), (bB_pp, b_B_ppI), (bC_pp, b_C_ppI),
                       (bo_pp, b_o_ppI), (bcv_pp, b_cv_ppI), (wcv, wcv_ppI),
                       (g2_pp, g_ppI), (be2_pp, be_ppI), (b1p, b1_ppI),
                       (b2p, b2_ppI), (hmsk, halomskI), (hz, halozI),
                       (hmq, hmqI), (hmpa, hmpaI)):
        p.dma("sync", dst[:, :], src[:, :], slot="ini")
    p.dma("sync", id_sb[:, :], id_bfI[:, :], slot="ini")
    p.dma("sync", lnl_sb[:, :], lnlI[:, :], slot="ini")
    p.dma("sync", ones_sb[:, :], ones_bfI[:, :], slot="ini")
    tk_ini = p.drain("sync")
    tk_c = tk_la = tk_bp = tk_id = tk_lnl = tk_ones = tk_ini
    tk_A = p.s(lambda s: s.activation(A_sb[:, :], A_sb[:, :], ACT.Exp),
               wait=[tk_la, tk_c])
    tk_A = p.v(lambda v: v.tensor_scalar(A_sb[:, :], A_sb[:, :], -1.0, None,
                                         AL.mult), wait=[tk_A])
    tk_iA = p.v(lambda v: v.tensor_scalar(iA_sb[:, :], A_sb[:, :], 1e-8, None,
                                          AL.add), wait=[tk_A])
    tk_iA = p.v(lambda v: v.reciprocal(iA_sb[:, :], iA_sb[:, :]), wait=[tk_iA])
    tk_nbd = p.v(lambda v: v.tensor_scalar(nbd_pp[:, :], bd_pp[:, :], -1.0, None,
                                           AL.mult), wait=[tk_bp])
    tk_binh = p.v(lambda v: v.tensor_scalar(binh_pp[:, :], bin_pp[:, 0:8],
                                            hz[:, 0:1], None, AL.mult),
                  wait=[tk_bp])

    def mm_group(psname, psum, pairs, extra_free=None, start0=True,
                 stop_last=True):
        tok = None
        n = len(pairs)
        free = psget(psname)
        for i, (lh, rh, w) in enumerate(pairs):
            ww = list(w)
            if i == 0:
                ww += [free, extra_free]
            tok = p.t(lambda t, lh=lh, rh=rh, i=i, psum=psum, n=n: t.matmul(
                psum, lh, rh, start=(start0 and i == 0),
                stop=(stop_last and i == n - 1)), wait=ww)
        return tok

    def bcast(dst, dram_row_ap, slot, wait=()):
        return p.dma("gpsimd", dst, dram_row_ap.partition_broadcast(128),
                     wait=wait, slot=slot)

    def kload(eng, out2d, dram_ap, nchunk, slot, wait=()):
        in3 = dram_ap.rearrange("(c p) m -> p c m", p=128)
        out3 = out2d.rearrange("p (c m) -> p c m", c=nchunk)
        return p.dma(eng, out3, in3, wait=wait, slot=slot)

    def ln_stats(width=TOK):
        AB.push()
        xb = [AB.take(512) for _ in range(DC)]
        sq = [AB.take(512) for _ in range(DC)]
        evs = []
        xs_ready = p.cur("vv")
        prevg = {}
        for i in range(width // 512):
            tcp = []
            for k in range(DC):
                t1 = p.v(lambda v, k=k, i=i, xb=xb: v.tensor_copy(
                    xb[k][:, :], XS[k][:, i * 512:(i + 1) * 512]),
                    wait=[prevg.get(k)])
                t2 = p.s(lambda s, k=k, i=i, sq=sq: s.activation(
                    sq[k][:, :], XS[k][:, i * 512:(i + 1) * 512], ACT.Square),
                    wait=[xs_ready, prevg.get(k)])
                tcp.append((t1, t2))
            pairs = [(lnl_sb[:, 0:2], xb[k][:, :], [tcp[k][0], tk_lnl])
                     for k in range(DC)]
            pairs += [(lnl_sb[:, 2:4], sq[k][:, :], [tcp[k][1]])
                      for k in range(DC)]
            tok = mm_group("s", ps_s[0:2, :], pairs)
            ev = p.v(lambda v, i=i: v.tensor_copy(
                ROWS[0:2, i * 512:(i + 1) * 512], ps_s[0:2, :]), wait=[tok])
            psset("s", ev)
            evs.append(ev)
            for k in range(DC):
                prevg[k] = tok
        for e in ("vector", "scalar", "gpsimd", "sync"):
            p.pend[e] = p.pend[e] + [evs[-1]]
        AB.pop()
        return evs

    def finalize_reps(width, muR, m2R, scrF, rR, rcR=None):
        t1 = p.s(lambda s: s.activation(scrF, muR, ACT.Square))
        t2 = p.v(lambda v: v.tensor_tensor(m2R, m2R, scrF, AL.subtract),
                 wait=[t1])
        t3 = p.s(lambda s: s.activation(scrF, m2R, ACT.Ln,
                                        bias=cst[:, C_EPS:C_EPS + 1]), wait=[t2])
        t4 = p.s(lambda s: s.activation(rR, scrF, ACT.Exp,
                                        scale=cst[:, C_NEGH:C_NEGH + 1]),
                 wait=[t3])
        if rcR is None:
            return t4
        t5 = p.s(lambda s: s.activation(scrF, rR, ACT.Square), wait=[t4])
        t6 = p.v(lambda v: v.tensor_tensor(scrF, scrF, m2R, AL.mult), wait=[t5])
        t7 = p.s(lambda s: s.activation(scrF, scrF, ACT.Ln,
                                        bias=cst[:, C_EPS:C_EPS + 1]), wait=[t6])
        t8 = p.s(lambda s: s.activation(scrF, scrF, ACT.Exp,
                                        scale=cst[:, C_NEGH:C_NEGH + 1]),
                 wait=[t7])
        t9 = p.v(lambda v: v.tensor_tensor(rcR, rR, scrF, AL.mult), wait=[t8])
        return t9

    AB.push()
    wst = AB.take(8 * 96)
    tk_wst = kload("gpsimd", wst[:, :], wstackT.ap(), 8, "wst")
    for k in range(DC):
        p.dma("sync", XS[k][:, :], x_tok[k * 128:(k + 1) * 128, :],
              slot="xt")
    p.barrier()
    evs1 = ln_stats()
    tk_st1 = p.dma("sync", browD[0:2, :], ROWS[0:2, :], wait=evs1, slot="brw")
    s96pairs = [(wst[:, kc * 96:(kc + 1) * 96], ones_sb[:, :], [tk_wst, tk_ones])
                for kc in range(DC)]
    tk_s96 = mm_group("1", ps1[0:96, 0:1], s96pairs)
    tk_s96 = p.v(lambda v: v.tensor_scalar(s96n[0:96, :], ps1[0:96, 0:1], -1.0,
                                           None, AL.mult), wait=[tk_s96])
    psset("1", tk_s96)
    AF.push()
    mur96 = AF.take(TOK)
    rr96 = AF.take(TOK)
    scr96 = AF.take(TOK)
    tk_mu96 = bcast(mur96[:, :], browD[0:1, :], "mu", wait=[tk_st1])
    tk_m296 = bcast(rr96[:, :], browD[1:2, :], "rr", wait=[tk_st1])
    p.pend["vector"] += [tk_mu96, tk_m296]
    p.pend["scalar"] += [tk_mu96, tk_m296]
    tk_fin1 = finalize_reps(TOK, mur96[:, :], rr96[:, :], scr96[:, :],
                            rr96[:, :])
    tk_rr96 = tk_fin1
    xbz = [AB.take(TOK) for _ in range(DC)]
    tkxb = [p.v(lambda v, k=k: v.tensor_copy(xbz[k][:, :], XS[k][:, :]))
            for k in range(DC)]
    z96 = AF.take(TOK)
    tk_ag_in = []
    for i in range(TOK // 512):
        pairs = [(wst[:, kc * 96:(kc + 1) * 96],
                  xbz[kc][:, i * 512:(i + 1) * 512], [tkxb[kc], tk_wst])
                 for kc in range(DC)]
        tokm = mm_group("0", ps0[0:96, :], pairs)
        tf = p.v(lambda v, i=i: v.scalar_tensor_tensor(
            z96[0:96, i * 512:(i + 1) * 512], mur96[0:96, i * 512:(i + 1) * 512],
            s96n[0:96, 0:1], ps0[0:96, :], AL.mult, AL.add),
            wait=[tokm, tk_mu96, tk_fin1, tk_s96])
        psset("0", tf)
        evp = p.v(lambda v, i=i: v.tensor_tensor(
            z96[0:96, i * 512:(i + 1) * 512], z96[0:96, i * 512:(i + 1) * 512],
            rr96[0:96, i * 512:(i + 1) * 512], AL.mult), wait=[tf, tk_rr96])
        tk_ag_in.append(evp)
    p.dma("gpsimd", ag1_src[0:96, :], z96[0:96, :], wait=tk_ag_in, slot="agf")
    p.dma("gpsimd", ag1_src[96:97, :], mur96[0:1, :],
          wait=[tk_fin1] + tk_ag_in, slot="agf")
    p.dma("gpsimd", ag1_src[97:98, :], rr96[0:1, :],
          wait=[tk_fin1] + tk_ag_in, slot="agf")
    tk_ag1 = p.coll("AllGather", [ag1_src.ap()], [ag1_dst.ap()],
                    wait=[p.drain("gpsimd")])
    AF.pop()
    AB.pop()

    p.barrier()
    AF.push()
    AB.push()
    wdt2_sb = AB.take(128)
    tk_wdt2 = p.dma("gpsimd", wdt2_sb[0:64, :], wdt2T[:, :], slot="wd2")
    xd = AF.take(2048)
    dtt = AF.take(2048)
    xnf = AF.take(2048)
    aT = [AF.take(2048), AF.take(2048)]
    bscan = AF.take(2048)
    murB = AB.take(2048)
    rrB = AB.take(2048)
    zdtB = AB.take(2048)
    xnB = AB.take(2048)
    browB = [AB.take(2048), AB.take(2048)]
    crowB = [AB.take(2048), AB.take(2048)]
    brepB = [AB.take(2048), AB.take(2048)]
    crepB = [AB.take(2048), AB.take(2048)]
    onesr = AB.take(128)
    hB = AB.take(2048)
    pB = [AB.take(2048), AB.take(2048)]
    wtmpP = AB.take(1024)
    sgnP = AB.take(1024)
    tk_onesr = p.v(lambda v: v.memset(onesr[:, :], 1.0))

    P0B = []
    for kc in range(DC):
        for q in range(4):
            P0B.append(("s", w1T[kc * 128:(kc + 1) * 128,
                                 q * 1024:(q + 1) * 1024],
                        bw1_dram[kc * 128:(kc + 1) * 128,
                                 q * 1024:(q + 1) * 1024]))
            P0B.append(("a1", w1N[(kc * 4 + q) * 128:(kc * 4 + q + 1) * 128,
                                  :], kc * 4 + q))
    for kc2 in range(32):
        P0B.append(("s", w2T[kc2 * 128:(kc2 + 1) * 128, :],
                    bw2_dram[kc2 * 128:(kc2 + 1) * 128, :]))
        kco, q = kc2 // 4, kc2 % 4
        P0B.append(("a2", w2N[kco * 128:(kco + 1) * 128,
                              q * 1024:(q + 1) * 1024], kco))
    tk_az2 = p.v(lambda v: v.memset(a2_pp[:, :], 0.0))
    pz_cons = {}
    pz_load = {}

    def p0b_load(k):
        src = P0B[k][1]
        pz_load[k] = p.dma("gpsimd", wtmpP[:, :], src,
                           wait=[pz_cons.get(k - 1)], slot="wbl")

    def p0b_consume(k):
        kind = P0B[k][0]
        if kind == "s":
            dst = P0B[k][2]
            tsg = p.s(lambda s: s.activation(sgnP[:, :], wtmpP[:, :],
                                             ACT.Sign),
                      wait=[pz_load[k], pz_state["store"]])
            pz_state["store"] = p.dma("gpsimd", dst, sgnP[:, :],
                                      wait=[tsg, pz_state["store"]],
                                      slot="wst1")
            pz_cons[k] = tsg
        elif kind == "a1":
            col = P0B[k][2]
            trd = p.v(lambda v, col=col: v.tensor_reduce(
                a1_pp[:, col:col + 1], wtmpP[:, :], AXL.X, AL.add,
                apply_absolute_value=True), wait=[pz_load[k]])
            pz_cons[k] = trd
            pz_state["a1last"] = trd
        else:
            col = P0B[k][2]
            trd = p.v(lambda v: v.tensor_reduce(
                ascr[:, 0:1], wtmpP[:, :], AXL.X, AL.add,
                apply_absolute_value=True),
                wait=[pz_load[k], pz_state["a2chain"]])
            tad = p.v(lambda v, col=col: v.tensor_tensor(
                a2_pp[:, col:col + 1], a2_pp[:, col:col + 1], ascr[:, 0:1],
                AL.add), wait=[trd, pz_state["a2chain"]])
            pz_state["a2chain"] = tad
            pz_cons[k] = trd
            pz_state["a2last"] = tad
    pz_state = {"store": None, "a2chain": tk_az2, "a1last": None,
                "a2last": None}

    tk_yevac = tk_y_dma = None
    tk_dt = None
    tk_xncons = None
    tk_pcons = {}
    tk_pcons2 = {}
    tk_wcons = {}
    tk_evb = {}
    tk_evc = {}
    tk_acons = {}
    tk_brep = {}
    tk_crep = {}
    y_dma_toks = []
    for tc in range(R):
        tk_xd = p.dma("sync", xd[:, :], x_dsh[:, tc * TOK:(tc + 1) * TOK],
                      wait=[tk_y_dma], slot="xd")
        tk_mu = bcast(murB[:, :], ag1_dst[tc, 96:97, :], "mu",
                      wait=[tk_ag1, tk_xncons])
        tk_rr = bcast(rrB[:, :], ag1_dst[tc, 97:98, :], "rr",
                      wait=[tk_ag1, tk_xncons])
        tk_zdt = p.dma("gpsimd", zdtB[0:64, :], ag1_dst[tc, 0:64, :],
                       wait=[tk_ag1, p.cur("tt")], slot="zd")
        t1 = p.v(lambda v: v.tensor_tensor(xnf[:, :], xd[:, :], murB[:, :],
                                           AL.subtract),
                 wait=[tk_xd, tk_mu, tk_yevac])
        tk_xn = p.v(lambda v: v.tensor_tensor(xnf[:, :], xnf[:, :], rrB[:, :],
                                              AL.mult), wait=[t1, tk_rr])
        tk_xnb = p.v(lambda v: v.tensor_copy(xnB[:, :], xnf[:, :]), wait=[tk_xn])
        tk_xncons = tk_xnb
        for i in range(4):
            nmp = "0" if i % 2 == 0 else "1"
            psd = ps0 if i % 2 == 0 else ps1
            tmm = mm_group(nmp, psd[:, :],
                           [(wdt2_sb[0:64, :], zdtB[0:64, i * 512:(i + 1) * 512],
                             [tk_zdt, tk_wdt2])])
            ta1 = p.s(lambda s, psd=psd: s.activation(
                aT[0][:, 0:512], psd[:, :], ACT.Abs, bias=bdt2[:, 0:1]),
                wait=[tmm, tk_dt] + ([tk_acons.get(0), tk_acons.get(1)]
                                     if i == 0 else []))
            ta2 = p.s(lambda s: s.activation(
                aT[0][:, 512:1024], aT[0][:, 0:512], ACT.Exp,
                scale=cst[:, C_NEG1:C_NEG1 + 1]), wait=[ta1])
            ta3 = p.s(lambda s: s.activation(
                aT[0][:, 0:512], aT[0][:, 512:1024], ACT.Ln,
                bias=cst[:, C_ONE:C_ONE + 1]), wait=[ta2])
            ta4 = p.s(lambda s, psd=psd: s.activation(
                aT[0][:, 512:1024], psd[:, :], ACT.Relu, bias=bdt2[:, 0:1]),
                wait=[ta3])
            psset(nmp, ta4)
            tk_dt = p.v(lambda v, i=i: v.tensor_tensor(
                dtt[:, i * 512:(i + 1) * 512], aT[0][:, 0:512],
                aT[0][:, 512:1024], AL.add), wait=[ta4])
        tk_id_free = tk_yevac
        def rep_rows(n):
            sl = n % 2
            tk_brep[sl] = p.dma("gpsimd", browB[sl][0:1, :],
                                ag1_dst[tc, 64 + n:65 + n, :],
                                wait=[tk_brep.get(sl + 2), tk_ag1],
                                slot=f"br{sl}")
            tk_crep[sl] = p.dma("gpsimd", crowB[sl][0:1, :],
                                ag1_dst[tc, 80 + n:81 + n, :],
                                wait=[tk_crep.get(sl + 2), tk_ag1],
                                slot=f"cr{sl}")

        def rep_produce(n):
            sl = n % 2
            evb, evc = [], []
            mmb = mmc = None
            mmq = {}
            for i in range(4):
                bname = ["0", "1", "s", "p"][i]
                psd = [ps0, ps1, ps_s, ps_p][i]
                mmb = p.t(lambda t, sl=sl, i=i, psd=psd: t.matmul(
                    psd[:, :], onesr[0:1, :],
                    browB[sl][0:1, i * 512:(i + 1) * 512],
                    start=True, stop=True),
                    wait=[tk_brep[sl], tk_onesr, psget(bname)])
                mmq[i] = mmb
            for half, pst in ((0, psA), (1, psB)):
                ev = p.s(lambda s, sl=sl, half=half, pst=pst, n=n:
                         s.activation(
                    brepB[sl][:, half * 1024:(half + 1) * 1024], pst[:, :],
                    ACT.Identity, scale=iA_sb[:, n:n + 1]),
                    wait=[mmq[2 * half], mmq[2 * half + 1], tk_iA,
                          tk_wcons.get(sl)])
                psset(["0", "1", "s", "p"][2 * half], ev)
                psset(["0", "1", "s", "p"][2 * half + 1], ev)
                evb.append(ev)
            tk_brep[sl + 2] = mmb
            for i in range(4):
                cname = ["0", "1", "s", "p"][i]
                psd = [ps0, ps1, ps_s, ps_p][i]
                mmc = p.t(lambda t, sl=sl, i=i, psd=psd: t.matmul(
                    psd[:, :], onesr[0:1, :],
                    crowB[sl][0:1, i * 512:(i + 1) * 512],
                    start=True, stop=True),
                    wait=[tk_crep[sl], tk_onesr, psget(cname)])
                mmq[i] = mmc
            for half, pst in ((0, psA), (1, psB)):
                ev = p.s(lambda s, sl=sl, half=half, pst=pst: s.activation(
                    crepB[sl][:, half * 1024:(half + 1) * 1024], pst[:, :],
                    ACT.Identity),
                    wait=[mmq[2 * half], mmq[2 * half + 1],
                          tk_pcons2.get(sl)])
                psset(["0", "1", "s", "p"][2 * half], ev)
                psset(["0", "1", "s", "p"][2 * half + 1], ev)
                evc.append(ev)
            tk_crep[sl + 2] = mmc
            tk_evb[sl] = evb
            tk_evc[sl] = evc

        rep_rows(0)
        rep_rows(1)
        rep_produce(0)
        for n in range(N16):
            sl = n % 2
            ta = p.s(lambda s, sl=sl, n=n: s.activation(
                aT[sl][:, :], dtt[:, :], ACT.Exp, scale=A_sb[:, n:n + 1]),
                wait=[tk_dt, tk_A, tk_acons.get(sl)])
            if n + 2 < N16:
                rep_rows(n + 2)
            if n + 1 < N16:
                rep_produce(n + 1)
            p0b_load(tc * 16 + n)
            tw = p.v(lambda v, sl=sl: v.tensor_tensor(
                pB[sl][:, :], xnB[:, :], brepB[sl][:, :], AL.mult),
                wait=[tk_xnb, tk_pcons.get(sl)] + tk_evb[sl])
            tk_wcons[sl] = tw
            tb = p.v(lambda v, sl=sl: v.scalar_tensor_tensor(
                bscan[:, :], aT[sl][:, :], 1.0, pB[sl][:, :], AL.subtract,
                AL.mult), wait=[ta, tw])
            if tc % 4 == 0:
                th = p.v(lambda v, sl=sl: v.tensor_tensor_scan(
                    hB[:, :], aT[sl][:, :], bscan[:, :], 0.0, AL.mult, AL.add),
                    wait=[tb])
            else:
                th = p.v(lambda v, sl=sl, n=n: v.tensor_tensor_scan(
                    hB[:, :], aT[sl][:, :], bscan[:, :], carry0[:, n:n + 1],
                    AL.mult, AL.add), wait=[tb])
            tk_acons[sl] = th
            p.v(lambda v, n=n: v.tensor_copy(carry0[:, n:n + 1],
                                             hB[:, 2047:2048]), wait=[th])
            tp = p.v(lambda v, sl=sl: v.tensor_tensor(
                pB[sl][:, :], hB[:, :], crepB[sl][:, :], AL.mult),
                wait=[th] + tk_evc[sl])
            tk_pcons2[sl] = tp
            last = None
            for i in range(4):
                ww = [tp, tk_id] + ([tk_id_free] if (n == 0 and i == 0) else [])
                last = p.t(lambda t, sl=sl, i=i, n=n: t.matmul(
                    ps_y[:, i * 512:(i + 1) * 512], id_sb[:, :],
                    pB[sl][:, i * 512:(i + 1) * 512], start=(n == 0),
                    stop=(n == N16 - 1)), wait=ww if i == 0 else [])
            tk_pcons[sl] = last
            p0b_consume(tc * 16 + n)
        tk_yevac = p.v(lambda v: v.scalar_tensor_tensor(
            xd[:, :], xnf[:, :], dsk[:, 0:1], ps_y[:, :], AL.mult, AL.add),
            wait=[tk_pcons[0], tk_pcons[1], tk_xn])
        tk_y_dma = p.dma("gpsimd", ya_src[tc, :, :], xd[:, :],
                         wait=[tk_yevac], slot="yst")
        y_dma_toks.append(tk_y_dma)
    psset("y", tk_yevac)
    tk_a1 = p.v(lambda v: v.tensor_scalar(a1_pp[:, :], a1_pp[:, :], 1.0 / D,
                                          None, AL.mult),
                wait=[pz_state["a1last"]])
    tk_a2 = p.v(lambda v: v.tensor_scalar(a2_pp[:, :], a2_pp[:, :],
                                          1.0 / (4 * D), None, AL.mult),
                wait=[pz_state["a2last"]])
    tk_bw_st = p.drain("gpsimd")
    tk_a2a = p.coll("AllToAll", [ya_src.ap()], [ya_dst.ap()],
                    wait=[tk_bw_st])
    AF.pop()
    AB.pop()

    p.barrier()
    AF.push()
    AB.push()
    ytl = [AB.take(2048), AB.take(2048)]
    tk_x2 = []
    ty_prev = {}
    for k in range(DC):
        sl = k % 2
        tyl = p.dma("sync", ytl[sl][:, :], ya_dst[k, :, :],
                    wait=[tk_a2a, ty_prev.get(sl)], slot=f"yt{sl}")
        tadd = p.v(lambda v, k=k, sl=sl: v.tensor_tensor(
            XS[k][:, :], XS[k][:, :], ytl[sl][:, :], AL.add), wait=[tyl])
        ty_prev[sl] = tadd
        tk_x2.append(tadd)
    if stage == 1:
        p.barrier()
        for k in range(DC):
            p.dma("sync", out_c[k * 128:(k + 1) * 128, :], XS[k][:, :],
                  slot="out")
        p.g(lambda g: g.nop(), wait=[p.drain("sync")])
        return nc, p

    evs2 = ln_stats()
    tk_bw2 = p.dma("sync", browD[0:2, :], ROWS[0:2, :], wait=evs2, slot="brw")
    murF = AF.take(2048)
    rrF = AF.take(2048)
    tmpF = AF.take(2048)
    tk_mu = bcast(murF[:, :], browD[0:1, :], "mu", wait=[tk_bw2])
    tk_m2 = bcast(rrF[:, :], browD[1:2, :], "rr", wait=[tk_bw2])
    p.pend["vector"] += [tk_mu, tk_m2]
    p.pend["scalar"] += [tk_mu, tk_m2]
    tk_rr = finalize_reps(TOK, murF[:, :], rrF[:, :], tmpF[:, :], rrF[:, :])
    xn2B = [AB.take(2048) for _ in range(DC)]
    tk_xn2 = []
    for k in range(DC):
        t1 = p.v(lambda v, k=k: v.tensor_tensor(tmpF[:, :], XS[k][:, :],
                                                murF[:, :], AL.subtract),
                 wait=[tk_mu, tk_rr, tk_x2[k]] + ([tk_xn2[-1]] if tk_xn2 else []))
        t2 = p.v(lambda v, k=k: v.tensor_tensor(xn2B[k][:, :], tmpF[:, :],
                                                rrF[:, :], AL.mult),
                 wait=[t1, tk_rr])
        tk_xn2.append(t2)
    wfB = AB.take(8192)
    tk_wf = kload("gpsimd", wfB[:, :], l0w_T.ap(), 8, "wf")
    diB = AB.take(1024)
    tk_di = p.dma("gpsimd", diB[0:64, :], dishaI[:, :], slot="di")
    ppB = AB.take(64)
    tk_ppat = p.dma("sync", ppB[:, 0:64], pool_pat[:, :], slot="pp")
    poolB = AB.take(512)
    for i in range(4):
        ppairs = [(ppB[:, 0:64], xn2B[kc][:, i * 512:(i + 1) * 512],
                   [tk_xn2[kc], tk_ppat]) for kc in range(DC)]
        tkpool = mm_group("p", ps_p[0:64, :], ppairs)
        tkpev = p.v(lambda v: v.tensor_copy(poolB[0:64, :], ps_p[0:64, :]),
                    wait=[tkpool, p.cur("tt")])
        psset("p", tkpev)
        for m in range(DC):
            nmp = "0" if (i * 8 + m) % 2 == 0 else "1"
            psd = ps0 if (i * 8 + m) % 2 == 0 else ps1
            pairs = [(wfB[:, kc * 1024 + m * 128:kc * 1024 + (m + 1) * 128],
                      xn2B[kc][:, i * 512:(i + 1) * 512], [tk_xn2[kc], tk_wf])
                     for kc in range(DC)]
            pairs.append((diB[0:64, m * 128:(m + 1) * 128], poolB[0:64, :],
                          [tkpev, tk_di]))
            tkm = mm_group(nmp, psd[:, :], pairs)
            tke = p.v(lambda v, m=m, i=i, psd=psd: v.tensor_tensor(
                XS[m][:, i * 512:(i + 1) * 512], XS[m][:, i * 512:(i + 1) * 512],
                psd[:, :], AL.add), wait=[tkm])
            psset(nmp, tke)
    tk_x3 = [psget("0"), psget("1")]
    for k in range(DC):
        p.dma("gpsimd", x3t_src[:, k * 3:(k + 1) * 3],
              XS[k][:, 2045:2048], wait=tk_x3, slot="x3t")
    tk_agx3 = p.coll("AllGather", [x3t_src.ap()], [x3t_dst.ap()],
                     wait=[p.drain("gpsimd")])
    AF.pop()
    AB.pop()
    if stage == 2:
        p.barrier()
        for k in range(DC):
            p.dma("sync", out_c[k * 128:(k + 1) * 128, :], XS[k][:, :],
                  slot="out")
        p.g(lambda g: g.nop(), wait=[p.drain("sync")])
        return nc, p

    p.barrier()
    AF.push()
    AB.push()
    evs3 = ln_stats()
    tk_bw3 = p.dma("sync", browD[0:2, :], ROWS[0:2, :], wait=evs3, slot="brw")
    AF.push()
    muR3 = AF.take(2048)
    m2R3 = AF.take(2048)
    scR3 = AF.take(2048)
    rR3 = AF.take(2048)
    rcR3 = AF.take(2048)
    tkmu3 = bcast(muR3[:, :], browD[0:1, :], "mu", wait=[tk_bw3])
    tkm23 = bcast(m2R3[:, :], browD[1:2, :], "rr", wait=[tk_bw3])
    p.pend["vector"] += [tkmu3, tkm23]
    p.pend["scalar"] += [tkmu3, tkm23]
    tk_fin3 = finalize_reps(TOK, muR3[:, :], m2R3[:, :], scR3[:, :],
                            rR3[:, :], rcR3[:, :])
    tk_bw3a = tk_bw3
    tk_bw3b = p.dma("sync", browD[1:2, :], rR3[0:1, :],
                    wait=[tk_fin3, tkm23], slot="brw")
    tk_bw3c = p.dma("sync", browD[2:3, :], rcR3[0:1, :],
                    wait=[tk_fin3, tk_bw3b], slot="brw")
    AF.pop()

    xpv = AF.take(28)
    AB.push()
    xallB = AF.take(DC * 24 // 4 * 4) if False else None
    xall = AF.take(192)
    tk_xall = p.dma("sync", xall[:, :].rearrange("p (r c) -> p r c", r=8),
                    x3t_dst.ap().rearrange("r p c -> p r c"),
                    wait=[tk_agx3, tk_fin3, tk_bw3b, tk_bw3c], slot="xal")
    tk_sel = p.v(lambda v: v.tensor_scalar(xpv[:, 0:24], xall[:, 0:24],
                                           hmsk[:, 0:1], None, AL.mult),
                 wait=[tk_xall, tk_bp])
    for j in range(1, R):
        tk_sel = p.v(lambda v, j=j: v.scalar_tensor_tensor(
            xpv[:, 0:24], xall[:, j * 24:(j + 1) * 24], hmsk[:, j:j + 1],
            xpv[:, 0:24], AL.mult, AL.add), wait=[tk_sel])
    hghB = []
    tk_hgh = []
    hpb = [AB.take(4) for _ in range(DC)]
    hps = [AB.take(4) for _ in range(DC)]
    tcst = []
    for k in range(DC):
        t1 = p.v(lambda v, k=k: v.tensor_copy(hpb[k][:, 0:3],
                                              xpv[:, k * 3:k * 3 + 3]),
                 wait=[tk_sel])
        t2 = p.s(lambda s, k=k: s.activation(hps[k][:, 0:3],
                                             xpv[:, k * 3:k * 3 + 3],
                                             ACT.Square), wait=[tk_sel])
        tcst.append((t1, t2))
    pairs = [(lnl_sb[:, 0:2], hpb[k][:, 0:3], [tcst[k][0], tk_lnl])
             for k in range(DC)]
    pairs += [(lnl_sb[:, 2:4], hps[k][:, 0:3], [tcst[k][1]]) for k in range(DC)]
    tkh = mm_group("s", ps_s[0:2, 0:3], pairs)
    tkhe = p.v(lambda v: v.tensor_copy(ROWS[0:2, 0:3], ps_s[0:2, 0:3]),
               wait=[tkh])
    psset("s", tkhe)
    tkhd = p.dma("sync", browD[3:5, 0:3], ROWS[0:2, 0:3],
                 wait=[tkhe, tk_bw3c], slot="brw")
    mur3h = AF.take(4)
    rcr3h = AF.take(4)
    tmp3 = AF.take(4)
    scr3h = AF.take(4)
    r3h = AF.take(4)
    tkb3 = bcast(mur3h[:, 0:3], browD[3:4, 0:3], "m3h", wait=[tkhd])
    tkb4 = bcast(rcr3h[:, 0:3], browD[4:5, 0:3], "r3h", wait=[tkhd])
    p.pend["vector"] += [tkb3, tkb4]
    p.pend["scalar"] += [tkb3, tkb4]
    tkb4 = finalize_reps(3, mur3h[:, 0:3], rcr3h[:, 0:3], scr3h[:, 0:3],
                         r3h[:, 0:3], rcr3h[:, 0:3])
    for k in range(DC):
        hg = AB.take(4)
        hghB.append(hg)
        t1 = p.v(lambda v, k=k: v.tensor_tensor(
            tmp3[:, 0:3], xpv[:, k * 3:k * 3 + 3], mur3h[:, 0:3], AL.subtract),
            wait=[tkb3] + ([tk_hgh[-1]] if tk_hgh else []))
        t2 = p.v(lambda v: v.tensor_tensor(tmp3[:, 0:3], tmp3[:, 0:3],
                                           rcr3h[:, 0:3], AL.mult),
                 wait=[t1, tkb4])
        t3 = p.v(lambda v, k=k, hg=hg: v.tensor_scalar(
            hg[:, 0:3], tmp3[:, 0:3], g2_pp[:, k:k + 1], be2_pp[:, k:k + 1],
            AL.mult, AL.add), wait=[t2])
        tk_hgh.append(t3)

    wgtB = AB.take(8192)
    AB.push()
    p2fB = [AB.take(TH), AB.take(TH)]
    zeroB = AB.take(TH)
    ecdB = [AB.take(TH), AB.take(TH)]
    spillB = [AB.take(512), AB.take(512)]
    cmldB = [AB.take(TH), AB.take(TH)]
    tk_zero = p.v(lambda v: v.memset(zeroB[:, :], 0.0))

    tk_sv_dma = {}
    tk_y2loc = {}
    tk_p2mm = {}
    tk_spill_inp = {}
    tk_spill_cm = {}
    tk_inpst = {}
    tk_cmst = {}
    tk_ecs = {}
    tk_hcons = None
    tk_econs = None
    tk_conv = {}
    tk_wlast = None
    tk_win = {}
    tk_hg_prev_g = None
    tk_te_h = {}
    for half in range(2):
        toff = half * TH
        AB.push()
        ueB = [AB.take(TH + 3) for _ in range(DC)]
        AB.push()
        winB = [AB.take(1024), AB.take(1024)]
        hgB = [AB.take(512) for _ in range(DC)]
        svB = [AB.take(512), AB.take(512)]
        AF.push()
        mur3 = AF.take(TH)
        r3r = AF.take(TH)
        rcr = AF.take(TH)
        t1F = AF.take(512)
        if half == 0:
            hguards = [p.cur("vv")]
        else:
            hguards = [tk_te_h[(0, 0)], tk_te_h[(0, 1)], tk_te_h[(0, 2)]]
        tkm3 = bcast(mur3[:, :], browD[0:1, toff:toff + TH], "mu",
                     wait=[tk_bw3a] + hguards)
        tkr3 = bcast(r3r[:, :], browD[1:2, toff:toff + TH], "rr",
                     wait=[tk_bw3b] + hguards)
        tkrc = bcast(rcr[:, :], browD[2:3, toff:toff + TH], "rc",
                     wait=[tk_bw3c] + hguards)
        tk_uev = {}
        tk_ue_halo = [None] * DC
        if half == 1:
            for m in range(DC):
                tk_ue_halo[m] = p.v(lambda v, m=m: v.tensor_copy(
                    ueB[m][:, 0:3], ueB[m][:, TH:TH + 3]),
                    wait=[tk_conv[(0, m)]])
        for i in range(2):
            isl = slice(toff + i * 512, toff + (i + 1) * 512)
            hg_guard = p.cur("tt")
            tk_hg = {}
            tk_hg_prev = None
            for k in range(DC):
                tt1 = p.v(lambda v, k=k, isl=isl, i=i: v.tensor_tensor(
                    t1F[:, :], XS[k][:, isl], mur3[:, i * 512:(i + 1) * 512],
                    AL.subtract), wait=[tkm3, tk_hg_prev, hg_guard])
                tt2 = p.v(lambda v, k=k, i=i: v.tensor_tensor(
                    hgB[k][:, :], t1F[:, :], rcr[:, i * 512:(i + 1) * 512],
                    AL.mult), wait=[tt1, tkrc])
                tt3 = p.v(lambda v, k=k: v.tensor_scalar(
                    hgB[k][:, :], hgB[k][:, :], g2_pp[:, k:k + 1],
                    be2_pp[:, k:k + 1], AL.mult, AL.add), wait=[tt2])
                tt4 = p.v(lambda v, k=k, i=i: v.tensor_tensor(
                    t1F[:, :], t1F[:, :], r3r[:, i * 512:(i + 1) * 512],
                    AL.mult), wait=[tt3, tkr3])
                tt5 = p.v(lambda v, k=k, isl=isl: v.tensor_tensor(
                    XS[k][:, isl], XS[k][:, isl], t1F[:, :], AL.add),
                    wait=[tt4])
                tk_hg_prev = tt5
                tk_hg_prev_g = tt5
                tk_hg[k] = tt3
            for uv in range(2):
                for m in range(DC):
                    sl = m % 2
                    tk_win[sl] = kload(
                        "gpsimd", winB[sl][:, :],
                        w_inT.ap()[:, uv * 1024 + m * 128:uv * 1024 + (m + 1) * 128],
                        8, f"wi{sl}", wait=[tk_win.get(sl + 2)])
                    nmp = "0" if (uv * 8 + m) % 2 == 0 else "1"
                    psd = ps0 if (uv * 8 + m) % 2 == 0 else ps1
                    pairs = [(winB[sl][:, kc * 128:(kc + 1) * 128],
                              hgB[kc][:, :], [tk_hg[kc], tk_win[sl]])
                             for kc in range(DC)]
                    tkm = mm_group(nmp, psd[:, :], pairs)
                    tk_win[sl + 2] = tkm
                    if uv == 0:
                        tev = p.s(lambda s, m=m, i=i, psd=psd: s.activation(
                            ueB[m][:, 3 + i * 512:3 + (i + 1) * 512],
                            psd[:, :], ACT.Identity, bias=bin_pp[:, m:m + 1]),
                            wait=[tkm, tk_ue_halo[m]])
                        tk_uev[(i, m)] = tev
                        psset(nmp, tev)
                    else:
                        ssl = m % 2
                        tsv = p.s(lambda s, ssl=ssl, psd=psd, m=m: s.activation(
                            svB[ssl][:, :], psd[:, :], ACT.Sigmoid,
                            bias=bin_pp[:, 8 + m:9 + m]),
                            wait=[tkm, tk_sv_dma.get(ssl)])
                        psset(nmp, tsv)
                        tk_sv_dma[ssl] = p.dma(
                            "sync",
                            sv_spill[m * 128:(m + 1) * 128,
                                     toff + i * 512:toff + (i + 1) * 512],
                            svB[ssl][:, :], wait=[tsv], slot=f"sv{ssl}")
        if half == 0:
            for m in range(DC):
                tkw = kload("gpsimd", winB[0][:, :],
                            w_inT.ap()[:, m * 128:(m + 1) * 128], 8, "wi0",
                            wait=[p.cur("tt")])
                pairs = [(winB[0][:, kc * 128:(kc + 1) * 128],
                          hghB[kc][:, 0:3], [tk_hgh[kc], tkw])
                         for kc in range(DC)]
                tkm = mm_group("0", ps0[:, 0:3], pairs)
                tev = p.s(lambda s, m=m: s.activation(
                    ueB[m][:, 0:3], ps0[:, 0:3], ACT.Identity,
                    scale=hz[:, 0:1], bias=binh_pp[:, m:m + 1]),
                    wait=[tkm, tk_binh, tk_uev[(0, m)], tk_uev[(1, m)]])
                psset("0", tev)
                tk_ue_halo[m] = tev
        AF.pop()
        AB.pop()
        AB.push()
        ucB = [AB.take(TH) for _ in range(DC)]
        AF.push()
        convF = [AF.take(TH), AF.take(TH)]
        for m in range(DC):
            tc0 = p.v(lambda v, m=m: v.tensor_scalar(
                convF[0][:, :], ueB[m][:, 3:3 + TH],
                wcv[:, m * 4 + 3:m * 4 + 4], bcv_pp[:, m:m + 1],
                AL.mult, AL.add),
                wait=[tk_uev[(0, m)], tk_uev[(1, m)], tk_ue_halo[m],
                      tk_conv.get((half - 1, m)), tk_wlast])
            src, dst = 0, 1
            for j in range(3):
                tc0 = p.v(lambda v, m=m, j=j, src=src, dst=dst:
                          v.scalar_tensor_tensor(
                              convF[dst][:, :], ueB[m][:, j:j + TH],
                              wcv[:, m * 4 + j:m * 4 + j + 1],
                              convF[src][:, :], AL.mult, AL.add), wait=[tc0])
                src, dst = dst, src
            tk_conv[(half, m)] = p.v(lambda v, m=m, src=src: v.tensor_copy(
                ucB[m][:, :], convF[src][:, :]), wait=[tc0, p.cur("tt")])
        AF.pop()
        AF.push()
        decayF = [AF.take(TH) for _ in range(DC)]
        hF = AF.take(TH)
        eF = AF.take(TH)
        inldF = [AF.take(TH), AF.take(TH)]
        specs = [("d", wdT), ("B", wBT), ("C", wCT)]
        tk_sp = {}
        for (nm, wt) in specs:
            tkwg = kload("gpsimd", wgtB[:, :], wt.ap(), 8, "wg",
                         wait=[tk_wlast, p.cur("tt")])
            for m in range(DC):
                for i in range(2):
                    nmp = "0" if (m * 2 + i) % 2 == 0 else "1"
                    psd = ps0 if (m * 2 + i) % 2 == 0 else ps1
                    pairs = [(wgtB[:, kc * 1024 + m * 128:kc * 1024 + (m + 1) * 128],
                              ucB[kc][:, i * 512:(i + 1) * 512],
                              [tk_conv[(half, kc)], tkwg]) for kc in range(DC)]
                    tkm = mm_group(nmp, psd[:, :], pairs)
                    if nm == "d":
                        tev = p.s(lambda s, m=m, i=i, psd=psd: s.activation(
                            decayF[m][:, i * 512:(i + 1) * 512], psd[:, :],
                            ACT.Sigmoid, scale=cst[:, C_NEG1:C_NEG1 + 1],
                            bias=nbd_pp[:, m:m + 1]),
                            wait=[tkm, tk_nbd, tk_te_h.get((half - 1, m))])
                    elif nm == "B":
                        sl = (m * 2 + i) % 2
                        tev = p.v(lambda v, m=m, sl=sl, psd=psd, i=i:
                                  v.scalar_tensor_tensor(
                                      spillB[sl][:, 0:512], psd[:, :],
                                      bB_pp[:, m:m + 1],
                                      ucB[m][:, i * 512:(i + 1) * 512],
                                      AL.add, AL.mult),
                                  wait=[tkm, tk_spill_inp.get(sl)])
                        tk_spill_inp[sl] = p.dma(
                            "sync", inp_spill[m * 128:(m + 1) * 128,
                                              i * 512:(i + 1) * 512],
                            spillB[sl][:, 0:512],
                            wait=[tev, tk_spill_inp.get(sl)], slot=f"si{sl}")
                        tk_inpst[(m, i)] = tk_spill_inp[sl]
                    else:
                        sl = (m * 2 + i) % 2
                        tev = p.s(lambda s, m=m, psd=psd, sl=sl: s.activation(
                            spillB[sl][:, 0:512], psd[:, :], ACT.Identity,
                            bias=bC_pp[:, m:m + 1]),
                            wait=[tkm, tk_spill_cm.get(sl)])
                        tk_spill_cm[sl] = p.dma(
                            "sync", cm_spill[m * 128:(m + 1) * 128,
                                             i * 512:(i + 1) * 512],
                            spillB[sl][:, 0:512],
                            wait=[tev, tk_spill_cm.get(sl)], slot=f"sc{sl}")
                        tk_cmst[(m, i)] = tk_spill_cm[sl]
                    psset(nmp, tev)
                    tk_sp[(nm, m, i)] = tev
            tk_wlast = tk_sp[(nm, 7, 1)]
        tk_cmld = {}
        tk_inld = {}
        for m in range(DC):
            sl = m % 2
            tk_inld[sl] = p.dma("gpsimd", inldF[sl][:, :],
                                inp_spill[m * 128:(m + 1) * 128, :],
                                wait=[tk_inpst[(m, 0)], tk_inpst[(m, 1)],
                                      tk_inld.get(sl + 2)], slot=f"ri{sl}")
            tk_cmld[sl] = p.dma("sync", cmldB[sl][:, :],
                                cm_spill[m * 128:(m + 1) * 128, :],
                                wait=[tk_cmst[(m, 0)], tk_cmst[(m, 1)],
                                      tk_cmld.get(sl + 2)], slot=f"cl{sl}")
            init_h = 0.0 if half == 0 else cL1[:, m:m + 1]
            init_e = 1.0 if half == 0 else cE1[:, m:m + 1]
            th = p.v(lambda v, m=m, sl=sl, init_h=init_h: v.tensor_tensor_scan(
                hF[:, :], decayF[m][:, :], inldF[sl][:, :], init_h,
                AL.mult, AL.add),
                wait=[tk_sp[("d", m, 0)], tk_sp[("d", m, 1)], tk_inld[sl],
                      tk_hcons])
            te = p.v(lambda v, m=m, init_e=init_e: v.tensor_tensor_scan(
                eF[:, :], decayF[m][:, :], zeroB[:, :], init_e,
                AL.mult, AL.add), wait=[tk_zero, tk_econs, th])
            p.v(lambda v, m=m: v.tensor_copy(cL1[:, m:m + 1], hF[:, TH - 1:TH]),
                wait=[th])
            p.v(lambda v, m=m: v.tensor_copy(cE1[:, m:m + 1], eF[:, TH - 1:TH]),
                wait=[te])
            tk_te_h[(half, m)] = te
            tp2 = p.v(lambda v, sl=sl: v.tensor_tensor(
                p2fB[sl][:, :], hF[:, :], cmldB[sl][:, :], AL.mult),
                wait=[th, tk_cmld[sl]])
            tk_hcons = tp2
            tec = p.v(lambda v, m=m, sl=sl: v.tensor_tensor(
                ecdB[sl][:, :], eF[:, :], cmldB[sl][:, :], AL.mult),
                wait=[te, tk_cmld[sl], tk_ecs.get(sl)])
            tk_econs = tec
            tk_ecs[sl] = p.dma("sync",
                               ec_spill[m * 128:(m + 1) * 128, toff:toff + TH],
                               ecdB[sl][:, :], wait=[tec], slot=f"ec{sl}")
            tk_cmld[sl + 2] = tec
            tk_inld[sl + 2] = tp2
            for i in range(2):
                nmp = "s" if i == 0 else "p"
                pst = ps_s[0:1, :] if i == 0 else ps_p[0:1, :]
                ww = [tp2, tk_ones] + ([psget(nmp)] if m == 0 else [])
                tky = p.t(lambda t, sl=sl, i=i, m=m, pst=pst: t.matmul(
                    pst, ones_sb[:, :], p2fB[sl][:, i * 512:(i + 1) * 512],
                    start=(m == 0), stop=(m == DC - 1)), wait=ww)
                tk_p2mm[sl] = tky
        for i in range(2):
            nmp = "s" if i == 0 else "p"
            pst = ps_s[0:1, :] if i == 0 else ps_p[0:1, :]
            ev = p.v(lambda v, i=i, toff=toff, pst=pst: v.tensor_copy(
                ROWS[0:1, toff + i * 512:toff + (i + 1) * 512], pst),
                wait=[tk_p2mm[0], tk_p2mm[1]])
            psset(nmp, ev)
            tk_y2loc[(half, i)] = ev
        AF.pop()
        AB.pop()
        AB.pop()
    p.dma("gpsimd", qp_src[:, 0:8], cL1[:, :], wait=[tk_hcons], slot="qps")
    p.dma("gpsimd", qp_src[:, 8:16], cE1[:, :], wait=[tk_econs], slot="qps")
    tk_agqp = p.coll("AllGather", [qp_src.ap()], [qp_dst.ap()],
                     wait=[p.drain("gpsimd")])
    tk_y2 = dict(tk_y2loc)
    tH = p.v(lambda v: v.memset(H_a[:, :], 0.0))
    cur, oth = H_a, H_b
    tql = None
    for j in range(R - 1):
        tql = p.dma("sync", qpt[:, :], qp_dst[j, :, :],
                    wait=[tk_agqp, tH, tql], slot="qpl")
        t0 = p.v(lambda v, j=j: v.tensor_scalar(
            qpt[:, 8:16], qpt[:, 8:16], hmq[:, j:j + 1], hmpa[:, j:j + 1],
            AL.mult, AL.add), wait=[tql, tk_bp])
        t0b = p.v(lambda v, j=j: v.tensor_scalar(
            qpt[:, 0:8], qpt[:, 0:8], hmq[:, j:j + 1], None, AL.mult),
            wait=[tql])
        t1 = p.v(lambda v, cur=cur, oth=oth: v.tensor_tensor(
            oth[:, :], cur[:, :], qpt[:, 8:16], AL.mult), wait=[tH, t0])
        tH = p.v(lambda v, oth=oth: v.tensor_tensor(
            oth[:, :], oth[:, :], qpt[:, 0:8], AL.add), wait=[t1, t0b])
        cur, oth = oth, cur
        tql = tH
    Hfin = cur
    tk_z2c = {}
    for half in range(2):
        toff = half * TH
        tk_zmm = {}
        for m in range(DC):
            sl = m % 2
            tld = p.dma("sync", ecdB[sl][:, :],
                        ec_spill[m * 128:(m + 1) * 128, toff:toff + TH],
                        wait=[tk_ecs.get(0), tk_ecs.get(1), tk_z2c.get(sl)],
                        slot=f"ec{sl}")
            tz = p.v(lambda v, sl=sl, m=m: v.tensor_scalar(
                p2fB[sl][:, :], ecdB[sl][:, :], Hfin[:, m:m + 1], None,
                AL.mult), wait=[tH, tld, tk_zmm.get(sl), tk_p2mm.get(sl)])
            tk_z2c[sl] = tz
            for i in range(2):
                nmp = "s" if i == 0 else "p"
                pst = ps_s[0:1, :] if i == 0 else ps_p[0:1, :]
                ww = [tz, tk_ones] + ([psget(nmp)] if m == 0 else [])
                tky = p.t(lambda t, sl=sl, i=i, m=m, pst=pst: t.matmul(
                    pst, ones_sb[:, :], p2fB[sl][:, i * 512:(i + 1) * 512],
                    start=(m == 0), stop=(m == DC - 1)), wait=ww)
                tk_zmm[sl] = tky
        for i in range(2):
            nmp = "s" if i == 0 else "p"
            pst = ps_s[0:1, :] if i == 0 else ps_p[0:1, :]
            ev = p.v(lambda v, i=i, toff=toff, pst=pst: v.tensor_tensor(
                ROWS[0:1, toff + i * 512:toff + (i + 1) * 512],
                ROWS[0:1, toff + i * 512:toff + (i + 1) * 512],
                pst, AL.add), wait=[tk_zmm[0], tk_zmm[1]])
            psset(nmp, ev)
            tk_y2[(half, i)] = ev
    AB.pop()
    tk_y2b = p.dma("sync", browD[5:6, :], ROWS[0:1, :],
                   wait=list(tk_y2.values()), slot="brw")
    AB.push()
    y2rB = AB.take(2048)
    gatB = [AB.take(2048) for _ in range(DC)]
    svld = [AB.take(2048), AB.take(2048)]
    tk_y2r = p.dma("gpsimd", y2rB[:, :], browD[5:6, :].partition_broadcast(128),
                   wait=[tk_y2b], slot="y2r")
    tk_svld = {}
    tk_gat = []
    for k in range(DC):
        sl = k % 2
        tk_svld[sl] = p.dma("sync", svld[sl][:, :],
                            sv_spill[k * 128:(k + 1) * 128, :],
                            wait=[tk_sv_dma.get(0), tk_sv_dma.get(1),
                                  tk_svld.get(sl + 2)], slot=f"sv{sl}")
        tg = p.v(lambda v, k=k, sl=sl: v.tensor_tensor(
            gatB[k][:, :], svld[sl][:, :], y2rB[:, :], AL.mult),
            wait=[tk_svld[sl], tk_y2r])
        tk_svld[sl + 2] = tg
        tk_gat.append(tg)
    tkwo = kload("gpsimd", wgtB[:, :], woT.ap(), 8, "wg",
                 wait=[tk_wlast, p.cur("tt")])
    for m in range(DC):
        for i in range(4):
            nmp = "0" if (m * 4 + i) % 2 == 0 else "1"
            psd = ps0 if (m * 4 + i) % 2 == 0 else ps1
            pairs = [(wgtB[:, kc * 1024 + m * 128:kc * 1024 + (m + 1) * 128],
                      gatB[kc][:, i * 512:(i + 1) * 512], [tk_gat[kc], tkwo])
                     for kc in range(DC)]
            tkm = mm_group(nmp, psd[:, :], pairs)
            tev = p.v(lambda v, m=m, i=i, psd=psd: v.scalar_tensor_tensor(
                XS[m][:, i * 512:(i + 1) * 512], psd[:, :], bo_pp[:, m:m + 1],
                XS[m][:, i * 512:(i + 1) * 512], AL.add, AL.add), wait=[tkm])
            psset(nmp, tev)
    AB.pop()
    AF.pop()
    AB.pop()
    if stage == 3:
        p.barrier()
        for k in range(DC):
            p.dma("sync", out_c[k * 128:(k + 1) * 128, :], XS[k][:, :],
                  slot="out")
        p.g(lambda g: g.nop(), wait=[p.drain("sync")])
        return nc, p

    L1SL = [("sync", "b10"), ("sync", "b11"), ("sync", "b12"),
            ("scalar", "b14"), ("scalar", "b15"), ("scalar", "b16")]
    L2SL = [("sync", "b20"), ("sync", "b21"), ("sync", "b22"),
            ("scalar", "b24"), ("scalar", "b25"), ("scalar", "b26")]
    BNM8 = ["y0", "y1", "y2", "y3", "0", "1", "s", "p"]
    BAP8 = [ps_y[:, 0:512], ps_y[:, 512:1024], ps_y[:, 1024:1536],
            ps_y[:, 1536:2048], ps0, ps1, ps_s, ps_p]
    LOADS1 = [(it, mbp, kc) for it in range(4) for mbp in range(4)
              for kc in range(DC)]
    LOADS2 = [(it, kc2) for it in range(4) for kc2 in range(32)]
    tk_l1seq = {}
    tk_l2seq = {}
    tk_b1cons = {}
    tk_b2cons = {}

    def emit_load1(idx):
        if idx >= len(LOADS1):
            return
        it_, mbp_, kc_ = LOADS1[idx]
        til = idx % 6
        eng, slot = L1SL[til]
        tk_l1seq[idx] = p.dma(
            eng, bw1B[til][:, :],
            bw1_dram.ap()[kc_ * 128:(kc_ + 1) * 128,
                          mbp_ * 1024:(mbp_ + 1) * 1024],
            wait=[tk_bw_st, tk_b1cons.get(til)], slot=slot)

    def emit_load2(idx):
        if idx >= len(LOADS2):
            return
        it_, kc2_ = LOADS2[idx]
        til = idx % 6
        eng, slot = L2SL[til]
        tk_l2seq[idx] = p.dma(
            eng, bw2B[til][:, :],
            bw2_dram.ap()[kc2_ * 128:(kc2_ + 1) * 128, :],
            wait=[tk_bw_st, tk_b2cons.get(til)], slot=slot)

    p.barrier()
    AF.push()
    AB.push()
    bw1B = [AB.take(1024) for _ in range(6)]
    bw2B = [AB.take(1024) for _ in range(6)]
    for idx in range(6):
        emit_load1(idx)
        emit_load2(idx)
    evs4 = ln_stats()
    xn4B = [AB.take(512) for _ in range(DC)]
    h1B = [AB.take(512) for _ in range(32)]
    tk_bw4 = p.dma("sync", browD[0:2, :], ROWS[0:2, :], wait=evs4, slot="brw")
    murE = AF.take(2048)
    rrE = AF.take(2048)
    scrE = AF.take(2048)
    tmpE = AF.take(512)
    x4bF = AF.take(512)
    tk_mu = bcast(murE[:, :], browD[0:1, :], "mu", wait=[tk_bw4])
    tk_m2 = bcast(rrE[:, :], browD[1:2, :], "rr", wait=[tk_bw4])
    p.pend["vector"] += [tk_mu, tk_m2]
    p.pend["scalar"] += [tk_mu, tk_m2]
    tk_rr = finalize_reps(TOK, murE[:, :], rrE[:, :], scrE[:, :], rrE[:, :])
    tk_h1 = {}
    for it in range(4):
        isl = slice(it * 512, (it + 1) * 512)
        curT = p.cur("tt")
        tk_xn4 = {}
        for k in range(DC):
            t1 = p.v(lambda v, k=k, isl=isl: v.tensor_tensor(
                tmpE[:, :], XS[k][:, isl], murE[:, isl], AL.subtract),
                wait=[tk_mu, curT, tk_xn4.get(k - 1)])
            t2 = p.v(lambda v, k=k, isl=isl: v.tensor_tensor(
                xn4B[k][:, :], tmpE[:, :], rrE[:, isl], AL.mult),
                wait=[t1, tk_rr])
            tk_xn4[k] = t2
        for mbp in range(4):
            tk_mmj = {}
            for kc in range(DC):
                idx = (it * 4 + mbp) * DC + kc
                til = idx % 6
                for j in range(8):
                    extra = ([psget(BNM8[j])] if kc == 0 else [])
                    tk_mmj[j] = p.t(lambda t, til=til, j=j, kc=kc: t.matmul(
                        BAP8[j][:, :],
                        bw1B[til][:, j * 128:(j + 1) * 128],
                        xn4B[kc][:, :], start=(kc == 0), stop=(kc == DC - 1)),
                        wait=[tk_l1seq[idx], tk_xn4[kc]] + extra)
                tk_b1cons[til] = tk_mmj[7]
                emit_load1(idx + 6)
            for j in range(8):
                m = mbp * 8 + j
                tev = p.s(lambda s, m=m, j=j: s.activation(
                    h1B[m][:, :], BAP8[j][:, :], ACT.Gelu,
                    bias=b1p[:, m:m + 1], scale=a1_pp[:, m:m + 1]),
                    wait=[tk_mmj[j], tk_a1, curT])
                psset(BNM8[j], tev)
                tk_h1[(it, m)] = tev
        tk_m2j = {}
        for kc2 in range(32):
            idx = it * 32 + kc2
            til = idx % 6
            for j in range(8):
                extra = ([psget(BNM8[j])] if kc2 == 0 else [])
                tk_m2j[j] = p.t(lambda t, til=til, j=j, kc2=kc2: t.matmul(
                    BAP8[j][:, :], bw2B[til][:, j * 128:(j + 1) * 128],
                    h1B[kc2][:, :], start=(kc2 == 0), stop=(kc2 == 31)),
                    wait=[tk_l2seq[idx], tk_h1[(it, kc2)]] + extra)
            tk_b2cons[til] = tk_m2j[7]
            emit_load2(idx + 6)
        for j in range(8):
            m2 = j
            tb = p.v(lambda v, m2=m2, isl=isl: v.tensor_scalar(
                x4bF[:, :], XS[m2][:, isl], b2p[:, m2:m2 + 1], None,
                AL.add))
            tev = p.v(lambda v, m2=m2, isl=isl, j=j:
                      v.scalar_tensor_tensor(
                XS[m2][:, isl], BAP8[j][:, :], a2_pp[:, m2:m2 + 1],
                x4bF[:, :], AL.mult, AL.add),
                wait=[tk_m2j[j], tb, tk_a2])
            psset(BNM8[j], tev)
            p.dma("sync", out_c[m2 * 128:(m2 + 1) * 128, isl],
                  XS[m2][:, isl], wait=[tev], slot="out")
    p.g(lambda g: g.nop(), wait=[p.drain("sync")])
    AF.pop()
    AB.pop()
    return nc, p


_NC_CACHE = {}


def _get_program(stage):
    if stage not in _NC_CACHE:
        nc, p = build_program(stage)
        with nc.Block() as block:
            p.build(block)
        _NC_CACHE[stage] = nc
    return _NC_CACHE[stage]


def make_in_maps(inputs):
    z = inputs
    x = np.ascontiguousarray(z["L0_attn_x"], dtype=np.float32)
    Xf = x.reshape(T, D)
    Wstack = np.concatenate([z["L0_attn_W_dt1"], z["L0_attn_W_B"],
                             z["L0_attn_W_C"]], axis=0)
    pool_pat = np.zeros((128, 64), dtype=BF)
    for pp in range(128):
        pool_pat[pp, pp % 64] = 1.0
    lnl = np.zeros((128, 4), dtype=BF)
    lnl[:, 0] = 1.0 / D
    lnl[:, 3] = 1.0 / D
    consts = np.zeros((1, NCONST), dtype=np.float32)
    consts[0, C_EPS] = 1e-5
    consts[0, C_NEG1] = -1.0
    consts[0, C_NEGH] = -0.5
    consts[0, C_ONE] = 1.0

    def pp8(b):
        return np.ascontiguousarray(
            np.asarray(b, dtype=np.float32).reshape(8, 128).T)

    shared = {
        "wstackT": np.ascontiguousarray(Wstack.T, dtype=np.float32),
        "l0w_T": np.ascontiguousarray(z["L0_ffn_weight"].T, dtype=np.float32),
        "dishaI": np.ascontiguousarray(z["L0_ffn_disha"], dtype=np.float32),
        "pool_pat": pool_pat,
        "w_inT": np.ascontiguousarray(z["L1_attn_w_in"].T, dtype=np.float32),
        "wdT": np.ascontiguousarray(z["L1_attn_w_delta"].T, dtype=np.float32),
        "wBT": np.ascontiguousarray(z["L1_attn_w_B"].T, dtype=np.float32),
        "wCT": np.ascontiguousarray(z["L1_attn_w_C"].T, dtype=np.float32),
        "woT": np.ascontiguousarray(z["L1_attn_w_out"].T, dtype=np.float32),
        "b_in_ppI": np.ascontiguousarray(
            np.asarray(z["L1_attn_b_in"], dtype=np.float32).reshape(16, 128).T),
        "b_d_ppI": pp8(z["L1_attn_b_delta"]),
        "b_B_ppI": pp8(z["L1_attn_b_B"]),
        "b_C_ppI": pp8(z["L1_attn_b_C"]),
        "b_o_ppI": pp8(z["L1_attn_b_out"]),
        "b_cv_ppI": pp8(z["L1_attn_b_conv"]),
        "wcv_ppI": np.ascontiguousarray(
            np.asarray(z["L1_attn_w_conv"], dtype=np.float32)[:, 0, :]
            .reshape(8, 128, 4).transpose(1, 0, 2).reshape(128, 32)),
        "g_ppI": pp8(z["L1_attn_ln_g"]),
        "be_ppI": pp8(z["L1_attn_ln_b"]),
        "w1T": np.ascontiguousarray(z["L1_ffn_w1"].T, dtype=np.float32),
        "w2T": np.ascontiguousarray(z["L1_ffn_w2"].T, dtype=np.float32),
        "w1N": np.ascontiguousarray(z["L1_ffn_w1"], dtype=np.float32),
        "w2N": np.ascontiguousarray(z["L1_ffn_w2"], dtype=np.float32),
        "b1_ppI": np.ascontiguousarray(
            np.asarray(z["L1_ffn_b1"], dtype=np.float32).reshape(32, 128).T),
        "b2_ppI": pp8(z["L1_ffn_b2"]),
        "lnlI": lnl,
        "ones_bfI": np.ones((128, 1), dtype=BF),
        "id_bfI": np.eye(128, dtype=BF),
        "constr": consts,
    }
    in_maps = []
    for c in range(R):
        m = dict(shared)
        m["x_tok"] = np.ascontiguousarray(
            Xf[c * TOK:(c + 1) * TOK, :].T, dtype=np.float32)
        m["x_dsh"] = np.ascontiguousarray(
            Xf[:, c * 128:(c + 1) * 128].T, dtype=np.float32)
        m["wdt2T"] = np.ascontiguousarray(
            z["L0_attn_W_dt2"][c * 128:(c + 1) * 128, :].T, dtype=np.float32)
        m["b_dt2c"] = np.ascontiguousarray(
            z["L0_attn_b_dt2"][c * 128:(c + 1) * 128].reshape(128, 1),
            dtype=np.float32)
        m["logAc"] = np.ascontiguousarray(
            z["L0_attn_log_A"][c * 128:(c + 1) * 128, :], dtype=np.float32)
        m["dskipc"] = np.ascontiguousarray(
            z["L0_attn_D_skip"][c * 128:(c + 1) * 128].reshape(128, 1),
            dtype=np.float32)
        seq_first = (c % 4 == 0)
        hm = np.zeros((128, 8), dtype=np.float32)
        if not seq_first:
            hm[:, c - 1] = 1.0
        m["halomskI"] = hm
        m["halozI"] = np.full((128, 1), 0.0 if seq_first else 1.0,
                              dtype=np.float32)
        q = np.zeros((128, 8), dtype=np.float32)
        lo = (c // 4) * 4
        q[:, lo:c] = 1.0
        m["hmqI"] = q
        m["hmpaI"] = np.ascontiguousarray(1.0 - q)
        in_maps.append(m)
    return in_maps


def kernel(**inputs):
    in_maps = make_in_maps(inputs)
    nc = _get_program(STAGE)
    res = run_bass_kernel_spmd(nc, in_maps, core_ids=list(range(R)))
    out = np.empty((T, D), dtype=np.float32)
    for c in range(R):
        out[c * TOK:(c + 1) * TOK, :] = np.asarray(res.results[c]["out_c"]).T
    return out.reshape(2, 8192, D)
